# version 04
# baseline (speedup 1.0000x reference)
import numpy as np
import ml_dtypes
import concourse.bass as bass
import concourse.mybir as mybir
from concourse.bass_utils import run_bass_kernel_spmd

F32 = mybir.dt.float32
BF16 = mybir.dt.bfloat16
AF = mybir.ActivationFunctionType
ALU = mybir.AluOpType
AX = mybir.AxisListType

B = 4
L = 2048
H = 2048
NH = 16
NT = 8
NE = 16
IC = 11
SCALE = 1.0 / float(np.sqrt(128.0))
CBIG = 1.0e30
N_LIST = [16, 14, 12, 10, 8, 6, 4, 2]
SEQ = [7, 6, 5, 4, 3, 2, 1, 0]

JORD = [x for i in range(11) for x in (i, 11 + i)]
SJORD = [x for i in range(22) for x in (i, 22 + i)]

W13SEQ = [("r", e, k) for e in range(NE) for k in range(11)] + \
         [("s", None, k) for k in range(22)]
W2SEQ = [("r", e, hh) for e in range(NE) for hh in range(2)] + \
        [("s", None, q) for q in range(4)]


class Plan:
    def __init__(self):
        self.ops = {"tensor": [], "vector": [], "scalar": [], "sync": [], "gpsimd": []}
        self.cnt = {"sd": 0, "gd": 0, "t": 0, "v": 0, "s": 0,
                    "kva": 0, "kvb": 0, "w1a": 0, "w1b": 0, "w2a": 0, "w2b": 0}

    def emit(self, eng, fn, waits=None, inc=None, amt=1):
        w = {k: v for k, v in (waits or {}).items() if v and v > 0}
        self.ops[eng].append((w, fn, inc, amt))
        if inc is not None:
            self.cnt[inc] += amt
        return dict(self.cnt)


LIMIT = {"heads": NH, "ln1": 1, "p3": 1, "experts": 18, "ln2": 1, "astage": 9}


def build_program():
    nheads = LIMIT["heads"]
    nroute = min(NE, LIMIT["experts"])
    do_shared = LIMIT["experts"] >= 17
    A = LIMIT["astage"]
    nc = bass.Bass("TRN2", target_bir_lowering=False)

    d_xmy = nc.dram_tensor("xmy", [1024, H], F32, kind="ExternalInput").ap()
    d_xkv = nc.dram_tensor("xkv", [L, H], F32, kind="ExternalInput").ap()
    d_w13 = nc.dram_tensor("w13t", [NE, 22, 128, 2048], F32, kind="ExternalInput").ap()
    d_sw13 = nc.dram_tensor("sw13t", [44, 128, 2048], F32, kind="ExternalInput").ap()
    d_w2 = nc.dram_tensor("w2t", [NE, 2, IC, 128, 1024], F32, kind="ExternalInput").ap()
    d_sw2 = nc.dram_tensor("sw2t", [4, IC, 128, 1024], F32, kind="ExternalInput").ap()
    d_gw = nc.dram_tensor("gwt", [16, 128, 16], F32, kind="ExternalInput").ap()
    d_bx = nc.dram_tensor("bxm", [8, 128, 256], F32, kind="ExternalInput").ap()
    d_by = nc.dram_tensor("bym", [8, 128, 256], F32, kind="ExternalInput").ap()
    d_id = nc.dram_tensor("identb", [128, 128], BF16, kind="ExternalInput").ap()
    d_tri = nc.dram_tensor("trib", [128, 128], BF16, kind="ExternalInput").ap()
    d_one = nc.dram_tensor("onesb", [128, 128], BF16, kind="ExternalInput").ap()
    d_out = nc.dram_tensor("out", [1024, H], F32, kind="ExternalOutput").ap()

    from contextlib import ExitStack
    with ExitStack() as ctx:
        stream = ctx.enter_context(nc.sbuf_tensor("stream", [128, NT * H], F32))
        xt = ctx.enter_context(nc.sbuf_tensor("xt", [128, NT * H], BF16))
        arena = ctx.enter_context(nc.sbuf_tensor("arena", [128, 22528], BF16))
        wb0 = ctx.enter_context(nc.sbuf_tensor("wb0", [128, 4096], BF16))
        wb1 = ctx.enter_context(nc.sbuf_tensor("wb1", [128, 4096], BF16))
        ws0 = ctx.enter_context(nc.sbuf_tensor("ws0", [128, IC * 1024], BF16))
        ws1 = ctx.enter_context(nc.sbuf_tensor("ws1", [128, IC * 1024], BF16))
        sg = ctx.enter_context(nc.sbuf_tensor("sg", [128, 1024], F32))
        comb = ctx.enter_context(nc.sbuf_tensor("comb", [128, NT * 16], F32))
        stv = ctx.enter_context(nc.sbuf_tensor("stat", [128, 64], F32))
        gsm = ctx.enter_context(nc.sbuf_tensor("gsm", [128, 16], F32))
        g16 = ctx.enter_context(nc.sbuf_tensor("g16", [128, 16], F32))
        m8 = ctx.enter_context(nc.sbuf_tensor("m8", [128, 8], F32))
        pb0 = ctx.enter_context(nc.psum_tensor("pb0", [128, 512], F32))
        pb1 = ctx.enter_context(nc.psum_tensor("pb1", [128, 512], F32))
        pb2 = ctx.enter_context(nc.psum_tensor("pb2", [128, 512], F32))
        pb3 = ctx.enter_context(nc.psum_tensor("pb3", [128, 512], F32))
        pb4 = ctx.enter_context(nc.psum_tensor("pb4", [128, 512], F32))
        pb5 = ctx.enter_context(nc.psum_tensor("pb5", [128, 512], F32))
        pbtA = ctx.enter_context(nc.psum_tensor("pbtA", [128, 1024], BF16))
        pbtB = ctx.enter_context(nc.psum_tensor("pbtB", [128, 1024], BF16))
        sem_sd = ctx.enter_context(nc.semaphore("sem_sd"))
        sem_gd = ctx.enter_context(nc.semaphore("sem_gd"))
        sem_t = ctx.enter_context(nc.semaphore("sem_t"))
        sem_v = ctx.enter_context(nc.semaphore("sem_v"))
        sem_s = ctx.enter_context(nc.semaphore("sem_s"))
        sem_kva = ctx.enter_context(nc.semaphore("sem_kva"))
        sem_kvb = ctx.enter_context(nc.semaphore("sem_kvb"))
        sem_w1a = ctx.enter_context(nc.semaphore("sem_w1a"))
        sem_w1b = ctx.enter_context(nc.semaphore("sem_w1b"))
        sem_w2a = ctx.enter_context(nc.semaphore("sem_w2a"))
        sem_w2b = ctx.enter_context(nc.semaphore("sem_w2b"))
        sems = {"sd": sem_sd, "gd": sem_gd, "t": sem_t, "v": sem_v, "s": sem_s,
                "kva": sem_kva, "kvb": sem_kvb, "w1a": sem_w1a, "w1b": sem_w1b,
                "w2a": sem_w2a, "w2b": sem_w2b}
        pbs = [pb0, pb1, pb2, pb3, pb4, pb5]
        pbth = [pbtA, pbtB]
        wbufs = [wb0, wb1]
        wsbufs = [ws0, ws1]

        def ar(off, n):
            return arena[:, off:off + n]
        VB = [ar(0, 2048), ar(2048, 2048)]
        KTB = [ar(4096, 2048), ar(6144, 2048)]
        PSB = ar(8192, 2048)
        PTB = ar(10240, 2048)
        QTA = ar(12288, 1024)
        BXS = ar(13312, 2048)
        BYS = ar(15360, 2048)
        IDT = ar(17408, 128)
        TRI = ar(17536, 128)
        ONE = ar(17664, 128)
        GWS = ar(17792, 256)
        TMPB = ar(18048, 2048)
        ABUF = [arena[:, 0:11264], arena[:, 11264:22528]]


        P = Plan()
        cp = {}

        for tt in range(NT):
            def ld(E, tt=tt):
                return E.dma_start(out=stream[:, tt * H:(tt + 1) * H],
                                   in_=d_xmy[tt * 128:(tt + 1) * 128, :])
            P.emit("sync", ld, inc="sd", amt=16)
        for dst, src in ((IDT, d_id), (TRI, d_tri), (ONE, d_one)):
            def ld(E, dst=dst, src=src):
                return E.dma_start(out=dst, in_=src)
            P.emit("sync", ld, inc="sd", amt=16)
        cp["stream"] = P.cnt["sd"]
        cp["consts_sd"] = P.cnt["sd"]

        if LIMIT["heads"] < NH or LIMIT["experts"] < 18:
            for src in (d_xkv[0:128, 0:128], d_w13[0, 0, :, 0:128], d_sw13[0, :, 0:128],
                        d_w2[0, 0, 0, :, 0:128], d_sw2[0, 0, :, 0:128], d_gw[0],
                        d_bx[0, :, 0:128], d_by[0, :, 0:128]):
                def ld(E, src=src):
                    return E.dma_start(out=wb0[:, 0:src.shape[-1]], in_=src)
                P.emit("gpsimd", ld, waits={"gd": P.cnt["gd"]}, inc="gd", amt=16)

        for half in range(2):
            def ld(E, half=half):
                return E.dma_start(
                    out=xt[:, half * 4 * H:(half + 1) * 4 * H].rearrange("p (tt e) -> p tt e", tt=4),
                    in_=d_xmy[half * 512:(half + 1) * 512, :].rearrange("(tt p) e -> p tt e", p=128))
            P.emit("gpsimd", ld, inc="gd", amt=16)
        for dst, src in ((BXS, d_bx), (BYS, d_by), (GWS, d_gw)):
            def ld(E, dst=dst, src=src):
                return E.dma_start(out=dst.rearrange("p (a b) -> p a b", b=src.shape[2]),
                                   in_=src.rearrange("a p b -> p a b"))
            P.emit("gpsimd", ld, inc="gd", amt=16)
        cp["xtb"] = P.cnt["gd"]
        cp["consts_gd"] = P.cnt["gd"]

        w13_emitted = set()

        def emit_w13_load(gj):
            if gj >= len(W13SEQ) or gj in w13_emitted:
                return
            w13_emitted.add(gj)
            kind, e, k = W13SEQ[gj]
            buf = gj % 2
            waits = {}
            if ("w13used", buf) in cp:
                waits["t"] = cp[("w13used", buf)]
            if kind == "r":
                def ld(E, e=e, k=k, buf=buf):
                    return E.dma_start(
                        out=wbufs[buf][:].rearrange("p (g c) -> p g c", g=2),
                        in_=d_w13[e, 2 * k:2 * k + 2].rearrange("g p c -> p g c"))
            else:
                def ld(E, k=k, buf=buf):
                    return E.dma_start(
                        out=wbufs[buf][:].rearrange("p (g c) -> p g c", g=2),
                        in_=d_sw13[2 * k:2 * k + 2].rearrange("g p c -> p g c"))
            sk = "w1a" if buf == 0 else "w1b"
            cp[("w13ld", gj)] = (sk, P.emit("gpsimd", ld, waits=waits, inc=sk, amt=16)[sk])

        w2_emitted = set()

        def emit_w2_load(wj):
            if wj >= len(W2SEQ) or wj in w2_emitted:
                return
            w2_emitted.add(wj)
            kind, e, hh = W2SEQ[wj]
            buf = wj % 2
            waits = {}
            if ("w2used", buf) in cp:
                waits["t"] = cp[("w2used", buf)]
            if kind == "r":
                def ld(E, e=e, hh=hh, buf=buf):
                    return E.dma_start(
                        out=wsbufs[buf][:].rearrange("p (i c) -> p i c", i=IC),
                        in_=d_w2[e, hh].rearrange("i p c -> p i c"))
            else:
                def ld(E, hh=hh, buf=buf):
                    return E.dma_start(
                        out=wsbufs[buf][:].rearrange("p (i c) -> p i c", i=IC),
                        in_=d_sw2[hh].rearrange("i p c -> p i c"))
            sk = "w2a" if buf == 0 else "w2b"
            cp[("w2ld", wj)] = (sk, P.emit("gpsimd", ld, waits=waits, inc=sk, amt=16)[sk])

        def emit_vload(h):
            waits = {}
            if h >= 2:
                waits["t"] = cp.get(("pv_done", h - 2), 0)
            def ld(E, h=h):
                return E.dma_start(
                    out=VB[h % 2].rearrange("p (kt d) -> p kt d", kt=16),
                    in_=d_xkv[:, h * 128:(h + 1) * 128].rearrange("(kt p) d -> p kt d", p=128))
            sk = "kva" if h % 2 == 0 else "kvb"
            cp[("vload", h)] = (sk, P.emit("gpsimd", ld, waits=waits, inc=sk, amt=16)[sk])

        if nheads > 0:
            emit_vload(0)
        if nheads > 1:
            emit_vload(1)

        pbt_free = [0, 0]

        for h in range(nheads):
            buf = h % 2
            for g in range(4 if A >= 1 else 0):
                half = g % 2
                vsk, vsv = cp[("vload", h)]
                waits = {vsk: vsv, "v": pbt_free[half], "sd": cp["consts_sd"]}
                def tr(E, g=g, half=half, buf=buf):
                    ins = None
                    for j in range(4):
                        ins = E.transpose(pbth[half][:, j * 128:(j + 1) * 128],
                                          VB[buf][:, (g * 4 + j) * 128:(g * 4 + j + 1) * 128], IDT)
                    return ins
                tcnt = P.emit("tensor", tr, waits=waits, inc="t")["t"]
                def ev(E, g=g, half=half, buf=buf):
                    return E.tensor_copy(KTB[buf][:, g * 512:(g + 1) * 512],
                                         pbth[half][:, 0:512])
                pbt_free[half] = P.emit("vector", ev, waits={"t": tcnt}, inc="v")["v"]
                cp[("ktev", h, g)] = pbt_free[half]
            for g in ((1, 0) if A >= 2 else ()):
                half = g % 2
                waits = {"gd": cp["xtb"], "v": pbt_free[half]}
                if h >= 1:
                    waits["t"] = cp.get(("qk_done", h - 1, 4 * g), 0)
                def tr(E, g=g, half=half, h=h):
                    ins = None
                    for j in range(4):
                        s = g * 4 + j
                        ins = E.transpose(pbth[half][:, j * 128:(j + 1) * 128],
                                          xt[:, s * H + h * 128: s * H + (h + 1) * 128], IDT)
                    return ins
                tcnt = P.emit("tensor", tr, waits=waits, inc="t")["t"]
                def ev(E, g=g, half=half):
                    return E.tensor_copy(QTA[:, g * 512:(g + 1) * 512],
                                         pbth[half][:, 0:512])
                pbt_free[half] = P.emit("vector", ev, waits={"t": tcnt}, inc="v")["v"]
                cp[("qtev", h, g)] = pbt_free[half]
            cp[("qtready", h)] = P.cnt["v"]

            def emit_qk(s, i):
                N = N_LIST[s]
                ncols = N * 128
                kbn = (ncols + 511) // 512
                par = i % 2
                ktw = cp[("ktev", h, (N + 3) // 4 - 1)]
                qtw = cp[("qtev", h, s // 4)]
                waits = {"v": max(ktw, qtw), "gd": cp["consts_gd"]}
                if (h, i) != (0, 0):
                    ph, ps = (h, SEQ[i - 1]) if i > 0 else (h - 1, SEQ[7])
                    waits["s"] = cp.get(("exp_done", ph, ps), 0)
                def qk(E, s=s, ncols=ncols, kbn=kbn, h=h):
                    ins = None
                    for kb in range(kbn):
                        n = min(512, ncols - kb * 512)
                        last = kb == kbn - 1
                        ins = E.matmul(pbs[kb][:, 0:n], QTA[:, s * 128:(s + 1) * 128],
                                       KTB[h % 2][:, kb * 512: kb * 512 + n],
                                       start=True, stop=not last)
                        if last:
                            noff = n - 256
                            ins = E.matmul(pbs[kb][:, noff:noff + 256], TRI,
                                           BXS[:, s * 256:(s + 1) * 256],
                                           start=False, stop=False)
                            ins = E.matmul(pbs[kb][:, noff:noff + 256], ONE,
                                           BYS[:, s * 256:(s + 1) * 256],
                                           start=False, stop=True)
                    return ins
                cp[("qk_done", h, s)] = P.emit("tensor", qk, waits=waits, inc="t")["t"]
                waits = {"t": cp[("qk_done", h, s)]}
                if not (h == 0 and i < 2):
                    ph, ps = (h, SEQ[i - 2]) if i >= 2 else (h - 1, SEQ[6 + i])
                    waits["s"] = cp.get(("exp_done", ph, ps), 0)
                def mx(E, s=s, ncols=ncols, kbn=kbn, par=par):
                    for kb in range(kbn):
                        n = min(512, ncols - kb * 512)
                        E.reduce_max(stv[:, 59 + kb:60 + kb], pbs[kb][:, 0:n], axis=AX.X)
                    E.drain()
                    E.reduce_max(stv[:, 48 + par:49 + par], stv[:, 59:59 + kbn],
                                 axis=AX.X, negate=True)
                    E.drain()
                    return E.tensor_scalar(stv[:, 48 + par:49 + par],
                                           stv[:, 48 + par:49 + par], SCALE, None,
                                           op0=ALU.mult)
                cp[("negm", h, s)] = P.emit("vector", mx, waits=waits, inc="v")["v"]

            def emit_exp(s, i):
                N = N_LIST[s]
                ncols = N * 128
                kbn = (ncols + 511) // 512
                par = i % 2
                waits = {"v": cp[("negm", h, s)]}
                if (h, i) != (0, 0):
                    ph, ps = (h, SEQ[i - 1]) if i > 0 else (h - 1, SEQ[7])
                    waits["t"] = cp.get(("ptr_done", ph, ps), 0)
                def ex(E, s=s, ncols=ncols, kbn=kbn, par=par):
                    ins = None
                    for kb in range(kbn):
                        n = min(512, ncols - kb * 512)
                        ins = E.activation(PSB[:, kb * 512: kb * 512 + n], pbs[kb][:, 0:n],
                                           AF.Exp, bias=stv[:, 48 + par:49 + par],
                                           scale=SCALE,
                                           accum_out=stv[:, 40 + par * 4 + kb:41 + par * 4 + kb])
                    return ins
                cp[("exp_done", h, s)] = P.emit("scalar", ex, waits=waits, inc="s")["s"]
                def lv(E, s=s, kbn=kbn, par=par):
                    E.reduce_sum(stv[:, 50 + par:51 + par],
                                 stv[:, 40 + par * 4: 40 + par * 4 + kbn], axis=AX.X)
                    E.drain()
                    E.reciprocal(stv[:, 52 + par:53 + par], stv[:, 50 + par:51 + par])
                    return E.drain()
                P.emit("vector", lv, waits={"s": cp[("exp_done", h, s)]}, inc=None)

            def emit_pt(s):
                N = N_LIST[s]
                kt = 0
                while kt < N:
                    m = min(4, N - kt)
                    half = (kt // 4) % 2
                    waits = {"s": cp[("exp_done", h, s)], "v": pbt_free[half]}
                    def tr(E, kt=kt, m=m, half=half):
                        ins = None
                        for j in range(m):
                            ins = E.transpose(pbth[half][:, j * 128:(j + 1) * 128],
                                              PSB[:, (kt + j) * 128:(kt + j + 1) * 128], IDT)
                        return ins
                    tcnt = P.emit("tensor", tr, waits=waits, inc="t")["t"]
                    if kt + m >= N:
                        cp[("ptr_done", h, s)] = tcnt
                    def ev(E, kt=kt, m=m, half=half):
                        return E.tensor_copy(PTB[:, kt * 128:(kt + m) * 128],
                                             pbth[half][:, 0:m * 128])
                    pbt_free[half] = P.emit("vector", ev, waits={"t": tcnt}, inc="v")["v"]
                    kt += m
                cp[("ptev", h, s)] = P.cnt["v"]

            def emit_pv(s, i):
                N = N_LIST[s]
                par = i % 2
                waits = {"v": max(cp[("ptev", h, s)], cp.get(("pv_evac_prev",), 0))}
                def pv(E, s=s, N=N, h=h):
                    ins = None
                    for kt in range(N):
                        ins = E.matmul(pb4[:, 0:128], PTB[:, kt * 128:(kt + 1) * 128],
                                       VB[h % 2][:, kt * 128:(kt + 1) * 128],
                                       start=(kt == 0), stop=(kt == N - 1))
                    return ins
                tcnt = P.emit("tensor", pv, waits=waits, inc="t")["t"]
                cp[("pv_done_slot", h, s)] = tcnt
                if A < 7:
                    return
                def evp(E, s=s, h=h, par=par):
                    return E.scalar_tensor_tensor(
                        out=stream[:, s * H + h * 128: s * H + (h + 1) * 128],
                        in0=pb4[:, 0:128], scalar=stv[:, 52 + par:53 + par],
                        in1=stream[:, s * H + h * 128: s * H + (h + 1) * 128],
                        op0=ALU.mult, op1=ALU.add)
                cp[("pv_evac_prev",)] = P.emit(
                    "vector", evp, waits={"t": tcnt, "sd": cp["stream"]}, inc="v")["v"]

            if A >= 3:
                emit_qk(SEQ[0], 0)
            if A >= 4:
                emit_exp(SEQ[0], 0)
            for i in range(1, 8):
                if A >= 3:
                    emit_qk(SEQ[i], i)
                if A >= 5:
                    emit_pt(SEQ[i - 1])
                if A >= 4:
                    emit_exp(SEQ[i], i)
                if A >= 6:
                    emit_pv(SEQ[i - 1], i - 1)
            if A >= 5:
                emit_pt(SEQ[7])
            if A >= 6:
                emit_pv(SEQ[7], 7)
                cp[("pv_done", h)] = cp[("pv_done_slot", h, SEQ[7])]
            if h + 2 < nheads:
                emit_vload(h + 2)
            if h == 1:
                emit_w13_load(0)
            elif h == 2:
                emit_w13_load(1)
            elif h == 3:
                emit_w2_load(0)
            elif h == 6:
                emit_w2_load(1)

        if LIMIT["experts"] > 0:
            emit_w13_load(0)
            emit_w13_load(1)
            emit_w2_load(0)
            emit_w2_load(1)

        def emit_ln():
            for tt in range(NT):
                def rs(E, tt=tt):
                    if tt == 0:
                        E.drain()
                    return E.reduce_sum(stv[:, tt:tt + 1], stream[:, tt * H:(tt + 1) * H], axis=AX.X)
                P.emit("vector", rs, inc="v")
            vafter = P.cnt["v"]
            for tt in range(NT):
                def sq(E, tt=tt):
                    return E.activation(ABUF[0][:, 0:2048], stream[:, tt * H:(tt + 1) * H],
                                        AF.Square, accum_out=stv[:, 8 + tt:9 + tt])
                P.emit("scalar", sq,
                       waits={"v": vafter, "s": cp.get(("sq_prev",), 0)}, inc="s")
                cp[("sq_prev",)] = P.cnt["s"]
            sq_done = P.cnt["s"]
            def fin1(E):
                E.drain()
                E.tensor_scalar(stv[:, 16:24], stv[:, 0:8], 1.0 / H, None, op0=ALU.mult)
                E.tensor_scalar(stv[:, 24:32], stv[:, 8:16], 1.0 / H, None, op0=ALU.mult)
                E.drain()
                E.tensor_mul(stv[:, 32:40], stv[:, 16:24], stv[:, 16:24])
                E.drain()
                E.tensor_sub(stv[:, 24:32], stv[:, 24:32], stv[:, 32:40])
                E.drain()
                return E.tensor_scalar(stv[:, 24:32], stv[:, 24:32], 1e-5, None, op0=ALU.add)
            c1 = P.emit("vector", fin1, waits={"s": sq_done}, inc="v")["v"]
            def fsq(E):
                return E.activation(stv[:, 32:40], stv[:, 24:32], AF.Sqrt)
            c2 = P.emit("scalar", fsq, waits={"v": c1}, inc="s")["s"]
            def fin2(E):
                E.drain()
                E.reciprocal(stv[:, 40:48], stv[:, 32:40])
                E.drain()
                E.tensor_mul(stv[:, 32:40], stv[:, 40:48], stv[:, 40:48])
                E.drain()
                E.tensor_mul(stv[:, 32:40], stv[:, 32:40], stv[:, 24:32])
                E.drain()
                E.tensor_scalar(stv[:, 32:40], stv[:, 32:40], -0.5, 1.5, op0=ALU.mult, op1=ALU.add)
                E.drain()
                E.tensor_mul(stv[:, 24:32], stv[:, 40:48], stv[:, 32:40])
                return E.drain()
            P.emit("vector", fin2, waits={"s": c2}, inc=None)
            for tt in range(NT):
                def ap(E, tt=tt):
                    return E.tensor_scalar(stream[:, tt * H:(tt + 1) * H],
                                           stream[:, tt * H:(tt + 1) * H],
                                           stv[:, 16 + tt:17 + tt], stv[:, 24 + tt:25 + tt],
                                           op0=ALU.subtract, op1=ALU.mult)
                P.emit("vector", ap, inc="v")
            return P.cnt["v"]

        if LIMIT["ln1"]:
            emit_ln()

        p3range = range(NT) if (LIMIT["p3"] or LIMIT["experts"] > 0) else range(0)
        for tt in p3range:
            def cst(E, tt=tt):
                if tt == 0:
                    E.drain()
                return E.tensor_copy(TMPB, stream[:, tt * H:(tt + 1) * H])
            ccast = P.emit("vector", cst, waits={"t": cp.get(("x1tr", tt - 1), 0)},
                           inc="v")["v"]
            for g in range(4):
                half = g % 2
                waits = {"v": max(ccast, pbt_free[half])}
                if tt == 0 and g < 2 and nheads > 0:
                    waits["t"] = cp.get(("pv_done", nheads - 1), 0)
                def tr(E, g=g, half=half):
                    ins = None
                    for j in range(4):
                        hc = g * 4 + j
                        ins = E.transpose(pbth[half][:, j * 128:(j + 1) * 128],
                                          TMPB[:, hc * 128:(hc + 1) * 128], IDT)
                    return ins
                tcnt = P.emit("tensor", tr, waits=waits, inc="t")["t"]
                def ev(E, g=g, half=half, tt=tt):
                    return E.tensor_copy(
                        xt[:].rearrange("p (hc t) -> p hc t", hc=16)[:, g * 4:(g + 1) * 4,
                                                                     tt * 128:(tt + 1) * 128],
                        pbth[half][:, 0:512].rearrange("p (a b) -> p a b", a=4))
                pbt_free[half] = P.emit("vector", ev, waits={"t": tcnt}, inc="v")["v"]
            cp[("x1t", tt)] = P.cnt["v"]
            cp[("x1tr", tt)] = P.cnt["t"]

        if len(p3range) > 0:
            waits = {"v": max(cp[("x1t", NT - 1)], cp.get(("pv_evac_prev",), 0)),
                     "gd": cp["consts_gd"]}
            def gmm(E):
                ins = None
                for hc in range(16):
                    for tt in range(NT):
                        ins = E.matmul(pb4[:, tt * 16:(tt + 1) * 16],
                                       xt[:, hc * 1024 + tt * 128: hc * 1024 + (tt + 1) * 128],
                                       GWS[:, hc * 16:(hc + 1) * 16],
                                       start=(hc == 0 and tt == 0),
                                       stop=(hc == 15 and tt == NT - 1),
                                       skip_group_check=True)
                return ins
            gq = P.emit("tensor", gmm, waits=waits, inc="t")["t"]
        for tt in p3range:
            waits = {"t": gq}
            if tt > 0:
                waits["s"] = cp[("gexp", tt - 1)]
            def gmx(E, tt=tt):
                return E.reduce_max(stv[:, 54:55], pb4[:, tt * 16:(tt + 1) * 16],
                                    axis=AX.X, negate=True)
            gv = P.emit("vector", gmx, waits=waits, inc="v")["v"]
            def gex(E, tt=tt):
                return E.activation(gsm[:], pb4[:, tt * 16:(tt + 1) * 16], AF.Exp,
                                    bias=stv[:, 54:55], accum_out=stv[:, 55:56])
            cp[("gexp", tt)] = P.emit("scalar", gex, waits={"v": gv}, inc="s")["s"]
            def gtk(E, tt=tt):
                E.drain()
                E.reciprocal(stv[:, 56:57], stv[:, 55:56])
                E.drain()
                E.tensor_scalar_mul(gsm[:], gsm[:], stv[:, 56:57])
                E.drain()
                E.max(out=m8[:], in_=gsm[:])
                E.drain()
                E.tensor_scalar(g16[:], gsm[:], m8[:, 5:6], None, op0=ALU.is_ge)
                E.drain()
                E.tensor_mul(g16[:], gsm[:], g16[:])
                E.drain()
                E.reduce_sum(stv[:, 57:58], g16[:], axis=AX.X)
                E.drain()
                E.tensor_scalar(stv[:, 57:58], stv[:, 57:58], 1e-20, None, op0=ALU.add)
                E.drain()
                E.reciprocal(stv[:, 58:59], stv[:, 57:58])
                E.drain()
                return E.tensor_scalar(comb[:, tt * 16:(tt + 1) * 16], g16[:],
                                       stv[:, 58:59], None, op0=ALU.mult)
            P.emit("vector", gtk, waits={"s": cp[("gexp", tt)]}, inc="v")
        def gdr(E):
            return E.drain()
        P.emit("vector", gdr, inc=None)
        comb_done = P.cnt["v"]

        cp[("m1_silu_prev",)] = cp.get(("exp_done", nheads - 1, 7), 0)

        def emit_m1(gj0, pairs_icols):
            for gj, ab, icol in pairs_icols:
                emit_w13_load(gj)
                wbuf = wbufs[gj % 2]
                wsk, wsv = cp[("w13ld", gj)]
                waits = {wsk: wsv, "v": cp[("x1t", NT - 1)],
                         "s": cp[("m1_silu_prev",)]}
                def gmmf(E, wbuf=wbuf):
                    ins = None
                    for hc in range(16):
                        for tb in range(2):
                            ins = E.matmul(pbs[tb][:, 0:512],
                                           wbuf[:, hc * 128:(hc + 1) * 128],
                                           xt[:, hc * 1024 + tb * 512: hc * 1024 + (tb + 1) * 512],
                                           start=(hc == 0), stop=(hc == 15),
                                           skip_group_check=True)
                    return ins
                gcnt = P.emit("tensor", gmmf, waits=waits, inc="t")["t"]
                waits = {}
                wv = max(cp.get(("m1_aev_prev",), 0), cp.get(("m2ev_odd",), 0))
                if wv:
                    waits["v"] = wv
                def ummf(E, wbuf=wbuf):
                    ins = None
                    for hc in range(16):
                        for tb in range(2):
                            ins = E.matmul(pbs[2 + tb][:, 0:512],
                                           wbuf[:, 2048 + hc * 128: 2048 + (hc + 1) * 128],
                                           xt[:, hc * 1024 + tb * 512: hc * 1024 + (tb + 1) * 512],
                                           start=(hc == 0), stop=(hc == 15),
                                           skip_group_check=True)
                    return ins
                ucnt = P.emit("tensor", ummf, waits=waits, inc="t")["t"]
                cp[("w13used", gj % 2)] = ucnt
                emit_w13_load(gj + 2)
                def slf(E):
                    E.activation(sg[:, 0:512], pbs[0][:, 0:512], AF.Silu)
                    return E.activation(sg[:, 512:1024], pbs[1][:, 0:512], AF.Silu)
                scnt = P.emit("scalar", slf,
                              waits={"t": gcnt, "v": cp.get(("m1_aev_prev",), 0)},
                              inc="s")["s"]
                cp[("m1_silu_prev",)] = scnt
                def aevf(E, ab=ab, icol=icol):
                    E.tensor_mul(ABUF[ab][:, icol * 1024: icol * 1024 + 512],
                                 sg[:, 0:512], pbs[2][:, 0:512])
                    return E.tensor_mul(ABUF[ab][:, icol * 1024 + 512: (icol + 1) * 1024],
                                        sg[:, 512:1024], pbs[3][:, 0:512])
                acnt = P.emit("vector", aevf, waits={"t": ucnt, "s": scnt}, inc="v")["v"]
                cp[("m1_aev_prev",)] = acnt
            return P.cnt["v"]

        def emit_m2(wj0, adep, ab_fn, scalar_is_one, eidx):
            for hh in range(2):
                wj = wj0 + hh
                wsb = wsbufs[wj % 2]
                for tt in range(NT):
                    bk = (pb4, pb5) if tt % 2 == 0 else (pb2, pb3)
                    pkey = ("m2ev_even",) if tt % 2 == 0 else ("m2ev_odd",)
                    if ("w2ld", wj) not in cp:
                        emit_w2_load(wj)
                    w2k, w2v = cp[("w2ld", wj)]
                    waits = {w2k: w2v, "v": adep,
                             "s": cp[("m1_silu_prev",)]}
                    if pkey in cp:
                        waits["v"] = max(waits["v"], cp[pkey])
                    def mmf(E, tt=tt, wsb=wsb, bk=bk, ab_fn=ab_fn):
                        ins = None
                        for ici in range(IC):
                            abuf, icol = ab_fn(ici)
                            lh = ABUF[abuf][:, icol * 1024 + tt * 128: icol * 1024 + (tt + 1) * 128]
                            for hb in range(2):
                                ins = E.matmul(bk[hb][:, 0:512], lh,
                                               wsb[:, ici * 1024 + hb * 512: ici * 1024 + (hb + 1) * 512],
                                               start=(ici == 0), stop=(ici == IC - 1),
                                               skip_group_check=True)
                        return ins
                    tcnt = P.emit("tensor", mmf, waits=waits, inc="t")["t"]
                    if tt == NT - 1:
                        cp[("w2used", wj % 2)] = tcnt
                    def evf(E, hh=hh, tt=tt, bk=bk, scalar_is_one=scalar_is_one, eidx=eidx):
                        ins = None
                        for hb in range(2):
                            dst = stream[:, tt * H + hh * 1024 + hb * 512:
                                         tt * H + hh * 1024 + (hb + 1) * 512]
                            sc = 1.0 if scalar_is_one else comb[:, tt * 16 + eidx: tt * 16 + eidx + 1]
                            ins = E.scalar_tensor_tensor(out=dst, in0=bk[hb][:, 0:512],
                                                         scalar=sc, in1=dst,
                                                         op0=ALU.mult, op1=ALU.add)
                        return ins
                    c = P.emit("vector", evf,
                               waits={"t": tcnt,
                                      "v": max(comb_done,
                                               cp.get(("evf_prev", tt), 0))},
                               inc="v")["v"]
                    cp[pkey] = c
                    cp[("evf_prev", tt)] = c
                emit_w2_load(wj + 2)

        for e in range(nroute):
            adep = emit_m1(e * 11, [(e * 11 + k, e % 2, k) for k in range(11)])
            emit_m2(2 * e, adep, lambda ici, e=e: (e % 2, ici),
                    scalar_is_one=False, eidx=e)

        if do_shared:
            adep_a = emit_m1(176, [(176 + k, 0, k) for k in range(11)])
            emit_m2(32, adep_a, lambda ici: (0, ici), scalar_is_one=True, eidx=0)
            adep_b = emit_m1(187, [(187 + j, 1, j) for j in range(11)])
            emit_m2(34, adep_b, lambda ici: (1, ici), scalar_is_one=True, eidx=0)

        if LIMIT["ln2"]:
            emit_ln()
        st_wait = P.cnt["v"]
        for tt in range(NT):
            def st(E, tt=tt):
                return E.dma_start(out=d_out[tt * 128:(tt + 1) * 128, :],
                                   in_=stream[:, tt * H:(tt + 1) * H])
            P.emit("sync", st, waits={"v": st_wait}, inc="sd", amt=16)
        final_sd = P.cnt["sd"]
        def fw(E):
            return E.wait_ge(sems["sd"], final_sd)
        P.ops["sync"].append(({}, fw, None, 0))

        with nc.Block() as block:
            def replay(E, name):
                lastw = {}
                for waits, fn, inc, amt in P.ops[name]:
                    for sname, val in waits.items():
                        if val > 0 and lastw.get(sname, -1) < val:
                            E.wait_ge(sems[sname], val)
                            lastw[sname] = val
                    ins = fn(E)
                    if inc is not None:
                        ins.then_inc(sems[inc], amt)

            @block.sync
            def _(E):
                replay(E, "sync")

            @block.gpsimd
            def _(E):
                replay(E, "gpsimd")

            @block.tensor
            def _(E):
                replay(E, "tensor")

            @block.vector
            def _(E):
                replay(E, "vector")

            @block.scalar
            def _(E):
                replay(E, "scalar")

    return nc


def _host_prep(L0_ffn_gate_w, L0_ffn_w13, L0_ffn_w2, L0_ffn_sw13, L0_ffn_sw2):
    gw = np.asarray(L0_ffn_gate_w, dtype=np.float32)
    w13 = np.asarray(L0_ffn_w13, dtype=np.float32)
    w2 = np.asarray(L0_ffn_w2, dtype=np.float32)
    sw13 = np.asarray(L0_ffn_sw13, dtype=np.float32)
    sw2 = np.asarray(L0_ffn_sw2, dtype=np.float32)

    w13t = w13.reshape(16, 22, 128, 16, 128).transpose(0, 1, 4, 3, 2).reshape(16, 22, 128, 2048)
    w13t = np.ascontiguousarray(w13t[:, JORD])
    sw13t = sw13.reshape(44, 128, 16, 128).transpose(0, 3, 2, 1).reshape(44, 128, 2048)
    sw13t = np.ascontiguousarray(sw13t[SJORD])
    w2t = np.ascontiguousarray(
        w2.transpose(0, 2, 1).reshape(16, 11, 128, 2, 1024).transpose(0, 3, 1, 2, 4))
    sw2t = np.ascontiguousarray(
        sw2.T.reshape(2, 11, 128, 2, 1024).transpose(0, 3, 1, 2, 4).reshape(4, 11, 128, 1024))
    gwt = np.ascontiguousarray(gw.T.reshape(16, 128, 16))

    identb = np.eye(128, dtype=ml_dtypes.bfloat16)
    dd = np.arange(128)[:, None]
    qq = np.arange(128)[None, :]
    trib = (dd > qq).astype(ml_dtypes.bfloat16)
    onesb = np.ones((128, 128), dtype=ml_dtypes.bfloat16)

    kl = np.arange(128)[None, :]
    lower_eq = (dd <= kl).astype(np.float32)
    bx = np.zeros((2, 8, 128, 256), dtype=np.float32)
    by = np.zeros((2, 8, 128, 256), dtype=np.float32)
    for s in range(8):
        bx[0, s, :, 128:256] = -CBIG * lower_eq
        bx[1, s, :, 0:128] = -CBIG * lower_eq
        by[1, s, :, 128:256] = -CBIG / 128.0
    return w13t, sw13t, w2t, sw2t, gwt, identb, trib, onesb, bx, by


def _row_tiles(role):
    return [(N_LIST[s] - 1 - role) for s in range(8)]


def kernel(L0_attn_x, L0_ffn_gate_w, L0_ffn_w13, L0_ffn_w2, L0_ffn_sw13, L0_ffn_sw2):
    x = np.asarray(L0_attn_x, dtype=np.float32)
    w13t, sw13t, w2t, sw2t, gwt, identb, trib, onesb, bx, by = _host_prep(
        L0_ffn_gate_w, L0_ffn_w13, L0_ffn_w2, L0_ffn_sw13, L0_ffn_sw2)

    nc = build_program()

    in_maps = []
    for c in range(8):
        b, role = c // 2, c % 2
        tiles = _row_tiles(role)
        rows = np.concatenate([np.arange(t * 128, (t + 1) * 128) for t in tiles])
        in_maps.append({
            "xmy": np.ascontiguousarray(x[b][rows]),
            "xkv": np.ascontiguousarray(x[b]),
            "w13t": w13t,
            "sw13t": sw13t,
            "w2t": w2t,
            "sw2t": sw2t,
            "gwt": gwt,
            "bxm": np.ascontiguousarray(bx[role]),
            "bym": np.ascontiguousarray(by[role]),
            "identb": identb,
            "trib": trib,
            "onesb": onesb,
        })

    res = run_bass_kernel_spmd(nc, in_maps, core_ids=list(range(8)))

    out = np.empty((B, L, H), dtype=np.float32)
    for c in range(8):
        b, role = c // 2, c % 2
        o = res.results[c]["out"]
        for s, t in enumerate(_row_tiles(role)):
            out[b, t * 128:(t + 1) * 128, :] = o[s * 128:(s + 1) * 128, :]
    return out


# version 05
# speedup vs baseline: 1.2175x; 1.2175x over previous
import numpy as np
import ml_dtypes
import concourse.bass as bass
import concourse.mybir as mybir
from concourse.bass_utils import run_bass_kernel_spmd

F32 = mybir.dt.float32
BF16 = mybir.dt.bfloat16
FP16 = mybir.dt.float16
AF = mybir.ActivationFunctionType
ALU = mybir.AluOpType
AX = mybir.AxisListType

B = 4
L = 2048
H = 2048
NH = 16
NT = 8
NE = 16
IC = 11
SCALE = 1.0 / float(np.sqrt(128.0))
CBIG = 1.0e30
N_LIST = [16, 14, 12, 10, 8, 6, 4, 2]
SEQ = [7, 6, 5, 4, 3, 2, 1, 0]

JORD = [x for i in range(11) for x in (i, 11 + i)]
SJORD = [x for i in range(22) for x in (i, 22 + i)]

W13SEQ = [("s", th, k) for th in range(2) for k in range(22)] + \
         [("r", e, k) for e in range(NE) for k in range(11)]
W2SEQ = [("s", th, q, hb) for th in range(2) for q in range(4) for hb in range(2)] + \
        [("r", e, hh, hb) for e in range(NE) for hh in range(2) for hb in range(2)]


class Plan:
    def __init__(self):
        self.ops = {"tensor": [], "vector": [], "scalar": [], "sync": [], "gpsimd": []}
        self.cnt = {"sd": 0, "gd": 0, "t": 0, "v": 0, "s": 0,
                    "kva": 0, "kvb": 0, "w1a": 0, "w1b": 0, "w2a": 0, "w2b": 0}

    def emit(self, eng, fn, waits=None, inc=None, amt=1):
        w = {k: v for k, v in (waits or {}).items() if v and v > 0}
        self.ops[eng].append((w, fn, inc, amt))
        if inc is not None:
            self.cnt[inc] += amt
        return dict(self.cnt)


LIMIT = {"heads": NH, "ln1": 1, "p3": 1, "experts": 18, "ln2": 1, "astage": 9}


def build_program():
    nheads = LIMIT["heads"]
    nroute = min(NE, LIMIT["experts"])
    do_shared = LIMIT["experts"] >= 17
    A = LIMIT["astage"]
    nc = bass.Bass("TRN2", target_bir_lowering=False)

    d_xmy = nc.dram_tensor("xmy", [1024, H], F32, kind="ExternalInput").ap()
    d_xkv = nc.dram_tensor("xkv", [L, H], F32, kind="ExternalInput").ap()
    d_w13 = nc.dram_tensor("w13t", [NE, 22, 128, 2048], F32, kind="ExternalInput").ap()
    d_sw13 = nc.dram_tensor("sw13t", [44, 128, 2048], F32, kind="ExternalInput").ap()
    d_w2 = nc.dram_tensor("w2t", [NE, 2, IC, 128, 1024], F32, kind="ExternalInput").ap()
    d_sw2 = nc.dram_tensor("sw2t", [4, IC, 128, 1024], F32, kind="ExternalInput").ap()
    d_gw = nc.dram_tensor("gwt", [16, 128, 16], F32, kind="ExternalInput").ap()
    d_bx = nc.dram_tensor("bxm", [8, 128, 256], F32, kind="ExternalInput").ap()
    d_by = nc.dram_tensor("bym", [8, 128, 256], F32, kind="ExternalInput").ap()
    d_id = nc.dram_tensor("identb", [128, 128], BF16, kind="ExternalInput").ap()
    d_tri = nc.dram_tensor("trib", [128, 128], BF16, kind="ExternalInput").ap()
    d_one = nc.dram_tensor("onesb", [128, 128], BF16, kind="ExternalInput").ap()
    d_lts = nc.dram_tensor("ltsm", [128, 128], BF16, kind="ExternalInput").ap()
    d_iota = nc.dram_tensor("iotam", [128, 512], FP16, kind="ExternalInput").ap()
    d_out = nc.dram_tensor("out", [1024, H], F32, kind="ExternalOutput").ap()

    from contextlib import ExitStack
    with ExitStack() as ctx:
        stream = ctx.enter_context(nc.sbuf_tensor("stream", [128, NT * H], F32))
        xt = ctx.enter_context(nc.sbuf_tensor("xt", [128, NT * H], BF16))
        arena = ctx.enter_context(nc.sbuf_tensor("arena", [128, 22528], BF16))
        wb0 = ctx.enter_context(nc.sbuf_tensor("wb0", [128, 4096], BF16))
        wb1 = ctx.enter_context(nc.sbuf_tensor("wb1", [128, 4096], BF16))
        ws0 = ctx.enter_context(nc.sbuf_tensor("ws0", [128, IC * 512], BF16))
        ws1 = ctx.enter_context(nc.sbuf_tensor("ws1", [128, IC * 512], BF16))
        x1r6 = ctx.enter_context(nc.sbuf_tensor("x1r6", [128, 6 * H], BF16))
        ltsb = ctx.enter_context(nc.sbuf_tensor("ltsb", [128, 128], BF16))
        iotab = ctx.enter_context(nc.sbuf_tensor("iotab", [128, 512], FP16))
        sstat = ctx.enter_context(nc.sbuf_tensor("sstat", [128, 24], F32))
        sklb = ctx.enter_context(nc.sbuf_tensor("sklb", [128, 24], BF16))
        sg = ctx.enter_context(nc.sbuf_tensor("sg", [128, 512], F32))
        comb = ctx.enter_context(nc.sbuf_tensor("comb", [128, NT * 16], F32))
        stv = ctx.enter_context(nc.sbuf_tensor("stat", [128, 64], F32))
        gsm = ctx.enter_context(nc.sbuf_tensor("gsm", [128, 16], F32))
        g16 = ctx.enter_context(nc.sbuf_tensor("g16", [128, 16], F32))
        m8 = ctx.enter_context(nc.sbuf_tensor("m8", [128, 8], F32))
        pb0 = ctx.enter_context(nc.psum_tensor("pb0", [128, 512], F32))
        pb1 = ctx.enter_context(nc.psum_tensor("pb1", [128, 512], F32))
        pb2 = ctx.enter_context(nc.psum_tensor("pb2", [128, 512], F32))
        pb3 = ctx.enter_context(nc.psum_tensor("pb3", [128, 512], F32))
        pb4 = ctx.enter_context(nc.psum_tensor("pb4", [128, 512], F32))
        pb5 = ctx.enter_context(nc.psum_tensor("pb5", [128, 512], F32))
        pbtA = ctx.enter_context(nc.psum_tensor("pbtA", [128, 1024], BF16))
        pbtB = ctx.enter_context(nc.psum_tensor("pbtB", [128, 1024], BF16))
        sem_sd = ctx.enter_context(nc.semaphore("sem_sd"))
        sem_gd = ctx.enter_context(nc.semaphore("sem_gd"))
        sem_t = ctx.enter_context(nc.semaphore("sem_t"))
        sem_v = ctx.enter_context(nc.semaphore("sem_v"))
        sem_s = ctx.enter_context(nc.semaphore("sem_s"))
        sem_kva = ctx.enter_context(nc.semaphore("sem_kva"))
        sem_kvb = ctx.enter_context(nc.semaphore("sem_kvb"))
        sem_w1a = ctx.enter_context(nc.semaphore("sem_w1a"))
        sem_w1b = ctx.enter_context(nc.semaphore("sem_w1b"))
        sem_w2a = ctx.enter_context(nc.semaphore("sem_w2a"))
        sem_w2b = ctx.enter_context(nc.semaphore("sem_w2b"))
        sems = {"sd": sem_sd, "gd": sem_gd, "t": sem_t, "v": sem_v, "s": sem_s,
                "kva": sem_kva, "kvb": sem_kvb, "w1a": sem_w1a, "w1b": sem_w1b,
                "w2a": sem_w2a, "w2b": sem_w2b}
        pbs = [pb0, pb1, pb2, pb3, pb4, pb5]
        pbth = [pbtA, pbtB]
        wbufs = [wb0, wb1]
        wsbufs = [ws0, ws1]

        def ar(off, n):
            return arena[:, off:off + n]
        VB = [ar(0, 2048), ar(2048, 2048)]
        KTB = [ar(4096, 2048), ar(6144, 2048)]
        PSB = ar(8192, 2048)
        PTB = ar(10240, 2048)
        QTA = ar(12288, 1024)
        BXS = ar(13312, 2048)
        BYS = ar(15360, 2048)
        IDT = ar(17408, 128)
        TRI = ar(17536, 128)
        ONE = ar(17664, 128)
        GWS = ar(17792, 256)
        TMPB = ar(18048, 2048)
        ABUF = [arena[:, 0:11264], arena[:, 11264:22528]]

        def x1r(tt):
            if tt == 6:
                return arena[:, 13312:15360]
            if tt == 7:
                return arena[:, 15360:17408]
            return x1r6[:, tt * H:(tt + 1) * H]


        P = Plan()
        cp = {}

        for tt in range(NT):
            def ld(E, tt=tt):
                return E.dma_start(out=stream[:, tt * H:(tt + 1) * H],
                                   in_=d_xmy[tt * 128:(tt + 1) * 128, :])
            P.emit("sync", ld, inc="sd", amt=16)
        for dst, srcd in ((IDT, d_id), (TRI, d_tri), (ONE, d_one), (ltsb[:], d_lts),
                          (iotab[:], d_iota)):
            def ld(E, dst=dst, srcd=srcd):
                return E.dma_start(out=dst, in_=srcd)
            P.emit("sync", ld, inc="sd", amt=16)
        def zsr(E):
            return E.memset(sklb[:, 16:24], 0.0)
        P.emit("vector", zsr, inc="v")
        cp["stream"] = P.cnt["sd"]
        cp["consts_sd"] = P.cnt["sd"]

        if LIMIT["heads"] < NH or LIMIT["experts"] < 18:
            for src in (d_xkv[0:128, 0:128], d_w13[0, 0, :, 0:128], d_sw13[0, :, 0:128],
                        d_w2[0, 0, 0, :, 0:128], d_sw2[0, 0, :, 0:128], d_gw[0],
                        d_bx[0, :, 0:128], d_by[0, :, 0:128]):
                def ld(E, src=src):
                    return E.dma_start(out=wb0[:, 0:src.shape[-1]], in_=src)
                P.emit("gpsimd", ld, waits={"gd": P.cnt["gd"]}, inc="gd", amt=16)
                cp["dbg_touch"] = P.cnt["gd"]

        for half in range(2):
            def ld(E, half=half):
                return E.dma_start(
                    out=xt[:, half * 4 * H:(half + 1) * 4 * H].rearrange("p (tt e) -> p tt e", tt=4),
                    in_=d_xmy[half * 512:(half + 1) * 512, :].rearrange("(tt p) e -> p tt e", p=128))
            P.emit("gpsimd", ld, inc="gd", amt=16)
        for dst, src in ((BXS, d_bx), (BYS, d_by), (GWS, d_gw)):
            def ld(E, dst=dst, src=src):
                return E.dma_start(out=dst.rearrange("p (a b) -> p a b", b=src.shape[2]),
                                   in_=src.rearrange("a p b -> p a b"))
            P.emit("gpsimd", ld, inc="gd", amt=16)
        cp["xtb"] = P.cnt["gd"]
        cp["consts_gd"] = P.cnt["gd"]

        w13_emitted = set()

        def emit_w13_load(gj):
            if gj >= len(W13SEQ) or gj in w13_emitted:
                return
            w13_emitted.add(gj)
            kind, e, k = W13SEQ[gj][0], W13SEQ[gj][1], W13SEQ[gj][2]
            buf = gj % 2
            waits = {}
            if "dbg_touch" in cp:
                waits["gd"] = cp["consts_gd"]
            if ("w13used", buf) in cp:
                waits["t"] = cp[("w13used", buf)]
            if kind == "r":
                def ld(E, e=e, k=k, buf=buf):
                    return E.dma_start(
                        out=wbufs[buf][:].rearrange("p (g c) -> p g c", g=2),
                        in_=d_w13[e, 2 * k:2 * k + 2].rearrange("g p c -> p g c"))
            else:
                def ld(E, k=k, buf=buf):
                    return E.dma_start(
                        out=wbufs[buf][:].rearrange("p (g c) -> p g c", g=2),
                        in_=d_sw13[2 * k:2 * k + 2].rearrange("g p c -> p g c"))
            sk = "w1a" if buf == 0 else "w1b"
            cp[("w13ld", gj)] = (sk, P.emit("gpsimd", ld, waits=waits, inc=sk, amt=16)[sk])

        w2_emitted = set()

        def emit_w2_load(wj):
            if wj >= len(W2SEQ) or wj in w2_emitted:
                return
            w2_emitted.add(wj)
            kind, e, hh, hb = W2SEQ[wj][0], W2SEQ[wj][1], W2SEQ[wj][2], W2SEQ[wj][3]
            buf = wj % 2
            waits = {}
            if ("w2used", buf) in cp:
                waits["t"] = cp[("w2used", buf)]
            if kind == "r":
                def ld(E, e=e, hh=hh, hb=hb, buf=buf):
                    return E.dma_start(
                        out=wsbufs[buf][:].rearrange("p (i c) -> p i c", i=IC),
                        in_=d_w2[e, hh][:, :, hb * 512:(hb + 1) * 512]
                        .rearrange("i p c -> p i c"))
            else:
                def ld(E, hh=hh, hb=hb, buf=buf):
                    return E.dma_start(
                        out=wsbufs[buf][:].rearrange("p (i c) -> p i c", i=IC),
                        in_=d_sw2[hh][:, :, hb * 512:(hb + 1) * 512]
                        .rearrange("i p c -> p i c"))
            sk = "w2a" if buf == 0 else "w2b"
            cp[("w2ld", wj)] = (sk, P.emit("gpsimd", ld, waits=waits, inc=sk, amt=16)[sk])

        def emit_vload(h):
            waits = {}
            if h >= 2:
                waits["t"] = cp.get(("pv_done", h - 2), 0)
            def ld(E, h=h):
                return E.dma_start(
                    out=VB[h % 2].rearrange("p (kt d) -> p kt d", kt=16),
                    in_=d_xkv[:, h * 128:(h + 1) * 128].rearrange("(kt p) d -> p kt d", p=128))
            sk = "kva" if h % 2 == 0 else "kvb"
            cp[("vload", h)] = (sk, P.emit("gpsimd", ld, waits=waits, inc=sk, amt=16)[sk])

        if nheads > 0:
            emit_vload(0)
        if nheads > 1:
            emit_vload(1)

        pbt_free = [0, 0]

        for h in range(nheads):
            buf = h % 2
            for g in range(4 if A >= 1 else 0):
                half = g % 2
                vsk, vsv = cp[("vload", h)]
                waits = {vsk: vsv, "v": pbt_free[half], "sd": cp["consts_sd"]}
                def tr(E, g=g, half=half, buf=buf):
                    ins = None
                    for j in range(4):
                        ins = E.transpose(pbth[half][:, j * 128:(j + 1) * 128],
                                          VB[buf][:, (g * 4 + j) * 128:(g * 4 + j + 1) * 128], IDT)
                    return ins
                tcnt = P.emit("tensor", tr, waits=waits, inc="t")["t"]
                def ev(E, g=g, half=half, buf=buf):
                    return E.tensor_copy(KTB[buf][:, g * 512:(g + 1) * 512],
                                         pbth[half][:, 0:512])
                pbt_free[half] = P.emit("vector", ev, waits={"t": tcnt}, inc="v")["v"]
                cp[("ktev", h, g)] = pbt_free[half]
            for g in ((1, 0) if A >= 2 else ()):
                half = g % 2
                waits = {"gd": cp["xtb"], "v": pbt_free[half]}
                if h >= 1:
                    waits["t"] = cp.get(("qk_done", h - 1, 4 * g), 0)
                def tr(E, g=g, half=half, h=h):
                    ins = None
                    for j in range(4):
                        s = g * 4 + j
                        ins = E.transpose(pbth[half][:, j * 128:(j + 1) * 128],
                                          xt[:, s * H + h * 128: s * H + (h + 1) * 128], IDT)
                    return ins
                tcnt = P.emit("tensor", tr, waits=waits, inc="t")["t"]
                def ev(E, g=g, half=half):
                    return E.tensor_copy(QTA[:, g * 512:(g + 1) * 512],
                                         pbth[half][:, 0:512])
                pbt_free[half] = P.emit("vector", ev, waits={"t": tcnt}, inc="v")["v"]
                cp[("qtev", h, g)] = pbt_free[half]
            cp[("qtready", h)] = P.cnt["v"]

            def emit_qk(s, i):
                N = N_LIST[s]
                ncols = N * 128
                kbn = (ncols + 511) // 512
                par = i % 2
                ktw = cp[("ktev", h, (N + 3) // 4 - 1)]
                qtw = cp[("qtev", h, s // 4)]
                waits = {"v": max(ktw, qtw), "gd": cp["consts_gd"]}
                if (h, i) != (0, 0):
                    ph, ps = (h, SEQ[i - 1]) if i > 0 else (h - 1, SEQ[7])
                    waits["s"] = cp.get(("exp_done", ph, ps), 0)
                def qk(E, s=s, ncols=ncols, kbn=kbn, h=h):
                    ins = None
                    for kb in range(kbn):
                        n = min(512, ncols - kb * 512)
                        last = kb == kbn - 1
                        ins = E.matmul(pbs[kb][:, 0:n], QTA[:, s * 128:(s + 1) * 128],
                                       KTB[h % 2][:, kb * 512: kb * 512 + n],
                                       start=True, stop=not last)
                        if last:
                            noff = n - 256
                            ins = E.matmul(pbs[kb][:, noff:noff + 256], TRI,
                                           BXS[:, s * 256:(s + 1) * 256],
                                           start=False, stop=False)
                            ins = E.matmul(pbs[kb][:, noff:noff + 256], ONE,
                                           BYS[:, s * 256:(s + 1) * 256],
                                           start=False, stop=True)
                    return ins
                cp[("qk_done", h, s)] = P.emit("tensor", qk, waits=waits, inc="t")["t"]
                waits = {"t": cp[("qk_done", h, s)]}
                if not (h == 0 and i < 2):
                    ph, ps = (h, SEQ[i - 2]) if i >= 2 else (h - 1, SEQ[6 + i])
                    waits["s"] = cp.get(("exp_done", ph, ps), 0)
                def mx(E, s=s, ncols=ncols, kbn=kbn, par=par):
                    for kb in range(kbn):
                        n = min(512, ncols - kb * 512)
                        E.reduce_max(stv[:, 59 + kb:60 + kb], pbs[kb][:, 0:n], axis=AX.X)
                    E.drain()
                    E.reduce_max(stv[:, 48 + par:49 + par], stv[:, 59:59 + kbn],
                                 axis=AX.X, negate=True)
                    E.drain()
                    return E.tensor_scalar(stv[:, 48 + par:49 + par],
                                           stv[:, 48 + par:49 + par], SCALE, None,
                                           op0=ALU.mult)
                cp[("negm", h, s)] = P.emit("vector", mx, waits=waits, inc="v")["v"]

            def emit_exp(s, i):
                N = N_LIST[s]
                ncols = N * 128
                kbn = (ncols + 511) // 512
                par = i % 2
                waits = {"v": cp[("negm", h, s)]}
                if (h, i) != (0, 0):
                    ph, ps = (h, SEQ[i - 1]) if i > 0 else (h - 1, SEQ[7])
                    waits["t"] = cp.get(("ptr_done", ph, ps), 0)
                def ex(E, s=s, ncols=ncols, kbn=kbn, par=par):
                    ins = None
                    for kb in range(kbn):
                        n = min(512, ncols - kb * 512)
                        ins = E.activation(PSB[:, kb * 512: kb * 512 + n], pbs[kb][:, 0:n],
                                           AF.Exp, bias=stv[:, 48 + par:49 + par],
                                           scale=SCALE,
                                           accum_out=stv[:, 40 + par * 4 + kb:41 + par * 4 + kb])
                    return ins
                cp[("exp_done", h, s)] = P.emit("scalar", ex, waits=waits, inc="s")["s"]
                def lv(E, s=s, kbn=kbn, par=par):
                    E.reduce_sum(stv[:, 50 + par:51 + par],
                                 stv[:, 40 + par * 4: 40 + par * 4 + kbn], axis=AX.X)
                    E.drain()
                    E.reciprocal(stv[:, 52 + par:53 + par], stv[:, 50 + par:51 + par])
                    return E.drain()
                P.emit("vector", lv, waits={"s": cp[("exp_done", h, s)]}, inc=None)

            def emit_pt(s):
                N = N_LIST[s]
                kt = 0
                while kt < N:
                    m = min(4, N - kt)
                    half = (kt // 4) % 2
                    waits = {"s": cp[("exp_done", h, s)], "v": pbt_free[half]}
                    def tr(E, kt=kt, m=m, half=half):
                        ins = None
                        for j in range(m):
                            ins = E.transpose(pbth[half][:, j * 128:(j + 1) * 128],
                                              PSB[:, (kt + j) * 128:(kt + j + 1) * 128], IDT)
                        return ins
                    tcnt = P.emit("tensor", tr, waits=waits, inc="t")["t"]
                    if kt + m >= N:
                        cp[("ptr_done", h, s)] = tcnt
                    def ev(E, kt=kt, m=m, half=half):
                        return E.tensor_copy(PTB[:, kt * 128:(kt + m) * 128],
                                             pbth[half][:, 0:m * 128])
                    pbt_free[half] = P.emit("vector", ev, waits={"t": tcnt}, inc="v")["v"]
                    kt += m
                cp[("ptev", h, s)] = P.cnt["v"]

            def emit_pv(s, i):
                N = N_LIST[s]
                par = i % 2
                waits = {"v": max(cp[("ptev", h, s)], cp.get(("pv_evac_prev",), 0))}
                def pv(E, s=s, N=N, h=h):
                    ins = None
                    for kt in range(N):
                        ins = E.matmul(pb4[:, 0:128], PTB[:, kt * 128:(kt + 1) * 128],
                                       VB[h % 2][:, kt * 128:(kt + 1) * 128],
                                       start=(kt == 0), stop=(kt == N - 1))
                    return ins
                tcnt = P.emit("tensor", pv, waits=waits, inc="t")["t"]
                cp[("pv_done_slot", h, s)] = tcnt
                if A < 7:
                    return
                def evp(E, s=s, h=h, par=par):
                    return E.scalar_tensor_tensor(
                        out=stream[:, s * H + h * 128: s * H + (h + 1) * 128],
                        in0=pb4[:, 0:128], scalar=stv[:, 52 + par:53 + par],
                        in1=stream[:, s * H + h * 128: s * H + (h + 1) * 128],
                        op0=ALU.mult, op1=ALU.add)
                cp[("pv_evac_prev",)] = P.emit(
                    "vector", evp, waits={"t": tcnt, "sd": cp["stream"]}, inc="v")["v"]

            if A >= 3:
                emit_qk(SEQ[0], 0)
            if A >= 4:
                emit_exp(SEQ[0], 0)
            for i in range(1, 8):
                if A >= 3:
                    emit_qk(SEQ[i], i)
                if A >= 5:
                    emit_pt(SEQ[i - 1])
                if A >= 4:
                    emit_exp(SEQ[i], i)
                if A >= 6:
                    emit_pv(SEQ[i - 1], i - 1)
            if A >= 5:
                emit_pt(SEQ[7])
            if A >= 6:
                emit_pv(SEQ[7], 7)
                cp[("pv_done", h)] = cp[("pv_done_slot", h, SEQ[7])]
            if h + 2 < nheads:
                emit_vload(h + 2)
            g0 = 0 if do_shared else 44
            q0 = 0 if do_shared else 16
            if h == 1:
                emit_w13_load(g0)
            elif h == 2:
                emit_w13_load(g0 + 1)
            elif h == 3:
                emit_w2_load(q0)
            elif h == 6:
                emit_w2_load(q0 + 1)

        if LIMIT["experts"] > 0:
            g0 = 0 if do_shared else 44
            q0 = 0 if do_shared else 16
            emit_w13_load(g0)
            emit_w13_load(g0 + 1)
            emit_w2_load(q0)
            emit_w2_load(q0 + 1)

        def emit_ln():
            for tt in range(NT):
                def rs(E, tt=tt):
                    if tt == 0:
                        E.drain()
                    return E.reduce_sum(stv[:, tt:tt + 1], stream[:, tt * H:(tt + 1) * H], axis=AX.X)
                P.emit("vector", rs, waits={"sd": cp["stream"]}, inc="v")
            vafter = P.cnt["v"]
            for tt in range(NT):
                def sq(E, tt=tt):
                    return E.activation(ABUF[0][:, 0:2048], stream[:, tt * H:(tt + 1) * H],
                                        AF.Square, accum_out=stv[:, 8 + tt:9 + tt])
                P.emit("scalar", sq,
                       waits={"v": vafter, "s": cp.get(("sq_prev",), 0)}, inc="s")
                cp[("sq_prev",)] = P.cnt["s"]
            sq_done = P.cnt["s"]
            def fin1(E):
                E.drain()
                E.tensor_scalar(stv[:, 16:24], stv[:, 0:8], 1.0 / H, None, op0=ALU.mult)
                E.tensor_scalar(stv[:, 24:32], stv[:, 8:16], 1.0 / H, None, op0=ALU.mult)
                E.drain()
                E.tensor_mul(stv[:, 32:40], stv[:, 16:24], stv[:, 16:24])
                E.drain()
                E.tensor_sub(stv[:, 24:32], stv[:, 24:32], stv[:, 32:40])
                E.drain()
                return E.tensor_scalar(stv[:, 24:32], stv[:, 24:32], 1e-5, None, op0=ALU.add)
            c1 = P.emit("vector", fin1, waits={"s": sq_done}, inc="v")["v"]
            def fsq(E):
                return E.activation(stv[:, 32:40], stv[:, 24:32], AF.Sqrt)
            c2 = P.emit("scalar", fsq, waits={"v": c1}, inc="s")["s"]
            def fin2(E):
                E.drain()
                E.reciprocal(stv[:, 40:48], stv[:, 32:40])
                E.drain()
                E.tensor_mul(stv[:, 32:40], stv[:, 40:48], stv[:, 40:48])
                E.drain()
                E.tensor_mul(stv[:, 32:40], stv[:, 32:40], stv[:, 24:32])
                E.drain()
                E.tensor_scalar(stv[:, 32:40], stv[:, 32:40], -0.5, 1.5, op0=ALU.mult, op1=ALU.add)
                E.drain()
                E.tensor_mul(stv[:, 24:32], stv[:, 40:48], stv[:, 32:40])
                return E.drain()
            P.emit("vector", fin2, waits={"s": c2}, inc=None)
            for tt in range(NT):
                def ap(E, tt=tt):
                    return E.tensor_scalar(stream[:, tt * H:(tt + 1) * H],
                                           stream[:, tt * H:(tt + 1) * H],
                                           stv[:, 16 + tt:17 + tt], stv[:, 24 + tt:25 + tt],
                                           op0=ALU.subtract, op1=ALU.mult)
                P.emit("vector", ap, inc="v")
            return P.cnt["v"]

        if LIMIT["ln1"]:
            emit_ln()

        p3range = range(NT) if (LIMIT["p3"] or LIMIT["experts"] > 0) else range(0)
        for tt in p3range:
            def cst(E, tt=tt):
                if tt == 0:
                    E.drain()
                return E.tensor_copy(x1r(tt), stream[:, tt * H:(tt + 1) * H])
            ccast = P.emit("vector", cst, waits={"t": cp.get(("x1tr", tt - 1), 0)},
                           inc="v")["v"]
            for g in range(4):
                half = g % 2
                waits = {"v": max(ccast, pbt_free[half]), "gd": cp["xtb"]}
                if tt == 0 and g < 2 and nheads > 0:
                    waits["t"] = cp.get(("pv_done", nheads - 1), 0)
                def tr(E, g=g, half=half, tt=tt):
                    ins = None
                    for j in range(4):
                        hc = g * 4 + j
                        ins = E.transpose(pbth[half][:, j * 128:(j + 1) * 128],
                                          x1r(tt)[:, hc * 128:(hc + 1) * 128], IDT)
                    return ins
                tcnt = P.emit("tensor", tr, waits=waits, inc="t")["t"]
                def ev(E, g=g, half=half, tt=tt):
                    return E.tensor_copy(
                        xt[:].rearrange("p (hc t) -> p hc t", hc=16)[:, g * 4:(g + 1) * 4,
                                                                     tt * 128:(tt + 1) * 128],
                        pbth[half][:, 0:512].rearrange("p (a b) -> p a b", a=4))
                pbt_free[half] = P.emit("vector", ev, waits={"t": tcnt}, inc="v")["v"]
            cp[("x1t", tt)] = P.cnt["v"]
            cp[("x1tr", tt)] = P.cnt["t"]

        if len(p3range) > 0:
            waits = {"v": max(cp[("x1t", NT - 1)], cp.get(("pv_evac_prev",), 0)),
                     "gd": cp["consts_gd"]}
            def gmm(E):
                ins = None
                for hc in range(16):
                    for tt in range(NT):
                        ins = E.matmul(pb4[:, tt * 16:(tt + 1) * 16],
                                       xt[:, hc * 1024 + tt * 128: hc * 1024 + (tt + 1) * 128],
                                       GWS[:, hc * 16:(hc + 1) * 16],
                                       start=(hc == 0 and tt == 0),
                                       stop=(hc == 15 and tt == NT - 1),
                                       skip_group_check=True)
                return ins
            gq = P.emit("tensor", gmm, waits=waits, inc="t")["t"]
        for tt in p3range:
            waits = {"t": gq}
            if tt > 0:
                waits["s"] = cp[("gexp", tt - 1)]
            def gmx(E, tt=tt):
                return E.reduce_max(stv[:, 54:55], pb4[:, tt * 16:(tt + 1) * 16],
                                    axis=AX.X, negate=True)
            gv = P.emit("vector", gmx, waits=waits, inc="v")["v"]
            def gex(E, tt=tt):
                return E.activation(gsm[:], pb4[:, tt * 16:(tt + 1) * 16], AF.Exp,
                                    bias=stv[:, 54:55], accum_out=stv[:, 55:56])
            cp[("gexp", tt)] = P.emit("scalar", gex, waits={"v": gv}, inc="s")["s"]
            def gtk(E, tt=tt):
                E.drain()
                E.reciprocal(stv[:, 56:57], stv[:, 55:56])
                E.drain()
                E.tensor_scalar_mul(gsm[:], gsm[:], stv[:, 56:57])
                E.drain()
                E.max(out=m8[:], in_=gsm[:])
                E.drain()
                E.tensor_scalar(g16[:], gsm[:], m8[:, 5:6], None, op0=ALU.is_ge)
                E.drain()
                E.tensor_mul(g16[:], gsm[:], g16[:])
                E.drain()
                E.reduce_sum(stv[:, 57:58], g16[:], axis=AX.X)
                E.drain()
                E.tensor_scalar(stv[:, 57:58], stv[:, 57:58], 1e-20, None, op0=ALU.add)
                E.drain()
                E.reciprocal(stv[:, 58:59], stv[:, 57:58])
                E.drain()
                return E.tensor_scalar(comb[:, tt * 16:(tt + 1) * 16], g16[:],
                                       stv[:, 58:59], None, op0=ALU.mult)
            P.emit("vector", gtk, waits={"s": cp[("gexp", tt)]}, inc="v")
        def gdr(E):
            return E.drain()
        P.emit("vector", gdr, inc=None)
        comb_done = P.cnt["v"]

        cp[("m1_silu_prev",)] = cp.get(("exp_done", nheads - 1, 7), 0)

        def selb(tt):
            return xt[:, tt * 2048 + 512: tt * 2048 + 1024]
        SELT = arena[:, 18048:22144]

        def abase(half):
            return half * 5632

        def xgv(hc):
            return xt[:, hc * 1024: hc * 1024 + 512]

        def emit_m1s(gj0, nk, ab_off, rhs_fn, xg_key):
            for k in range(nk):
                gj = gj0 + k
                emit_w13_load(gj)
                wbuf = wbufs[gj % 2]
                wsk, wsv = cp[("w13ld", gj)]
                gb, ub = (pb2, pb3) if k % 2 == 0 else (pb0, pb1)
                waits = {wsk: wsv, "s": cp[("m1_silu_prev",)],
                         "v": cp.get(xg_key, cp[("x1t", NT - 1)])}
                def gmmf(E, wbuf=wbuf, gb=gb, rhs_fn=rhs_fn):
                    ins = None
                    for hc in range(16):
                        ins = E.matmul(gb[:, 0:512], wbuf[:, hc * 128:(hc + 1) * 128],
                                       rhs_fn(hc), start=(hc == 0), stop=(hc == 15),
                                       skip_group_check=True)
                    return ins
                gcnt = P.emit("tensor", gmmf, waits=waits, inc="t")["t"]
                uw = {}
                if cp.get(("m1_aev2",), 0):
                    uw["v"] = cp[("m1_aev2",)]
                def ummf(E, wbuf=wbuf, ub=ub, rhs_fn=rhs_fn):
                    ins = None
                    for hc in range(16):
                        ins = E.matmul(ub[:, 0:512],
                                       wbuf[:, 2048 + hc * 128: 2048 + (hc + 1) * 128],
                                       rhs_fn(hc), start=(hc == 0), stop=(hc == 15),
                                       skip_group_check=True)
                    return ins
                ucnt = P.emit("tensor", ummf, waits=uw, inc="t")["t"]
                cp[("w13used", gj % 2)] = ucnt
                emit_w13_load(gj + 2)
                def slf(E, gb=gb):
                    return E.activation(sg[:], gb[:, 0:512], AF.Silu)
                scnt = P.emit("scalar", slf,
                              waits={"t": gcnt, "v": cp.get(("m1_aev_prev",), 0)},
                              inc="s")["s"]
                cp[("m1_silu_prev",)] = scnt
                icol = ab_off + k * 512
                def aevf(E, ub=ub, icol=icol):
                    return E.tensor_mul(arena[:, icol: icol + 512], sg[:], ub[:, 0:512])
                acnt = P.emit("vector", aevf, waits={"t": ucnt, "s": scnt}, inc="v")["v"]
                cp[("m1_aev2",)] = cp.get(("m1_aev_prev",), 0)
                cp[("m1_aev_prev",)] = acnt
            return P.cnt["v"]

        def emit_routing(en):
            def mskf(E, en=en):
                E.drain()
                E.tensor_scalar(sklb[:, 0:8].rearrange("p (a b) -> p a b", b=1),
                                comb[:].rearrange("p (t g) -> p t g", g=16)[:, :, en:en + 1],
                                1e-30, None, op0=ALU.is_ge)
                E.drain()
                E.tensor_tensor_scan(sklb[:, 8:16], sklb[:, 0:8], sklb[:, 16:24],
                                     0.0, op0=ALU.add, op1=ALU.add)
                E.drain()
                E.tensor_sub(sklb[:, 8:16], sklb[:, 8:16], sklb[:, 0:8])
                return E.drain()
            mc = P.emit("vector", mskf,
                        waits={"v": comb_done, "t": cp.get(("posmm", en - 1), 0)},
                        inc="v")["v"]
            def posmm(E):
                E.matmul(pb0[:, 0:8], ltsb[:], sklb[:, 0:8], start=True, stop=False,
                         skip_group_check=True)
                return E.matmul(pb0[:, 0:8], ONE, sklb[:, 8:16], start=False, stop=True,
                                skip_group_check=True)
            pt_ = P.emit("tensor", posmm,
                         waits={"v": mc, "s": cp[("m1_silu_prev",)]}, inc="t")["t"]
            cp[("posmm", en)] = pt_
            def posev(E):
                E.drain()
                E.tensor_scalar(sstat[:, 0:8], pb0[:, 0:8], 1e6, None, op0=ALU.add)
                E.drain()
                E.scalar_tensor_tensor(out=sstat[:, 8:16], in0=sklb[:, 0:8],
                                       scalar=-1e6, in1=sstat[:, 0:8],
                                       op0=ALU.mult, op1=ALU.add)
                return E.drain()
            pv_ = P.emit("vector", posev, waits={"t": pt_}, inc="v")["v"]
            cp[("pos", en)] = pv_
            for tt in range(NT):
                def self_(E, tt=tt):
                    return E.tensor_scalar(selb(tt), iotab[:],
                                           sstat[:, 8 + tt:9 + tt], None, op0=ALU.is_equal)
                P.emit("vector", self_,
                       waits={"v": pv_, "t": cp.get(("gath_done", en - 1), 0)},
                       inc="v")
            cp[("sel", en)] = P.cnt["v"]

        def emit_selt(en):
            for tt in range(NT):
                half = tt % 2
                waits = {"v": max(cp[("sel", en)], pbt_free[half])}
                def trf(E, tt=tt, half=half):
                    ins = None
                    for ct in range(4):
                        ins = E.transpose(pbth[half][:, ct * 128:(ct + 1) * 128],
                                          selb(tt)[:, ct * 128:(ct + 1) * 128],
                                          IDT)
                    return ins
                tcnt = P.emit("tensor", trf, waits=waits, inc="t")["t"]
                def evt(E, tt=tt, half=half):
                    return E.tensor_copy(
                        SELT[:].rearrange("p (ct t c) -> p ct t c", ct=4, t=8)[:, :, tt, :],
                        pbth[half][:, 0:512].rearrange("p (a b) -> p a b", a=4))
                pbt_free[half] = P.emit("vector", evt, waits={"t": tcnt}, inc="v")["v"]
            cp[("selt", en)] = P.cnt["v"]

        def emit_gather(en):
            for hc in range(16):
                gb = pb0 if hc % 2 == 0 else pb1
                wv = max(cp[("sel", en)], cp.get(("xgev", hc % 2), 0))
                if hc < 2:
                    wv = max(wv, cp.get(("scatev", 0), 0), cp.get(("scatev", 1), 0),
                             cp.get(("pos", en), 0))
                def gmf(E, gb=gb, hc=hc):
                    ins = None
                    for tt in range(NT):
                        ins = E.matmul(gb[:, 0:512], x1r(tt)[:, hc * 128:(hc + 1) * 128],
                                       selb(tt),
                                       start=(tt == 0), stop=(tt == NT - 1))
                    return ins
                tcnt = P.emit("tensor", gmf, waits={"v": wv}, inc="t")["t"]
                def xev(E, gb=gb, hc=hc):
                    return E.tensor_copy(xgv(hc), gb[:, 0:512])
                cp[("xgev", hc % 2)] = P.emit("vector", xev, waits={"t": tcnt},
                                              inc="v")["v"]
            cp[("xgd", en)] = P.cnt["v"]
            cp[("gath_done", en)] = P.cnt["t"]

        def emit_m2_shared(th, adep):
            gidx = cp.get(("m2gidx",), 0)
            for q in range(4):
                for hb in range(2):
                    wj = th * 8 + q * 2 + hb
                    emit_w2_load(wj)
                    wsb = wsbufs[wj % 2]
                    w2k, w2v = cp[("w2ld", wj)]
                    for ct in range(4):
                        bk = pb4 if gidx % 2 == 0 else pb5
                        pkey = ("m2ev", gidx % 2)
                        waits = {w2k: w2v, "v": adep, "s": cp[("m1_silu_prev",)]}
                        if pkey in cp:
                            waits["v"] = max(waits["v"], cp[pkey])
                        ihalf = q // 2
                        def mmf(E, bk=bk, wsb=wsb, ct=ct, ihalf=ihalf):
                            ins = None
                            for ici in range(IC):
                                icol = (ihalf * 11 + ici) * 512
                                ins = E.matmul(bk[:, 0:512],
                                               arena[:, icol + ct * 128: icol + (ct + 1) * 128],
                                               wsb[:, ici * 512:(ici + 1) * 512],
                                               start=(ici == 0), stop=(ici == IC - 1),
                                               skip_group_check=True)
                            return ins
                        tcnt = P.emit("tensor", mmf, waits=waits, inc="t")["t"]
                        if ct == 3:
                            cp[("w2used", wj % 2)] = tcnt
                        tt = th * 4 + ct
                        def evf(E, bk=bk, tt=tt, q=q, hb=hb):
                            dst = stream[:, tt * H + (q % 2) * 1024 + hb * 512:
                                         tt * H + (q % 2) * 1024 + (hb + 1) * 512]
                            return E.scalar_tensor_tensor(out=dst, in0=bk[:, 0:512],
                                                          scalar=1.0, in1=dst,
                                                          op0=ALU.mult, op1=ALU.add)
                        c = P.emit("vector", evf,
                                   waits={"t": tcnt,
                                          "v": max(comb_done,
                                                   cp.get(("evf_prev", tt), 0))},
                                   inc="v")["v"]
                        cp[pkey] = c
                        cp[("evf_prev", tt)] = c
                        gidx += 1
                    emit_w2_load(wj + 2)
            cp[("m2gidx",)] = gidx

        def emit_m2_routed(e, wj0, adep):
            ybase = abase(1 - e % 2)
            gidx = cp.get(("m2gidx",), 0)
            for hh in range(2):
                for hb in range(2):
                    wj = wj0 + hh * 2 + hb
                    emit_w2_load(wj)
                    wsb = wsbufs[wj % 2]
                    w2k, w2v = cp[("w2ld", wj)]
                    for ct in range(4):
                        bk = pb4 if gidx % 2 == 0 else pb5
                        pkey = ("m2ev", gidx % 2)
                        waits = {w2k: w2v, "v": adep, "s": cp[("m1_silu_prev",)]}
                        if pkey in cp:
                            waits["v"] = max(waits["v"], cp[pkey])
                        abse = abase(e % 2)
                        def mmf(E, bk=bk, wsb=wsb, ct=ct, abse=abse):
                            ins = None
                            for ici in range(IC):
                                ins = E.matmul(
                                    bk[:, 0:512],
                                    arena[:, abse + ici * 512 + ct * 128:
                                           abse + ici * 512 + (ct + 1) * 128],
                                    wsb[:, ici * 512:(ici + 1) * 512],
                                    start=(ici == 0), stop=(ici == IC - 1),
                                    skip_group_check=True)
                            return ins
                        tcnt = P.emit("tensor", mmf, waits=waits, inc="t")["t"]
                        if ct == 3:
                            cp[("w2used", wj % 2)] = tcnt
                        yc = ybase + ct * 1024 + hb * 512
                        def evf(E, bk=bk, yc=yc):
                            return E.tensor_copy(arena[:, yc: yc + 512], bk[:, 0:512])
                        c = P.emit("vector", evf, waits={"t": tcnt}, inc="v")["v"]
                        cp[pkey] = c
                        gidx += 1
                    emit_w2_load(wj + 2)
                if hh == 0 and e + 1 < nroute:
                    emit_routing(e + 1)
                ev_y = P.cnt["v"]
                for tt in range(NT):
                    sbk = (pb0, pb1) if tt % 2 == 0 else (pb2, pb3)
                    skey = ("scatev", tt % 2)
                    wv = max(ev_y, cp[("selt", e)], cp.get(skey, 0))
                    if tt < 2:
                        wv = max(wv, cp.get(("pos", e + 1), 0))
                    def smm(E, sbk=sbk, tt=tt, ybase=ybase):
                        ins = None
                        for ct in range(4):
                            lhs = SELT[:, (ct * 8 + tt) * 128:(ct * 8 + tt + 1) * 128]
                            for hb in range(2):
                                ins = E.matmul(
                                    sbk[hb][:, 0:512], lhs,
                                    arena[:, ybase + ct * 1024 + hb * 512:
                                           ybase + ct * 1024 + (hb + 1) * 512],
                                    start=(ct == 0), stop=(ct == 3),
                                    skip_group_check=True)
                        return ins
                    tcnt = P.emit("tensor", smm, waits={"v": wv}, inc="t")["t"]
                    def sev(E, sbk=sbk, tt=tt, hh=hh, e=e):
                        ins = None
                        for hb in range(2):
                            dst = stream[:, tt * H + hh * 1024 + hb * 512:
                                         tt * H + hh * 1024 + (hb + 1) * 512]
                            ins = E.scalar_tensor_tensor(
                                out=dst, in0=sbk[hb][:, 0:512],
                                scalar=comb[:, tt * 16 + e: tt * 16 + e + 1],
                                in1=dst, op0=ALU.mult, op1=ALU.add)
                        return ins
                    c = P.emit("vector", sev,
                               waits={"t": tcnt,
                                      "v": max(comb_done,
                                               cp.get(("evf_prev", tt), 0))},
                               inc="v")["v"]
                    cp[skey] = c
                    cp[("evf_prev", tt)] = c
            cp[("m2gidx",)] = gidx

        if do_shared:
            for th in range(2):
                adep = emit_m1s(th * 22, 22, 0,
                                (lambda hc, th=th:
                                 xt[:, hc * 1024 + th * 512: hc * 1024 + th * 512 + 512]),
                                None)
                if th == 1 and nroute > 0:
                    emit_routing(0)
                emit_m2_shared(th, adep)
        if nroute > 0:
            if ("sel", 0) not in cp:
                emit_routing(0)
            for e in range(nroute):
                emit_selt(e)
                emit_gather(e)
                adep = emit_m1s(44 + e * 11, 11, abase(e % 2), xgv, ("xgd", e))
                emit_m2_routed(e, 16 + e * 4, adep)

        if LIMIT["ln2"]:
            emit_ln()
        st_wait = P.cnt["v"]
        for tt in range(NT):
            def st(E, tt=tt):
                return E.dma_start(out=d_out[tt * 128:(tt + 1) * 128, :],
                                   in_=stream[:, tt * H:(tt + 1) * H])
            P.emit("sync", st, waits={"v": st_wait}, inc="sd", amt=16)
        final_sd = P.cnt["sd"]
        def fw(E):
            return E.wait_ge(sems["sd"], final_sd)
        P.ops["sync"].append(({}, fw, None, 0))

        with nc.Block() as block:
            def replay(E, name):
                lastw = {}
                for waits, fn, inc, amt in P.ops[name]:
                    for sname, val in waits.items():
                        if val > 0 and lastw.get(sname, -1) < val:
                            E.wait_ge(sems[sname], val)
                            lastw[sname] = val
                    ins = fn(E)
                    if inc is not None:
                        ins.then_inc(sems[inc], amt)

            @block.sync
            def _(E):
                replay(E, "sync")

            @block.gpsimd
            def _(E):
                replay(E, "gpsimd")

            @block.tensor
            def _(E):
                replay(E, "tensor")

            @block.vector
            def _(E):
                replay(E, "vector")

            @block.scalar
            def _(E):
                replay(E, "scalar")

    return nc


def _host_prep(L0_ffn_gate_w, L0_ffn_w13, L0_ffn_w2, L0_ffn_sw13, L0_ffn_sw2):
    gw = np.asarray(L0_ffn_gate_w, dtype=np.float32)
    w13 = np.asarray(L0_ffn_w13, dtype=np.float32)
    w2 = np.asarray(L0_ffn_w2, dtype=np.float32)
    sw13 = np.asarray(L0_ffn_sw13, dtype=np.float32)
    sw2 = np.asarray(L0_ffn_sw2, dtype=np.float32)

    w13t = w13.reshape(16, 22, 128, 16, 128).transpose(0, 1, 4, 3, 2).reshape(16, 22, 128, 2048)
    w13t = np.ascontiguousarray(w13t[:, JORD])
    sw13t = sw13.reshape(44, 128, 16, 128).transpose(0, 3, 2, 1).reshape(44, 128, 2048)
    sw13t = np.ascontiguousarray(sw13t[SJORD])
    w2t = np.ascontiguousarray(
        w2.transpose(0, 2, 1).reshape(16, 11, 128, 2, 1024).transpose(0, 3, 1, 2, 4))
    sw2t = np.ascontiguousarray(
        sw2.T.reshape(2, 11, 128, 2, 1024).transpose(0, 3, 1, 2, 4).reshape(4, 11, 128, 1024))
    gwt = np.ascontiguousarray(gw.T.reshape(16, 128, 16))

    identb = np.eye(128, dtype=ml_dtypes.bfloat16)
    dd = np.arange(128)[:, None]
    qq = np.arange(128)[None, :]
    trib = (dd > qq).astype(ml_dtypes.bfloat16)
    onesb = np.ones((128, 128), dtype=ml_dtypes.bfloat16)

    kl = np.arange(128)[None, :]
    lower_eq = (dd <= kl).astype(np.float32)
    bx = np.zeros((2, 8, 128, 256), dtype=np.float32)
    by = np.zeros((2, 8, 128, 256), dtype=np.float32)
    for s in range(8):
        bx[0, s, :, 128:256] = -CBIG * lower_eq
        bx[1, s, :, 0:128] = -CBIG * lower_eq
        by[1, s, :, 128:256] = -CBIG / 128.0

    lts = (dd < kl).astype(ml_dtypes.bfloat16)
    iota = np.broadcast_to(np.arange(512, dtype=np.float16)[None, :],
                           (128, 512)).copy()
    return (w13t, sw13t, w2t, sw2t, gwt, identb, trib, onesb, bx, by,
            lts, iota)


def _row_tiles(role):
    return [(N_LIST[s] - 1 - role) for s in range(8)]


def kernel(L0_attn_x, L0_ffn_gate_w, L0_ffn_w13, L0_ffn_w2, L0_ffn_sw13, L0_ffn_sw2):
    x = np.asarray(L0_attn_x, dtype=np.float32)
    (w13t, sw13t, w2t, sw2t, gwt, identb, trib, onesb, bx, by,
     lts, iota) = _host_prep(
        L0_ffn_gate_w, L0_ffn_w13, L0_ffn_w2, L0_ffn_sw13, L0_ffn_sw2)

    nc = build_program()

    in_maps = []
    for c in range(8):
        b, role = c // 2, c % 2
        tiles = _row_tiles(role)
        rows = np.concatenate([np.arange(t * 128, (t + 1) * 128) for t in tiles])
        in_maps.append({
            "xmy": np.ascontiguousarray(x[b][rows]),
            "xkv": np.ascontiguousarray(x[b]),
            "w13t": w13t,
            "sw13t": sw13t,
            "w2t": w2t,
            "sw2t": sw2t,
            "gwt": gwt,
            "bxm": np.ascontiguousarray(bx[role]),
            "bym": np.ascontiguousarray(by[role]),
            "identb": identb,
            "trib": trib,
            "onesb": onesb,
            "ltsm": lts,
            "iotam": iota,
        })

    res = run_bass_kernel_spmd(nc, in_maps, core_ids=list(range(8)))

    out = np.empty((B, L, H), dtype=np.float32)
    for c in range(8):
        b, role = c // 2, c % 2
        o = res.results[c]["out"]
        for s, t in enumerate(_row_tiles(role)):
            out[b, t * 128:(t + 1) * 128, :] = o[s * 128:(s + 1) * 128, :]
    return out


# version 06
# speedup vs baseline: 1.2255x; 1.0066x over previous
import numpy as np
import ml_dtypes
import concourse.bass as bass
import concourse.mybir as mybir
from concourse.bass_utils import run_bass_kernel_spmd

F32 = mybir.dt.float32
BF16 = mybir.dt.bfloat16
FP16 = mybir.dt.float16
AF = mybir.ActivationFunctionType
ALU = mybir.AluOpType
AX = mybir.AxisListType

B = 4
L = 2048
H = 2048
NH = 16
NT = 8
NE = 16
IC = 11
SCALE = 1.0 / float(np.sqrt(128.0))
CBIG = 1.0e30
N_LIST = [16, 14, 12, 10, 8, 6, 4, 2]
SEQ = [7, 6, 5, 4, 3, 2, 1, 0]

JORD = [x for i in range(11) for x in (i, 11 + i)]
SJORD = [x for i in range(22) for x in (i, 22 + i)]

W13SEQ = [("s", None, k) for k in range(22)] + \
         [("r", e, k) for e in range(NE) for k in range(11)]
W2SEQ = [("s", None, q, hb) for q in range(4) for hb in range(2)] + \
        [("r", e, hh, hb) for e in range(NE) for hh in range(2) for hb in range(2)]


class Plan:
    def __init__(self):
        self.ops = {"tensor": [], "vector": [], "scalar": [], "sync": [], "gpsimd": []}
        self.cnt = {"sd": 0, "gd": 0, "t": 0, "v": 0, "s": 0,
                    "kva": 0, "kvb": 0, "w1a": 0, "w1b": 0, "w2a": 0, "w2b": 0}

    def emit(self, eng, fn, waits=None, inc=None, amt=1):
        w = {k: v for k, v in (waits or {}).items() if v and v > 0}
        self.ops[eng].append((w, fn, inc, amt))
        if inc is not None:
            self.cnt[inc] += amt
        return dict(self.cnt)


LIMIT = {"heads": NH, "ln1": 1, "p3": 1, "experts": 18, "ln2": 1, "astage": 9}


def build_program():
    nheads = LIMIT["heads"]
    nroute = min(NE, LIMIT["experts"])
    do_shared = LIMIT["experts"] >= 17
    A = LIMIT["astage"]
    nc = bass.Bass("TRN2", target_bir_lowering=False)

    d_xmy = nc.dram_tensor("xmy", [1024, H], F32, kind="ExternalInput").ap()
    d_xkv = nc.dram_tensor("xkv", [L, H], F32, kind="ExternalInput").ap()
    d_w13 = nc.dram_tensor("w13t", [NE, 22, 128, 2048], F32, kind="ExternalInput").ap()
    d_sw13 = nc.dram_tensor("sw13t", [44, 128, 2048], F32, kind="ExternalInput").ap()
    d_w2 = nc.dram_tensor("w2t", [NE, 2, IC, 128, 1024], F32, kind="ExternalInput").ap()
    d_sw2 = nc.dram_tensor("sw2t", [4, IC, 128, 1024], F32, kind="ExternalInput").ap()
    d_gw = nc.dram_tensor("gwt", [16, 128, 16], F32, kind="ExternalInput").ap()
    d_bx = nc.dram_tensor("bxm", [8, 128, 256], F32, kind="ExternalInput").ap()
    d_by = nc.dram_tensor("bym", [8, 128, 256], F32, kind="ExternalInput").ap()
    d_id = nc.dram_tensor("identb", [128, 128], BF16, kind="ExternalInput").ap()
    d_tri = nc.dram_tensor("trib", [128, 128], BF16, kind="ExternalInput").ap()
    d_one = nc.dram_tensor("onesb", [128, 128], BF16, kind="ExternalInput").ap()
    d_lts = nc.dram_tensor("ltsm", [128, 128], BF16, kind="ExternalInput").ap()
    d_iota = nc.dram_tensor("iotam", [128, 512], FP16, kind="ExternalInput").ap()
    d_out = nc.dram_tensor("out", [1024, H], F32, kind="ExternalOutput").ap()

    from contextlib import ExitStack
    with ExitStack() as ctx:
        stream = ctx.enter_context(nc.sbuf_tensor("stream", [128, NT * H], F32))
        xt = ctx.enter_context(nc.sbuf_tensor("xt", [128, NT * H], BF16))
        arena = ctx.enter_context(nc.sbuf_tensor("arena", [128, 22528], BF16))
        wb0 = ctx.enter_context(nc.sbuf_tensor("wb0", [128, 4096], BF16))
        wb1 = ctx.enter_context(nc.sbuf_tensor("wb1", [128, 4096], BF16))
        ws0 = ctx.enter_context(nc.sbuf_tensor("ws0", [128, IC * 512], BF16))
        ws1 = ctx.enter_context(nc.sbuf_tensor("ws1", [128, IC * 512], BF16))
        x1r6 = ctx.enter_context(nc.sbuf_tensor("x1r6", [128, 6 * H], BF16))
        ltsb = ctx.enter_context(nc.sbuf_tensor("ltsb", [128, 128], BF16))
        iotab = ctx.enter_context(nc.sbuf_tensor("iotab", [128, 512], FP16))
        sstat = ctx.enter_context(nc.sbuf_tensor("sstat", [128, 24], F32))
        sklb = ctx.enter_context(nc.sbuf_tensor("sklb", [128, 24], BF16))
        sg = ctx.enter_context(nc.sbuf_tensor("sg", [128, 512], F32))
        comb = ctx.enter_context(nc.sbuf_tensor("comb", [128, NT * 16], F32))
        stv = ctx.enter_context(nc.sbuf_tensor("stat", [128, 64], F32))
        gsm = ctx.enter_context(nc.sbuf_tensor("gsm", [128, 16], F32))
        g16 = ctx.enter_context(nc.sbuf_tensor("g16", [128, 16], F32))
        m8 = ctx.enter_context(nc.sbuf_tensor("m8", [128, 8], F32))
        pb0 = ctx.enter_context(nc.psum_tensor("pb0", [128, 512], F32))
        pb1 = ctx.enter_context(nc.psum_tensor("pb1", [128, 512], F32))
        pb2 = ctx.enter_context(nc.psum_tensor("pb2", [128, 512], F32))
        pb3 = ctx.enter_context(nc.psum_tensor("pb3", [128, 512], F32))
        pb4 = ctx.enter_context(nc.psum_tensor("pb4", [128, 512], F32))
        pb5 = ctx.enter_context(nc.psum_tensor("pb5", [128, 512], F32))
        pbtA = ctx.enter_context(nc.psum_tensor("pbtA", [128, 1024], BF16))
        pbtB = ctx.enter_context(nc.psum_tensor("pbtB", [128, 1024], BF16))
        sem_sd = ctx.enter_context(nc.semaphore("sem_sd"))
        sem_gd = ctx.enter_context(nc.semaphore("sem_gd"))
        sem_t = ctx.enter_context(nc.semaphore("sem_t"))
        sem_v = ctx.enter_context(nc.semaphore("sem_v"))
        sem_s = ctx.enter_context(nc.semaphore("sem_s"))
        sem_kva = ctx.enter_context(nc.semaphore("sem_kva"))
        sem_kvb = ctx.enter_context(nc.semaphore("sem_kvb"))
        sem_w1a = ctx.enter_context(nc.semaphore("sem_w1a"))
        sem_w1b = ctx.enter_context(nc.semaphore("sem_w1b"))
        sem_w2a = ctx.enter_context(nc.semaphore("sem_w2a"))
        sem_w2b = ctx.enter_context(nc.semaphore("sem_w2b"))
        sems = {"sd": sem_sd, "gd": sem_gd, "t": sem_t, "v": sem_v, "s": sem_s,
                "kva": sem_kva, "kvb": sem_kvb, "w1a": sem_w1a, "w1b": sem_w1b,
                "w2a": sem_w2a, "w2b": sem_w2b}
        pbs = [pb0, pb1, pb2, pb3, pb4, pb5]
        pbth = [pbtA, pbtB]
        wbufs = [wb0, wb1]
        wsbufs = [ws0, ws1]

        def ar(off, n):
            return arena[:, off:off + n]
        VB = [ar(0, 2048), ar(2048, 2048)]
        KTB = [ar(4096, 2048), ar(6144, 2048)]
        PSB = ar(8192, 2048)
        PTB = ar(10240, 2048)
        QTA = ar(12288, 1024)
        BXS = ar(13312, 2048)
        BYS = ar(15360, 2048)
        IDT = ar(17408, 128)
        TRI = ar(17536, 128)
        ONE = ar(17664, 128)
        GWS = ar(17792, 256)
        TMPB = ar(18048, 2048)
        ABUF = [arena[:, 0:11264], arena[:, 11264:22528]]

        def x1r(tt):
            if tt == 6:
                return arena[:, 13312:15360]
            if tt == 7:
                return arena[:, 15360:17408]
            return x1r6[:, tt * H:(tt + 1) * H]


        P = Plan()
        cp = {}

        for tt in range(NT):
            def ld(E, tt=tt):
                return E.dma_start(out=stream[:, tt * H:(tt + 1) * H],
                                   in_=d_xmy[tt * 128:(tt + 1) * 128, :])
            P.emit("sync", ld, inc="sd", amt=16)
        for dst, srcd in ((IDT, d_id), (TRI, d_tri), (ONE, d_one), (ltsb[:], d_lts),
                          (iotab[:], d_iota)):
            def ld(E, dst=dst, srcd=srcd):
                return E.dma_start(out=dst, in_=srcd)
            P.emit("sync", ld, inc="sd", amt=16)
        def zsr(E):
            return E.memset(sklb[:, 16:24], 0.0)
        P.emit("vector", zsr, inc="v")
        cp["stream"] = P.cnt["sd"]
        cp["consts_sd"] = P.cnt["sd"]

        if LIMIT["heads"] < NH or LIMIT["experts"] < 18:
            for src in (d_xkv[0:128, 0:128], d_w13[0, 0, :, 0:128], d_sw13[0, :, 0:128],
                        d_w2[0, 0, 0, :, 0:128], d_sw2[0, 0, :, 0:128], d_gw[0],
                        d_bx[0, :, 0:128], d_by[0, :, 0:128]):
                def ld(E, src=src):
                    return E.dma_start(out=wb0[:, 0:src.shape[-1]], in_=src)
                P.emit("gpsimd", ld, waits={"gd": P.cnt["gd"]}, inc="gd", amt=16)
                cp["dbg_touch"] = P.cnt["gd"]

        for half in range(2):
            def ld(E, half=half):
                return E.dma_start(
                    out=xt[:, half * 4 * H:(half + 1) * 4 * H].rearrange("p (tt e) -> p tt e", tt=4),
                    in_=d_xmy[half * 512:(half + 1) * 512, :].rearrange("(tt p) e -> p tt e", p=128))
            P.emit("gpsimd", ld, inc="gd", amt=16)
        for dst, src in ((BXS, d_bx), (BYS, d_by), (GWS, d_gw)):
            def ld(E, dst=dst, src=src):
                return E.dma_start(out=dst.rearrange("p (a b) -> p a b", b=src.shape[2]),
                                   in_=src.rearrange("a p b -> p a b"))
            P.emit("gpsimd", ld, inc="gd", amt=16)
        cp["xtb"] = P.cnt["gd"]
        cp["consts_gd"] = P.cnt["gd"]

        w13_emitted = set()

        def emit_w13_load(gj):
            if gj >= len(W13SEQ) or gj in w13_emitted:
                return
            w13_emitted.add(gj)
            kind, e, k = W13SEQ[gj][0], W13SEQ[gj][1], W13SEQ[gj][2]
            buf = gj % 2
            waits = {}
            if "dbg_touch" in cp:
                waits["gd"] = cp["consts_gd"]
            if ("w13used", buf) in cp:
                waits["t"] = cp[("w13used", buf)]
            if kind == "r":
                def ld(E, e=e, k=k, buf=buf):
                    return E.dma_start(
                        out=wbufs[buf][:].rearrange("p (g c) -> p g c", g=2),
                        in_=d_w13[e, 2 * k:2 * k + 2].rearrange("g p c -> p g c"))
            else:
                def ld(E, k=k, buf=buf):
                    return E.dma_start(
                        out=wbufs[buf][:].rearrange("p (g c) -> p g c", g=2),
                        in_=d_sw13[2 * k:2 * k + 2].rearrange("g p c -> p g c"))
            sk = "w1a" if buf == 0 else "w1b"
            cp[("w13ld", gj)] = (sk, P.emit("gpsimd", ld, waits=waits, inc=sk, amt=16)[sk])

        w2_emitted = set()

        def emit_w2_load(wj):
            if wj >= len(W2SEQ) or wj in w2_emitted:
                return
            w2_emitted.add(wj)
            kind, e, hh, hb = W2SEQ[wj][0], W2SEQ[wj][1], W2SEQ[wj][2], W2SEQ[wj][3]
            buf = wj % 2
            waits = {}
            if ("w2used", buf) in cp:
                waits["t"] = cp[("w2used", buf)]
            if kind == "r":
                def ld(E, e=e, hh=hh, hb=hb, buf=buf):
                    return E.dma_start(
                        out=wsbufs[buf][:].rearrange("p (i c) -> p i c", i=IC),
                        in_=d_w2[e, hh][:, :, hb * 512:(hb + 1) * 512]
                        .rearrange("i p c -> p i c"))
            else:
                def ld(E, hh=hh, hb=hb, buf=buf):
                    return E.dma_start(
                        out=wsbufs[buf][:].rearrange("p (i c) -> p i c", i=IC),
                        in_=d_sw2[hh][:, :, hb * 512:(hb + 1) * 512]
                        .rearrange("i p c -> p i c"))
            sk = "w2a" if buf == 0 else "w2b"
            cp[("w2ld", wj)] = (sk, P.emit("gpsimd", ld, waits=waits, inc=sk, amt=16)[sk])

        def emit_vload(h):
            waits = {}
            if h >= 2:
                waits["t"] = cp.get(("pv_done", h - 2), 0)
            def ld(E, h=h):
                return E.dma_start(
                    out=VB[h % 2].rearrange("p (kt d) -> p kt d", kt=16),
                    in_=d_xkv[:, h * 128:(h + 1) * 128].rearrange("(kt p) d -> p kt d", p=128))
            sk = "kva" if h % 2 == 0 else "kvb"
            cp[("vload", h)] = (sk, P.emit("gpsimd", ld, waits=waits, inc=sk, amt=16)[sk])

        if nheads > 0:
            emit_vload(0)
        if nheads > 1:
            emit_vload(1)

        pbt_free = [0, 0]

        for h in range(nheads):
            buf = h % 2
            for g in range(4 if A >= 1 else 0):
                half = g % 2
                vsk, vsv = cp[("vload", h)]
                waits = {vsk: vsv, "v": pbt_free[half], "sd": cp["consts_sd"]}
                def tr(E, g=g, half=half, buf=buf):
                    ins = None
                    for j in range(4):
                        ins = E.transpose(pbth[half][:, j * 128:(j + 1) * 128],
                                          VB[buf][:, (g * 4 + j) * 128:(g * 4 + j + 1) * 128], IDT)
                    return ins
                tcnt = P.emit("tensor", tr, waits=waits, inc="t")["t"]
                def ev(E, g=g, half=half, buf=buf):
                    return E.tensor_copy(KTB[buf][:, g * 512:(g + 1) * 512],
                                         pbth[half][:, 0:512])
                pbt_free[half] = P.emit("vector", ev, waits={"t": tcnt}, inc="v")["v"]
                cp[("ktev", h, g)] = pbt_free[half]
            for g in ((1, 0) if A >= 2 else ()):
                half = g % 2
                waits = {"gd": cp["xtb"], "v": pbt_free[half]}
                if h >= 1:
                    waits["t"] = cp.get(("qk_done", h - 1, 4 * g), 0)
                def tr(E, g=g, half=half, h=h):
                    ins = None
                    for j in range(4):
                        s = g * 4 + j
                        ins = E.transpose(pbth[half][:, j * 128:(j + 1) * 128],
                                          xt[:, s * H + h * 128: s * H + (h + 1) * 128], IDT)
                    return ins
                tcnt = P.emit("tensor", tr, waits=waits, inc="t")["t"]
                def ev(E, g=g, half=half):
                    return E.tensor_copy(QTA[:, g * 512:(g + 1) * 512],
                                         pbth[half][:, 0:512])
                pbt_free[half] = P.emit("vector", ev, waits={"t": tcnt}, inc="v")["v"]
                cp[("qtev", h, g)] = pbt_free[half]
            cp[("qtready", h)] = P.cnt["v"]

            def emit_qk(s, i):
                N = N_LIST[s]
                ncols = N * 128
                kbn = (ncols + 511) // 512
                par = i % 2
                ktw = cp[("ktev", h, (N + 3) // 4 - 1)]
                qtw = cp[("qtev", h, s // 4)]
                waits = {"v": max(ktw, qtw), "gd": cp["consts_gd"]}
                if (h, i) != (0, 0):
                    ph, ps = (h, SEQ[i - 1]) if i > 0 else (h - 1, SEQ[7])
                    waits["s"] = cp.get(("exp_done", ph, ps), 0)
                def qk(E, s=s, ncols=ncols, kbn=kbn, h=h):
                    ins = None
                    for kb in range(kbn):
                        n = min(512, ncols - kb * 512)
                        last = kb == kbn - 1
                        ins = E.matmul(pbs[kb][:, 0:n], QTA[:, s * 128:(s + 1) * 128],
                                       KTB[h % 2][:, kb * 512: kb * 512 + n],
                                       start=True, stop=not last)
                        if last:
                            noff = n - 256
                            ins = E.matmul(pbs[kb][:, noff:noff + 256], TRI,
                                           BXS[:, s * 256:(s + 1) * 256],
                                           start=False, stop=False)
                            ins = E.matmul(pbs[kb][:, noff:noff + 256], ONE,
                                           BYS[:, s * 256:(s + 1) * 256],
                                           start=False, stop=True)
                    return ins
                cp[("qk_done", h, s)] = P.emit("tensor", qk, waits=waits, inc="t")["t"]
                waits = {"t": cp[("qk_done", h, s)]}
                if not (h == 0 and i < 2):
                    ph, ps = (h, SEQ[i - 2]) if i >= 2 else (h - 1, SEQ[6 + i])
                    waits["s"] = cp.get(("exp_done", ph, ps), 0)
                def mx(E, s=s, ncols=ncols, kbn=kbn, par=par):
                    for kb in range(kbn):
                        n = min(512, ncols - kb * 512)
                        E.reduce_max(stv[:, 59 + kb:60 + kb], pbs[kb][:, 0:n], axis=AX.X)
                    E.drain()
                    E.reduce_max(stv[:, 48 + par:49 + par], stv[:, 59:59 + kbn],
                                 axis=AX.X, negate=True)
                    E.drain()
                    return E.tensor_scalar(stv[:, 48 + par:49 + par],
                                           stv[:, 48 + par:49 + par], SCALE, None,
                                           op0=ALU.mult)
                cp[("negm", h, s)] = P.emit("vector", mx, waits=waits, inc="v")["v"]

            def emit_exp(s, i):
                N = N_LIST[s]
                ncols = N * 128
                kbn = (ncols + 511) // 512
                par = i % 2
                waits = {"v": cp[("negm", h, s)]}
                if (h, i) != (0, 0):
                    ph, ps = (h, SEQ[i - 1]) if i > 0 else (h - 1, SEQ[7])
                    waits["t"] = cp.get(("ptr_done", ph, ps), 0)
                def ex(E, s=s, ncols=ncols, kbn=kbn, par=par):
                    ins = None
                    for kb in range(kbn):
                        n = min(512, ncols - kb * 512)
                        ins = E.activation(PSB[:, kb * 512: kb * 512 + n], pbs[kb][:, 0:n],
                                           AF.Exp, bias=stv[:, 48 + par:49 + par],
                                           scale=SCALE,
                                           accum_out=stv[:, 40 + par * 4 + kb:41 + par * 4 + kb])
                    return ins
                cp[("exp_done", h, s)] = P.emit("scalar", ex, waits=waits, inc="s")["s"]
                def lv(E, s=s, kbn=kbn, par=par):
                    E.reduce_sum(stv[:, 50 + par:51 + par],
                                 stv[:, 40 + par * 4: 40 + par * 4 + kbn], axis=AX.X)
                    E.drain()
                    E.reciprocal(stv[:, 52 + par:53 + par], stv[:, 50 + par:51 + par])
                    return E.drain()
                P.emit("vector", lv, waits={"s": cp[("exp_done", h, s)]}, inc=None)

            def emit_pt(s):
                N = N_LIST[s]
                kt = 0
                while kt < N:
                    m = min(4, N - kt)
                    half = (kt // 4) % 2
                    waits = {"s": cp[("exp_done", h, s)], "v": pbt_free[half]}
                    def tr(E, kt=kt, m=m, half=half):
                        ins = None
                        for j in range(m):
                            ins = E.transpose(pbth[half][:, j * 128:(j + 1) * 128],
                                              PSB[:, (kt + j) * 128:(kt + j + 1) * 128], IDT)
                        return ins
                    tcnt = P.emit("tensor", tr, waits=waits, inc="t")["t"]
                    if kt + m >= N:
                        cp[("ptr_done", h, s)] = tcnt
                    def ev(E, kt=kt, m=m, half=half):
                        return E.tensor_copy(PTB[:, kt * 128:(kt + m) * 128],
                                             pbth[half][:, 0:m * 128])
                    pbt_free[half] = P.emit("vector", ev, waits={"t": tcnt}, inc="v")["v"]
                    kt += m
                cp[("ptev", h, s)] = P.cnt["v"]

            def emit_pv(s, i):
                N = N_LIST[s]
                par = i % 2
                waits = {"v": max(cp[("ptev", h, s)], cp.get(("pv_evac_prev",), 0))}
                def pv(E, s=s, N=N, h=h):
                    ins = None
                    for kt in range(N):
                        ins = E.matmul(pb4[:, 0:128], PTB[:, kt * 128:(kt + 1) * 128],
                                       VB[h % 2][:, kt * 128:(kt + 1) * 128],
                                       start=(kt == 0), stop=(kt == N - 1))
                    return ins
                tcnt = P.emit("tensor", pv, waits=waits, inc="t")["t"]
                cp[("pv_done_slot", h, s)] = tcnt
                if A < 7:
                    return
                def evp(E, s=s, h=h, par=par):
                    return E.scalar_tensor_tensor(
                        out=stream[:, s * H + h * 128: s * H + (h + 1) * 128],
                        in0=pb4[:, 0:128], scalar=stv[:, 52 + par:53 + par],
                        in1=stream[:, s * H + h * 128: s * H + (h + 1) * 128],
                        op0=ALU.mult, op1=ALU.add)
                cp[("pv_evac_prev",)] = P.emit(
                    "vector", evp, waits={"t": tcnt, "sd": cp["stream"]}, inc="v")["v"]

            if A >= 3:
                emit_qk(SEQ[0], 0)
            if A >= 4:
                emit_exp(SEQ[0], 0)
            for i in range(1, 8):
                if A >= 3:
                    emit_qk(SEQ[i], i)
                if A >= 5:
                    emit_pt(SEQ[i - 1])
                if A >= 4:
                    emit_exp(SEQ[i], i)
                if A >= 6:
                    emit_pv(SEQ[i - 1], i - 1)
            if A >= 5:
                emit_pt(SEQ[7])
            if A >= 6:
                emit_pv(SEQ[7], 7)
                cp[("pv_done", h)] = cp[("pv_done_slot", h, SEQ[7])]
            if h + 2 < nheads:
                emit_vload(h + 2)
            g0 = 0 if do_shared else 22
            q0 = 0 if do_shared else 8
            if h == 1:
                emit_w13_load(g0)
            elif h == 2:
                emit_w13_load(g0 + 1)
            elif h == 3:
                emit_w2_load(q0)
            elif h == 6:
                emit_w2_load(q0 + 1)

        if LIMIT["experts"] > 0:
            g0 = 0 if do_shared else 22
            q0 = 0 if do_shared else 8
            emit_w13_load(g0)
            emit_w13_load(g0 + 1)
            emit_w2_load(q0)
            emit_w2_load(q0 + 1)

        def emit_ln():
            for tt in range(NT):
                def rs(E, tt=tt):
                    if tt == 0:
                        E.drain()
                    return E.reduce_sum(stv[:, tt:tt + 1], stream[:, tt * H:(tt + 1) * H], axis=AX.X)
                P.emit("vector", rs, waits={"sd": cp["stream"]}, inc="v")
            vafter = P.cnt["v"]
            for tt in range(NT):
                def sq(E, tt=tt):
                    return E.activation(ABUF[0][:, 0:2048], stream[:, tt * H:(tt + 1) * H],
                                        AF.Square, accum_out=stv[:, 8 + tt:9 + tt])
                P.emit("scalar", sq,
                       waits={"v": vafter, "s": cp.get(("sq_prev",), 0)}, inc="s")
                cp[("sq_prev",)] = P.cnt["s"]
            sq_done = P.cnt["s"]
            def fin1(E):
                E.drain()
                E.tensor_scalar(stv[:, 16:24], stv[:, 0:8], 1.0 / H, None, op0=ALU.mult)
                E.tensor_scalar(stv[:, 24:32], stv[:, 8:16], 1.0 / H, None, op0=ALU.mult)
                E.drain()
                E.tensor_mul(stv[:, 32:40], stv[:, 16:24], stv[:, 16:24])
                E.drain()
                E.tensor_sub(stv[:, 24:32], stv[:, 24:32], stv[:, 32:40])
                E.drain()
                return E.tensor_scalar(stv[:, 24:32], stv[:, 24:32], 1e-5, None, op0=ALU.add)
            c1 = P.emit("vector", fin1, waits={"s": sq_done}, inc="v")["v"]
            def fsq(E):
                return E.activation(stv[:, 32:40], stv[:, 24:32], AF.Sqrt)
            c2 = P.emit("scalar", fsq, waits={"v": c1}, inc="s")["s"]
            def fin2(E):
                E.drain()
                E.reciprocal(stv[:, 40:48], stv[:, 32:40])
                E.drain()
                E.tensor_mul(stv[:, 32:40], stv[:, 40:48], stv[:, 40:48])
                E.drain()
                E.tensor_mul(stv[:, 32:40], stv[:, 32:40], stv[:, 24:32])
                E.drain()
                E.tensor_scalar(stv[:, 32:40], stv[:, 32:40], -0.5, 1.5, op0=ALU.mult, op1=ALU.add)
                E.drain()
                E.tensor_mul(stv[:, 24:32], stv[:, 40:48], stv[:, 32:40])
                return E.drain()
            P.emit("vector", fin2, waits={"s": c2}, inc=None)
            for tt in range(NT):
                def ap(E, tt=tt):
                    return E.tensor_scalar(stream[:, tt * H:(tt + 1) * H],
                                           stream[:, tt * H:(tt + 1) * H],
                                           stv[:, 16 + tt:17 + tt], stv[:, 24 + tt:25 + tt],
                                           op0=ALU.subtract, op1=ALU.mult)
                P.emit("vector", ap, inc="v")
            return P.cnt["v"]

        if LIMIT["ln1"]:
            emit_ln()

        p3range = range(NT) if (LIMIT["p3"] or LIMIT["experts"] > 0) else range(0)
        for tt in p3range:
            def cst(E, tt=tt):
                if tt == 0:
                    E.drain()
                return E.tensor_copy(x1r(tt), stream[:, tt * H:(tt + 1) * H])
            ccast = P.emit("vector", cst, waits={"t": cp.get(("x1tr", tt - 1), 0)},
                           inc="v")["v"]
            for g in range(4):
                half = g % 2
                waits = {"v": max(ccast, pbt_free[half]), "gd": cp["xtb"]}
                if tt == 0 and g < 2 and nheads > 0:
                    waits["t"] = cp.get(("pv_done", nheads - 1), 0)
                def tr(E, g=g, half=half, tt=tt):
                    ins = None
                    for j in range(4):
                        hc = g * 4 + j
                        ins = E.transpose(pbth[half][:, j * 128:(j + 1) * 128],
                                          x1r(tt)[:, hc * 128:(hc + 1) * 128], IDT)
                    return ins
                tcnt = P.emit("tensor", tr, waits=waits, inc="t")["t"]
                def ev(E, g=g, half=half, tt=tt):
                    return E.tensor_copy(
                        xt[:].rearrange("p (hc t) -> p hc t", hc=16)[:, g * 4:(g + 1) * 4,
                                                                     tt * 128:(tt + 1) * 128],
                        pbth[half][:, 0:512].rearrange("p (a b) -> p a b", a=4))
                pbt_free[half] = P.emit("vector", ev, waits={"t": tcnt}, inc="v")["v"]
            cp[("x1t", tt)] = P.cnt["v"]
            cp[("x1tr", tt)] = P.cnt["t"]

        if len(p3range) > 0:
            waits = {"v": max(cp[("x1t", NT - 1)], cp.get(("pv_evac_prev",), 0)),
                     "gd": cp["consts_gd"]}
            def gmm(E):
                ins = None
                for hc in range(16):
                    for tt in range(NT):
                        ins = E.matmul(pb4[:, tt * 16:(tt + 1) * 16],
                                       xt[:, hc * 1024 + tt * 128: hc * 1024 + (tt + 1) * 128],
                                       GWS[:, hc * 16:(hc + 1) * 16],
                                       start=(hc == 0 and tt == 0),
                                       stop=(hc == 15 and tt == NT - 1),
                                       skip_group_check=True)
                return ins
            gq = P.emit("tensor", gmm, waits=waits, inc="t")["t"]
        for tt in p3range:
            waits = {"t": gq}
            if tt > 0:
                waits["s"] = cp[("gexp", tt - 1)]
            def gmx(E, tt=tt):
                return E.reduce_max(stv[:, 54:55], pb4[:, tt * 16:(tt + 1) * 16],
                                    axis=AX.X, negate=True)
            gv = P.emit("vector", gmx, waits=waits, inc="v")["v"]
            def gex(E, tt=tt):
                return E.activation(gsm[:], pb4[:, tt * 16:(tt + 1) * 16], AF.Exp,
                                    bias=stv[:, 54:55], accum_out=stv[:, 55:56])
            cp[("gexp", tt)] = P.emit("scalar", gex, waits={"v": gv}, inc="s")["s"]
            def gtk(E, tt=tt):
                E.drain()
                E.reciprocal(stv[:, 56:57], stv[:, 55:56])
                E.drain()
                E.tensor_scalar_mul(gsm[:], gsm[:], stv[:, 56:57])
                E.drain()
                E.max(out=m8[:], in_=gsm[:])
                E.drain()
                E.tensor_scalar(g16[:], gsm[:], m8[:, 5:6], None, op0=ALU.is_ge)
                E.drain()
                E.tensor_mul(g16[:], gsm[:], g16[:])
                E.drain()
                E.reduce_sum(stv[:, 57:58], g16[:], axis=AX.X)
                E.drain()
                E.tensor_scalar(stv[:, 57:58], stv[:, 57:58], 1e-20, None, op0=ALU.add)
                E.drain()
                E.reciprocal(stv[:, 58:59], stv[:, 57:58])
                E.drain()
                return E.tensor_scalar(comb[:, tt * 16:(tt + 1) * 16], g16[:],
                                       stv[:, 58:59], None, op0=ALU.mult)
            P.emit("vector", gtk, waits={"s": cp[("gexp", tt)]}, inc="v")
        def gdr(E):
            return E.drain()
        P.emit("vector", gdr, inc=None)
        comb_done = P.cnt["v"]

        cp[("m1_silu_prev",)] = cp.get(("exp_done", nheads - 1, 7), 0)

        def selb(tt):
            return xt[:, tt * 2048 + 512: tt * 2048 + 1024]
        SELT = arena[:, 18048:22144]

        def abase(half):
            return half * 5632

        def xgv(hc):
            return xt[:, hc * 1024: hc * 1024 + 512]

        def emit_m1s(gj0, nk, ab_off, rhs_fn, xg_key):
            for k in range(nk):
                gj = gj0 + k
                emit_w13_load(gj)
                wbuf = wbufs[gj % 2]
                wsk, wsv = cp[("w13ld", gj)]
                gb, ub = (pb2, pb3) if k % 2 == 0 else (pb0, pb1)
                waits = {wsk: wsv, "s": cp[("m1_silu_prev",)],
                         "v": cp.get(xg_key, cp[("x1t", NT - 1)])}
                def gmmf(E, wbuf=wbuf, gb=gb, rhs_fn=rhs_fn):
                    ins = None
                    for hc in range(16):
                        ins = E.matmul(gb[:, 0:512], wbuf[:, hc * 128:(hc + 1) * 128],
                                       rhs_fn(hc), start=(hc == 0), stop=(hc == 15),
                                       skip_group_check=True)
                    return ins
                gcnt = P.emit("tensor", gmmf, waits=waits, inc="t")["t"]
                uw = {}
                if cp.get(("m1_aev2",), 0):
                    uw["v"] = cp[("m1_aev2",)]
                def ummf(E, wbuf=wbuf, ub=ub, rhs_fn=rhs_fn):
                    ins = None
                    for hc in range(16):
                        ins = E.matmul(ub[:, 0:512],
                                       wbuf[:, 2048 + hc * 128: 2048 + (hc + 1) * 128],
                                       rhs_fn(hc), start=(hc == 0), stop=(hc == 15),
                                       skip_group_check=True)
                    return ins
                ucnt = P.emit("tensor", ummf, waits=uw, inc="t")["t"]
                cp[("w13used", gj % 2)] = ucnt
                def slf(E, gb=gb):
                    return E.activation(sg[:], gb[:, 0:512], AF.Silu)
                scnt = P.emit("scalar", slf,
                              waits={"t": gcnt, "v": cp.get(("m1_aev_prev",), 0)},
                              inc="s")["s"]
                cp[("m1_silu_prev",)] = scnt
                emit_w13_load(gj + 2)
                icol = ab_off + k * 512
                def aevf(E, ub=ub, icol=icol):
                    return E.tensor_mul(arena[:, icol: icol + 512], sg[:], ub[:, 0:512])
                acnt = P.emit("vector", aevf, waits={"t": ucnt, "s": scnt}, inc="v")["v"]
                cp[("m1_aev2",)] = cp.get(("m1_aev_prev",), 0)
                cp[("m1_aev_prev",)] = acnt
            return P.cnt["v"]

        def emit_routing(en):
            def mskf(E, en=en):
                E.drain()
                E.tensor_scalar(sklb[:, 0:8].rearrange("p (a b) -> p a b", b=1),
                                comb[:].rearrange("p (t g) -> p t g", g=16)[:, :, en:en + 1],
                                1e-30, None, op0=ALU.is_ge)
                E.drain()
                E.tensor_tensor_scan(sklb[:, 8:16], sklb[:, 0:8], sklb[:, 16:24],
                                     0.0, op0=ALU.add, op1=ALU.add)
                E.drain()
                E.tensor_sub(sklb[:, 8:16], sklb[:, 8:16], sklb[:, 0:8])
                return E.drain()
            mc = P.emit("vector", mskf,
                        waits={"v": comb_done, "t": cp.get(("posmm", en - 1), 0)},
                        inc="v")["v"]
            def posmm(E):
                E.matmul(pb0[:, 0:8], ltsb[:], sklb[:, 0:8], start=True, stop=False,
                         skip_group_check=True)
                return E.matmul(pb0[:, 0:8], ONE, sklb[:, 8:16], start=False, stop=True,
                                skip_group_check=True)
            pt_ = P.emit("tensor", posmm,
                         waits={"v": mc, "s": cp[("m1_silu_prev",)]}, inc="t")["t"]
            cp[("posmm", en)] = pt_
            def posev(E):
                E.drain()
                E.tensor_scalar(sstat[:, 0:8], pb0[:, 0:8], 1e6, None, op0=ALU.add)
                E.drain()
                E.scalar_tensor_tensor(out=sstat[:, 8:16], in0=sklb[:, 0:8],
                                       scalar=-1e6, in1=sstat[:, 0:8],
                                       op0=ALU.mult, op1=ALU.add)
                return E.drain()
            pv_ = P.emit("vector", posev, waits={"t": pt_}, inc="v")["v"]
            cp[("pos", en)] = pv_
            for tt in range(NT):
                def self_(E, tt=tt):
                    return E.tensor_scalar(selb(tt), iotab[:],
                                           sstat[:, 8 + tt:9 + tt], None, op0=ALU.is_equal)
                P.emit("vector", self_,
                       waits={"v": pv_, "t": cp.get(("gath_done", en - 1), 0)},
                       inc="v")
            cp[("sel", en)] = P.cnt["v"]

        def emit_selt(en):
            for tt in range(NT):
                half = tt % 2
                waits = {"v": max(cp[("sel", en)], pbt_free[half])}
                def trf(E, tt=tt, half=half):
                    ins = None
                    for ct in range(4):
                        ins = E.transpose(pbth[half][:, ct * 128:(ct + 1) * 128],
                                          selb(tt)[:, ct * 128:(ct + 1) * 128],
                                          IDT)
                    return ins
                tcnt = P.emit("tensor", trf, waits=waits, inc="t")["t"]
                def evt(E, tt=tt, half=half):
                    return E.tensor_copy(
                        SELT[:].rearrange("p (ct t c) -> p ct t c", ct=4, t=8)[:, :, tt, :],
                        pbth[half][:, 0:512].rearrange("p (a b) -> p a b", a=4))
                pbt_free[half] = P.emit("vector", evt, waits={"t": tcnt}, inc="v")["v"]
            cp[("selt", en)] = P.cnt["v"]

        def emit_gather(en):
            for hc in range(16):
                gb = pb0 if hc % 2 == 0 else pb1
                wv = max(cp[("sel", en)], cp.get(("xgev", hc % 2), 0))
                if hc < 2:
                    wv = max(wv, cp.get(("scatev", 0), 0), cp.get(("scatev", 1), 0),
                             cp.get(("pos", en), 0))
                def gmf(E, gb=gb, hc=hc):
                    ins = None
                    for tt in range(NT):
                        ins = E.matmul(gb[:, 0:512], x1r(tt)[:, hc * 128:(hc + 1) * 128],
                                       selb(tt),
                                       start=(tt == 0), stop=(tt == NT - 1))
                    return ins
                tcnt = P.emit("tensor", gmf, waits={"v": wv}, inc="t")["t"]
                def xev(E, gb=gb, hc=hc):
                    return E.tensor_copy(xgv(hc), gb[:, 0:512])
                cp[("xgev", hc % 2)] = P.emit("vector", xev, waits={"t": tcnt},
                                              inc="v")["v"]
            cp[("xgd", en)] = P.cnt["v"]
            cp[("gath_done", en)] = P.cnt["t"]

        def emit_m2_shared(th, adep):
            gidx = cp.get(("m2gidx",), 0)
            for q in range(4):
                for hb in range(2):
                    wj = th * 8 + q * 2 + hb
                    emit_w2_load(wj)
                    wsb = wsbufs[wj % 2]
                    w2k, w2v = cp[("w2ld", wj)]
                    for ct in range(4):
                        bk = pb4 if gidx % 2 == 0 else pb5
                        pkey = ("m2ev", gidx % 2)
                        waits = {w2k: w2v, "v": adep, "s": cp[("m1_silu_prev",)]}
                        if pkey in cp:
                            waits["v"] = max(waits["v"], cp[pkey])
                        ihalf = q // 2
                        def mmf(E, bk=bk, wsb=wsb, ct=ct, ihalf=ihalf):
                            ins = None
                            for ici in range(IC):
                                icol = (ihalf * 11 + ici) * 512
                                ins = E.matmul(bk[:, 0:512],
                                               arena[:, icol + ct * 128: icol + (ct + 1) * 128],
                                               wsb[:, ici * 512:(ici + 1) * 512],
                                               start=(ici == 0), stop=(ici == IC - 1),
                                               skip_group_check=True)
                            return ins
                        tcnt = P.emit("tensor", mmf, waits=waits, inc="t")["t"]
                        if ct == 3:
                            cp[("w2used", wj % 2)] = tcnt
                        tt = th * 4 + ct
                        def evf(E, bk=bk, tt=tt, q=q, hb=hb):
                            dst = stream[:, tt * H + (q % 2) * 1024 + hb * 512:
                                         tt * H + (q % 2) * 1024 + (hb + 1) * 512]
                            return E.scalar_tensor_tensor(out=dst, in0=bk[:, 0:512],
                                                          scalar=1.0, in1=dst,
                                                          op0=ALU.mult, op1=ALU.add)
                        c = P.emit("vector", evf,
                                   waits={"t": tcnt,
                                          "v": max(comb_done,
                                                   cp.get(("evf_prev", tt), 0))},
                                   inc="v")["v"]
                        cp[pkey] = c
                        cp[("evf_prev", tt)] = c
                        gidx += 1
                    emit_w2_load(wj + 2)
            cp[("m2gidx",)] = gidx

        def emit_m2_routed(e, wj0, adep):
            ybase = abase(1 - e % 2)
            gidx = cp.get(("m2gidx",), 0)
            for hh in range(2):
                for hb in range(2):
                    wj = wj0 + hh * 2 + hb
                    emit_w2_load(wj)
                    wsb = wsbufs[wj % 2]
                    w2k, w2v = cp[("w2ld", wj)]
                    for ct in range(4):
                        bk = pb4 if gidx % 2 == 0 else pb5
                        pkey = ("m2ev", gidx % 2)
                        waits = {w2k: w2v, "v": adep, "s": cp[("m1_silu_prev",)]}
                        if pkey in cp:
                            waits["v"] = max(waits["v"], cp[pkey])
                        abse = abase(e % 2)
                        def mmf(E, bk=bk, wsb=wsb, ct=ct, abse=abse):
                            ins = None
                            for ici in range(IC):
                                ins = E.matmul(
                                    bk[:, 0:512],
                                    arena[:, abse + ici * 512 + ct * 128:
                                           abse + ici * 512 + (ct + 1) * 128],
                                    wsb[:, ici * 512:(ici + 1) * 512],
                                    start=(ici == 0), stop=(ici == IC - 1),
                                    skip_group_check=True)
                            return ins
                        tcnt = P.emit("tensor", mmf, waits=waits, inc="t")["t"]
                        if ct == 3:
                            cp[("w2used", wj % 2)] = tcnt
                        yc = ybase + ct * 1024 + hb * 512
                        def evf(E, bk=bk, yc=yc):
                            return E.tensor_copy(arena[:, yc: yc + 512], bk[:, 0:512])
                        c = P.emit("vector", evf, waits={"t": tcnt}, inc="v")["v"]
                        cp[pkey] = c
                        gidx += 1
                    emit_w2_load(wj + 2)
                if hh == 0 and e + 1 < nroute:
                    emit_routing(e + 1)
                ev_y = P.cnt["v"]
                for tt in range(NT):
                    sbk = (pb0, pb1) if tt % 2 == 0 else (pb2, pb3)
                    skey = ("scatev", tt % 2)
                    wv = max(ev_y, cp[("selt", e)], cp.get(skey, 0))
                    if tt < 2:
                        wv = max(wv, cp.get(("pos", e + 1), 0))
                    def smm(E, sbk=sbk, tt=tt, ybase=ybase):
                        ins = None
                        for ct in range(4):
                            lhs = SELT[:, (ct * 8 + tt) * 128:(ct * 8 + tt + 1) * 128]
                            for hb in range(2):
                                ins = E.matmul(
                                    sbk[hb][:, 0:512], lhs,
                                    arena[:, ybase + ct * 1024 + hb * 512:
                                           ybase + ct * 1024 + (hb + 1) * 512],
                                    start=(ct == 0), stop=(ct == 3),
                                    skip_group_check=True)
                        return ins
                    tcnt = P.emit("tensor", smm, waits={"v": wv}, inc="t")["t"]
                    def sev(E, sbk=sbk, tt=tt, hh=hh, e=e):
                        ins = None
                        for hb in range(2):
                            dst = stream[:, tt * H + hh * 1024 + hb * 512:
                                         tt * H + hh * 1024 + (hb + 1) * 512]
                            ins = E.scalar_tensor_tensor(
                                out=dst, in0=sbk[hb][:, 0:512],
                                scalar=comb[:, tt * 16 + e: tt * 16 + e + 1],
                                in1=dst, op0=ALU.mult, op1=ALU.add)
                        return ins
                    c = P.emit("vector", sev,
                               waits={"t": tcnt,
                                      "v": max(comb_done,
                                               cp.get(("evf_prev", tt), 0))},
                               inc="v")["v"]
                    cp[skey] = c
                    cp[("evf_prev", tt)] = c
            cp[("m2gidx",)] = gidx

        def emit_m1d(gj0, nk):
            for k in range(nk):
                gj = gj0 + k
                emit_w13_load(gj)
                wbuf = wbufs[gj % 2]
                wsk, wsv = cp[("w13ld", gj)]
                waits = {wsk: wsv, "s": cp[("m1_silu_prev",)],
                         "v": cp[("x1t", NT - 1)]}
                for tb in range(2):
                    gb, ub = (pb2, pb3) if tb == 0 else (pb0, pb1)
                    def gmmf(E, wbuf=wbuf, gb=gb, tb=tb):
                        ins = None
                        for hc in range(16):
                            ins = E.matmul(gb[:, 0:512], wbuf[:, hc * 128:(hc + 1) * 128],
                                           xt[:, hc * 1024 + tb * 512: hc * 1024 + (tb + 1) * 512],
                                           start=(hc == 0), stop=(hc == 15),
                                           skip_group_check=True)
                        return ins
                    gcnt = P.emit("tensor", gmmf, waits=waits if tb == 0 else {},
                                  inc="t")["t"]
                    uw = {}
                    if cp.get(("m1_aev2",), 0):
                        uw["v"] = cp[("m1_aev2",)]
                    def ummf(E, wbuf=wbuf, ub=ub, tb=tb):
                        ins = None
                        for hc in range(16):
                            ins = E.matmul(ub[:, 0:512],
                                           wbuf[:, 2048 + hc * 128: 2048 + (hc + 1) * 128],
                                           xt[:, hc * 1024 + tb * 512: hc * 1024 + (tb + 1) * 512],
                                           start=(hc == 0), stop=(hc == 15),
                                           skip_group_check=True)
                        return ins
                    ucnt = P.emit("tensor", ummf, waits=uw, inc="t")["t"]
                    if tb == 1:
                        cp[("w13used", gj % 2)] = ucnt
                    def slf(E, gb=gb):
                        return E.activation(sg[:], gb[:, 0:512], AF.Silu)
                    scnt = P.emit("scalar", slf,
                                  waits={"t": gcnt, "v": cp.get(("m1_aev_prev",), 0)},
                                  inc="s")["s"]
                    cp[("m1_silu_prev",)] = scnt
                    icol = k * 1024 + tb * 512
                    def aevf(E, ub=ub, icol=icol):
                        return E.tensor_mul(arena[:, icol: icol + 512], sg[:], ub[:, 0:512])
                    acnt = P.emit("vector", aevf, waits={"t": ucnt, "s": scnt},
                                  inc="v")["v"]
                    cp[("m1_aev2",)] = cp.get(("m1_aev_prev",), 0)
                    cp[("m1_aev_prev",)] = acnt
                emit_w13_load(gj + 2)
            return P.cnt["v"]

        def emit_m2_sharedAB(ihalf, adep):
            gidx = cp.get(("m2gidx",), 0)
            for q in (2 * ihalf, 2 * ihalf + 1):
                for hb in range(2):
                    wj = q * 2 + hb
                    emit_w2_load(wj)
                    wsb = wsbufs[wj % 2]
                    w2k, w2v = cp[("w2ld", wj)]
                    for tt in range(NT):
                        bk = pb4 if gidx % 2 == 0 else pb5
                        pkey = ("m2ev", gidx % 2)
                        waits = {w2k: w2v, "v": adep, "s": cp[("m1_silu_prev",)]}
                        if pkey in cp:
                            waits["v"] = max(waits["v"], cp[pkey])
                        def mmf(E, bk=bk, wsb=wsb, tt=tt):
                            ins = None
                            for ici in range(IC):
                                ins = E.matmul(bk[:, 0:512],
                                               arena[:, ici * 1024 + tt * 128: ici * 1024 + (tt + 1) * 128],
                                               wsb[:, ici * 512:(ici + 1) * 512],
                                               start=(ici == 0), stop=(ici == IC - 1),
                                               skip_group_check=True)
                            return ins
                        tcnt = P.emit("tensor", mmf, waits=waits, inc="t")["t"]
                        if tt == NT - 1:
                            cp[("w2used", wj % 2)] = tcnt
                        def evf(E, bk=bk, tt=tt, q=q, hb=hb):
                            dst = stream[:, tt * H + (q % 2) * 1024 + hb * 512:
                                         tt * H + (q % 2) * 1024 + (hb + 1) * 512]
                            return E.scalar_tensor_tensor(out=dst, in0=bk[:, 0:512],
                                                          scalar=1.0, in1=dst,
                                                          op0=ALU.mult, op1=ALU.add)
                        c = P.emit("vector", evf,
                                   waits={"t": tcnt,
                                          "v": max(comb_done,
                                                   cp.get(("evf_prev", tt), 0))},
                                   inc="v")["v"]
                        cp[pkey] = c
                        cp[("evf_prev", tt)] = c
                        gidx += 1
                    emit_w2_load(wj + 2)
            cp[("m2gidx",)] = gidx

        if do_shared:
            adep_a = emit_m1d(0, 11)
            emit_m2_sharedAB(0, adep_a)
            adep_b = emit_m1d(11, 11)
            if nroute > 0:
                emit_routing(0)
            emit_m2_sharedAB(1, adep_b)
        if nroute > 0:
            if ("sel", 0) not in cp:
                emit_routing(0)
            for e in range(nroute):
                emit_selt(e)
                emit_gather(e)
                adep = emit_m1s(22 + e * 11, 11, abase(e % 2), xgv, ("xgd", e))
                emit_m2_routed(e, 8 + e * 4, adep)

        if LIMIT["ln2"]:
            emit_ln()
        st_wait = P.cnt["v"]
        for tt in range(NT):
            def st(E, tt=tt):
                return E.dma_start(out=d_out[tt * 128:(tt + 1) * 128, :],
                                   in_=stream[:, tt * H:(tt + 1) * H])
            P.emit("sync", st, waits={"v": st_wait}, inc="sd", amt=16)
        final_sd = P.cnt["sd"]
        def fw(E):
            return E.wait_ge(sems["sd"], final_sd)
        P.ops["sync"].append(({}, fw, None, 0))

        with nc.Block() as block:
            def replay(E, name):
                lastw = {}
                for waits, fn, inc, amt in P.ops[name]:
                    for sname, val in waits.items():
                        if val > 0 and lastw.get(sname, -1) < val:
                            E.wait_ge(sems[sname], val)
                            lastw[sname] = val
                    ins = fn(E)
                    if inc is not None:
                        ins.then_inc(sems[inc], amt)

            @block.sync
            def _(E):
                replay(E, "sync")

            @block.gpsimd
            def _(E):
                replay(E, "gpsimd")

            @block.tensor
            def _(E):
                replay(E, "tensor")

            @block.vector
            def _(E):
                replay(E, "vector")

            @block.scalar
            def _(E):
                replay(E, "scalar")

    return nc


def _host_prep(L0_ffn_gate_w, L0_ffn_w13, L0_ffn_w2, L0_ffn_sw13, L0_ffn_sw2):
    gw = np.asarray(L0_ffn_gate_w, dtype=np.float32)
    w13 = np.asarray(L0_ffn_w13, dtype=np.float32)
    w2 = np.asarray(L0_ffn_w2, dtype=np.float32)
    sw13 = np.asarray(L0_ffn_sw13, dtype=np.float32)
    sw2 = np.asarray(L0_ffn_sw2, dtype=np.float32)

    w13t = w13.reshape(16, 22, 128, 16, 128).transpose(0, 1, 4, 3, 2).reshape(16, 22, 128, 2048)
    w13t = np.ascontiguousarray(w13t[:, JORD])
    sw13t = sw13.reshape(44, 128, 16, 128).transpose(0, 3, 2, 1).reshape(44, 128, 2048)
    sw13t = np.ascontiguousarray(sw13t[SJORD])
    w2t = np.ascontiguousarray(
        w2.transpose(0, 2, 1).reshape(16, 11, 128, 2, 1024).transpose(0, 3, 1, 2, 4))
    sw2t = np.ascontiguousarray(
        sw2.T.reshape(2, 11, 128, 2, 1024).transpose(0, 3, 1, 2, 4).reshape(4, 11, 128, 1024))
    gwt = np.ascontiguousarray(gw.T.reshape(16, 128, 16))

    identb = np.eye(128, dtype=ml_dtypes.bfloat16)
    dd = np.arange(128)[:, None]
    qq = np.arange(128)[None, :]
    trib = (dd > qq).astype(ml_dtypes.bfloat16)
    onesb = np.ones((128, 128), dtype=ml_dtypes.bfloat16)

    kl = np.arange(128)[None, :]
    lower_eq = (dd <= kl).astype(np.float32)
    bx = np.zeros((2, 8, 128, 256), dtype=np.float32)
    by = np.zeros((2, 8, 128, 256), dtype=np.float32)
    for s in range(8):
        bx[0, s, :, 128:256] = -CBIG * lower_eq
        bx[1, s, :, 0:128] = -CBIG * lower_eq
        by[1, s, :, 128:256] = -CBIG / 128.0

    lts = (dd < kl).astype(ml_dtypes.bfloat16)
    iota = np.broadcast_to(np.arange(512, dtype=np.float16)[None, :],
                           (128, 512)).copy()
    return (w13t, sw13t, w2t, sw2t, gwt, identb, trib, onesb, bx, by,
            lts, iota)


def _row_tiles(role):
    return [(N_LIST[s] - 1 - role) for s in range(8)]


def kernel(L0_attn_x, L0_ffn_gate_w, L0_ffn_w13, L0_ffn_w2, L0_ffn_sw13, L0_ffn_sw2):
    x = np.asarray(L0_attn_x, dtype=np.float32)
    (w13t, sw13t, w2t, sw2t, gwt, identb, trib, onesb, bx, by,
     lts, iota) = _host_prep(
        L0_ffn_gate_w, L0_ffn_w13, L0_ffn_w2, L0_ffn_sw13, L0_ffn_sw2)

    nc = build_program()

    in_maps = []
    for c in range(8):
        b, role = c // 2, c % 2
        tiles = _row_tiles(role)
        rows = np.concatenate([np.arange(t * 128, (t + 1) * 128) for t in tiles])
        in_maps.append({
            "xmy": np.ascontiguousarray(x[b][rows]),
            "xkv": np.ascontiguousarray(x[b]),
            "w13t": w13t,
            "sw13t": sw13t,
            "w2t": w2t,
            "sw2t": sw2t,
            "gwt": gwt,
            "bxm": np.ascontiguousarray(bx[role]),
            "bym": np.ascontiguousarray(by[role]),
            "identb": identb,
            "trib": trib,
            "onesb": onesb,
            "ltsm": lts,
            "iotam": iota,
        })

    res = run_bass_kernel_spmd(nc, in_maps, core_ids=list(range(8)))

    out = np.empty((B, L, H), dtype=np.float32)
    for c in range(8):
        b, role = c // 2, c % 2
        o = res.results[c]["out"]
        for s, t in enumerate(_row_tiles(role)):
            out[b, t * 128:(t + 1) * 128, :] = o[s * 128:(s + 1) * 128, :]
    return out


# version 07
# speedup vs baseline: 1.3167x; 1.0745x over previous
import numpy as np
import ml_dtypes
import concourse.bass as bass
import concourse.mybir as mybir
from concourse.bass_utils import run_bass_kernel_spmd

F32 = mybir.dt.float32
BF16 = mybir.dt.bfloat16
FP16 = mybir.dt.float16
AF = mybir.ActivationFunctionType
ALU = mybir.AluOpType
AX = mybir.AxisListType

B = 4
L = 2048
H = 2048
NH = 16
NT = 8
NE = 16
IC = 11
SCALE = 1.0 / float(np.sqrt(128.0))
CBIG = 1.0e30
N_LIST = [16, 14, 12, 10, 8, 6, 4, 2]
SEQ = [7, 6, 5, 4, 3, 2, 1, 0]

JORD = [x for i in range(11) for x in (i, 11 + i)]
SJORD = [x for i in range(22) for x in (i, 22 + i)]

W13SEQ = [("s", None, k) for k in range(22)] + \
         [("r", e, k) for e in range(NE) for k in range(11)]
W2SEQ = [("s", None, q, hb) for q in range(4) for hb in range(2)] + \
        [("r", e, hh, hb) for e in range(NE) for hh in range(2) for hb in range(2)]


class Plan:
    def __init__(self):
        self.ops = {"tensor": [], "vector": [], "scalar": [], "sync": [], "gpsimd": []}
        self.cnt = {"sd": 0, "gd": 0, "t": 0, "v": 0, "s": 0,
                    "kva": 0, "kvb": 0, "w1a": 0, "w1b": 0, "w2a": 0, "w2b": 0}

    def emit(self, eng, fn, waits=None, inc=None, amt=1):
        w = {k: v for k, v in (waits or {}).items() if v and v > 0}
        self.ops[eng].append((w, fn, inc, amt))
        if inc is not None:
            self.cnt[inc] += amt
        return dict(self.cnt)


LIMIT = {"heads": NH, "ln1": 1, "p3": 1, "experts": 18, "ln2": 1, "astage": 9}


def build_program():
    nheads = LIMIT["heads"]
    nroute = min(NE, LIMIT["experts"])
    do_shared = LIMIT["experts"] >= 17
    A = LIMIT["astage"]
    nc = bass.Bass("TRN2", target_bir_lowering=False)

    d_xmy = nc.dram_tensor("xmy", [1024, H], F32, kind="ExternalInput").ap()
    d_xkv = nc.dram_tensor("xkv", [L, H], F32, kind="ExternalInput").ap()
    d_w13 = nc.dram_tensor("w13t", [NE, 22, 128, 2048], F32, kind="ExternalInput").ap()
    d_sw13 = nc.dram_tensor("sw13t", [44, 128, 2048], F32, kind="ExternalInput").ap()
    d_w2 = nc.dram_tensor("w2t", [NE, 2, IC, 128, 1024], F32, kind="ExternalInput").ap()
    d_sw2 = nc.dram_tensor("sw2t", [4, IC, 128, 1024], F32, kind="ExternalInput").ap()
    d_gw = nc.dram_tensor("gwt", [16, 128, 16], F32, kind="ExternalInput").ap()
    d_bx = nc.dram_tensor("bxm", [8, 128, 256], F32, kind="ExternalInput").ap()
    d_by = nc.dram_tensor("bym", [8, 128, 256], F32, kind="ExternalInput").ap()
    d_id = nc.dram_tensor("identb", [128, 128], BF16, kind="ExternalInput").ap()
    d_tri = nc.dram_tensor("trib", [128, 128], BF16, kind="ExternalInput").ap()
    d_one = nc.dram_tensor("onesb", [128, 128], BF16, kind="ExternalInput").ap()
    d_lts = nc.dram_tensor("ltsm", [128, 128], BF16, kind="ExternalInput").ap()
    d_iota = nc.dram_tensor("iotam", [128, 512], FP16, kind="ExternalInput").ap()
    d_out = nc.dram_tensor("out", [1024, H], F32, kind="ExternalOutput").ap()

    from contextlib import ExitStack
    with ExitStack() as ctx:
        stream = ctx.enter_context(nc.sbuf_tensor("stream", [128, NT * H], F32))
        xt = ctx.enter_context(nc.sbuf_tensor("xt", [128, NT * H], BF16))
        arena = ctx.enter_context(nc.sbuf_tensor("arena", [128, 22528], BF16))
        wb0 = ctx.enter_context(nc.sbuf_tensor("wb0", [128, 4096], BF16))
        wb1 = ctx.enter_context(nc.sbuf_tensor("wb1", [128, 4096], BF16))
        ws0 = ctx.enter_context(nc.sbuf_tensor("ws0", [128, IC * 512], BF16))
        ws1 = ctx.enter_context(nc.sbuf_tensor("ws1", [128, IC * 512], BF16))
        x1r6 = ctx.enter_context(nc.sbuf_tensor("x1r6", [128, 6 * H], BF16))
        ltsb = ctx.enter_context(nc.sbuf_tensor("ltsb", [128, 128], BF16))
        iotab = ctx.enter_context(nc.sbuf_tensor("iotab", [128, 512], FP16))
        sstat = ctx.enter_context(nc.sbuf_tensor("sstat", [128, 24], F32))
        sklb = ctx.enter_context(nc.sbuf_tensor("sklb", [128, 24], BF16))
        sg = ctx.enter_context(nc.sbuf_tensor("sg", [128, 512], F32))
        comb = ctx.enter_context(nc.sbuf_tensor("comb", [128, NT * 16], F32))
        stv = ctx.enter_context(nc.sbuf_tensor("stat", [128, 64], F32))
        gsm = ctx.enter_context(nc.sbuf_tensor("gsm", [128, 16], F32))
        g16 = ctx.enter_context(nc.sbuf_tensor("g16", [128, 16], F32))
        m8 = ctx.enter_context(nc.sbuf_tensor("m8", [128, 8], F32))
        pb0 = ctx.enter_context(nc.psum_tensor("pb0", [128, 512], F32))
        pb1 = ctx.enter_context(nc.psum_tensor("pb1", [128, 512], F32))
        pb2 = ctx.enter_context(nc.psum_tensor("pb2", [128, 512], F32))
        pb3 = ctx.enter_context(nc.psum_tensor("pb3", [128, 512], F32))
        pb4 = ctx.enter_context(nc.psum_tensor("pb4", [128, 512], F32))
        pb5 = ctx.enter_context(nc.psum_tensor("pb5", [128, 512], F32))
        pbtA = ctx.enter_context(nc.psum_tensor("pbtA", [128, 1024], BF16))
        pbtB = ctx.enter_context(nc.psum_tensor("pbtB", [128, 1024], BF16))
        sem_sd = ctx.enter_context(nc.semaphore("sem_sd"))
        sem_gd = ctx.enter_context(nc.semaphore("sem_gd"))
        sem_t = ctx.enter_context(nc.semaphore("sem_t"))
        sem_v = ctx.enter_context(nc.semaphore("sem_v"))
        sem_s = ctx.enter_context(nc.semaphore("sem_s"))
        sem_kva = ctx.enter_context(nc.semaphore("sem_kva"))
        sem_kvb = ctx.enter_context(nc.semaphore("sem_kvb"))
        sem_w1a = ctx.enter_context(nc.semaphore("sem_w1a"))
        sem_w1b = ctx.enter_context(nc.semaphore("sem_w1b"))
        sem_w2a = ctx.enter_context(nc.semaphore("sem_w2a"))
        sem_w2b = ctx.enter_context(nc.semaphore("sem_w2b"))
        sems = {"sd": sem_sd, "gd": sem_gd, "t": sem_t, "v": sem_v, "s": sem_s,
                "kva": sem_kva, "kvb": sem_kvb, "w1a": sem_w1a, "w1b": sem_w1b,
                "w2a": sem_w2a, "w2b": sem_w2b}
        pbs = [pb0, pb1, pb2, pb3, pb4, pb5]
        pbth = [pbtA, pbtB]
        wbufs = [wb0, wb1]
        wsbufs = [ws0, ws1]

        def ar(off, n):
            return arena[:, off:off + n]
        VB = [ar(0, 2048), ar(2048, 2048)]
        KTB = [ar(4096, 2048), ar(6144, 2048)]
        PSB = ar(8192, 2048)
        PTB = ar(10240, 2048)
        QTA = ar(12288, 1024)
        BXS = ar(13312, 2048)
        BYS = ar(15360, 2048)
        IDT = ar(17408, 128)
        TRI = ar(17536, 128)
        ONE = ar(17664, 128)
        GWS = ar(17792, 256)
        TMPB = ar(18048, 2048)
        ABUF = [arena[:, 0:11264], arena[:, 11264:22528]]

        def x1r(tt):
            if tt == 6:
                return arena[:, 13312:15360]
            if tt == 7:
                return arena[:, 15360:17408]
            return x1r6[:, tt * H:(tt + 1) * H]


        P = Plan()
        cp = {}

        for tt in range(NT):
            def ld(E, tt=tt):
                return E.dma_start(out=stream[:, tt * H:(tt + 1) * H],
                                   in_=d_xmy[tt * 128:(tt + 1) * 128, :])
            P.emit("sync", ld, inc="sd", amt=16)
        for dst, srcd in ((IDT, d_id), (TRI, d_tri), (ONE, d_one), (ltsb[:], d_lts),
                          (iotab[:], d_iota)):
            def ld(E, dst=dst, srcd=srcd):
                return E.dma_start(out=dst, in_=srcd)
            P.emit("sync", ld, inc="sd", amt=16)
        def zsr(E):
            return E.memset(sklb[:, 16:24], 0.0)
        P.emit("vector", zsr, inc="v")
        cp["stream"] = P.cnt["sd"]
        cp["consts_sd"] = P.cnt["sd"]

        if LIMIT["heads"] < NH or LIMIT["experts"] < 18:
            for src in (d_xkv[0:128, 0:128], d_w13[0, 0, :, 0:128], d_sw13[0, :, 0:128],
                        d_w2[0, 0, 0, :, 0:128], d_sw2[0, 0, :, 0:128], d_gw[0],
                        d_bx[0, :, 0:128], d_by[0, :, 0:128]):
                def ld(E, src=src):
                    return E.dma_start(out=wb0[:, 0:src.shape[-1]], in_=src)
                P.emit("gpsimd", ld, waits={"gd": P.cnt["gd"]}, inc="gd", amt=16)
                cp["dbg_touch"] = P.cnt["gd"]

        for half in range(2):
            def ld(E, half=half):
                return E.dma_start(
                    out=xt[:, half * 4 * H:(half + 1) * 4 * H].rearrange("p (tt e) -> p tt e", tt=4),
                    in_=d_xmy[half * 512:(half + 1) * 512, :].rearrange("(tt p) e -> p tt e", p=128))
            P.emit("gpsimd", ld, inc="gd", amt=16)
        for dst, src in ((BXS, d_bx), (BYS, d_by), (GWS, d_gw)):
            def ld(E, dst=dst, src=src):
                return E.dma_start(out=dst.rearrange("p (a b) -> p a b", b=src.shape[2]),
                                   in_=src.rearrange("a p b -> p a b"))
            P.emit("gpsimd", ld, inc="gd", amt=16)
        cp["xtb"] = P.cnt["gd"]
        cp["consts_gd"] = P.cnt["gd"]

        for tt in range(NT):
            def sqn(E, tt=tt):
                return E.activation(PSB, stream[:, tt * H:(tt + 1) * H],
                                    AF.Square, accum_out=stv[:, 32 + tt:33 + tt])
            P.emit("scalar", sqn, waits={"sd": cp["stream"],
                                         "s": cp.get(("sqn_prev",), 0)}, inc="s")
            cp[("sqn_prev",)] = P.cnt["s"]
        def nrmsc(E):
            E.drain()
            E.tensor_scalar(stv[:, 32:40], stv[:, 32:40], -1.0 / 128.0, None,
                            op0=ALU.mult)
            return E.drain()
        cp["nrm_done"] = P.emit("vector", nrmsc, waits={"s": P.cnt["s"]},
                                inc="v")["v"]

        w13_emitted = set()

        def emit_w13_load(gj):
            if gj >= len(W13SEQ) or gj in w13_emitted:
                return
            w13_emitted.add(gj)
            kind, e, k = W13SEQ[gj][0], W13SEQ[gj][1], W13SEQ[gj][2]
            buf = gj % 2
            waits = {}
            if "dbg_touch" in cp:
                waits["gd"] = cp["consts_gd"]
            if ("w13used", buf) in cp:
                waits["t"] = cp[("w13used", buf)]
            if kind == "r":
                def ld(E, e=e, k=k, buf=buf):
                    return E.dma_start(
                        out=wbufs[buf][:].rearrange("p (g c) -> p g c", g=2),
                        in_=d_w13[e, 2 * k:2 * k + 2].rearrange("g p c -> p g c"))
            else:
                def ld(E, k=k, buf=buf):
                    return E.dma_start(
                        out=wbufs[buf][:].rearrange("p (g c) -> p g c", g=2),
                        in_=d_sw13[2 * k:2 * k + 2].rearrange("g p c -> p g c"))
            sk = "w1a" if buf == 0 else "w1b"
            cp[("w13ld", gj)] = (sk, P.emit("gpsimd", ld, waits=waits, inc=sk, amt=16)[sk])

        w2_emitted = set()

        def emit_w2_load(wj):
            if wj >= len(W2SEQ) or wj in w2_emitted:
                return
            w2_emitted.add(wj)
            kind, e, hh, hb = W2SEQ[wj][0], W2SEQ[wj][1], W2SEQ[wj][2], W2SEQ[wj][3]
            buf = wj % 2
            waits = {}
            if ("w2used", buf) in cp:
                waits["t"] = cp[("w2used", buf)]
            if kind == "r":
                def ld(E, e=e, hh=hh, hb=hb, buf=buf):
                    return E.dma_start(
                        out=wsbufs[buf][:].rearrange("p (i c) -> p i c", i=IC),
                        in_=d_w2[e, hh][:, :, hb * 512:(hb + 1) * 512]
                        .rearrange("i p c -> p i c"))
            else:
                def ld(E, hh=hh, hb=hb, buf=buf):
                    return E.dma_start(
                        out=wsbufs[buf][:].rearrange("p (i c) -> p i c", i=IC),
                        in_=d_sw2[hh][:, :, hb * 512:(hb + 1) * 512]
                        .rearrange("i p c -> p i c"))
            sk = "w2a" if buf == 0 else "w2b"
            cp[("w2ld", wj)] = (sk, P.emit("gpsimd", ld, waits=waits, inc=sk, amt=16)[sk])

        def emit_vload(h):
            waits = {}
            if h >= 2:
                waits["t"] = cp.get(("pv_done", h - 2), 0)
            def ld(E, h=h):
                return E.dma_start(
                    out=VB[h % 2].rearrange("p (kt d) -> p kt d", kt=16),
                    in_=d_xkv[:, h * 128:(h + 1) * 128].rearrange("(kt p) d -> p kt d", p=128))
            sk = "kva" if h % 2 == 0 else "kvb"
            cp[("vload", h)] = (sk, P.emit("gpsimd", ld, waits=waits, inc=sk, amt=16)[sk])

        if nheads > 0:
            emit_vload(0)
        if nheads > 1:
            emit_vload(1)

        pbt_free = [0, 0]

        for h in range(nheads):
            buf = h % 2
            for g in range(4 if A >= 1 else 0):
                half = g % 2
                vsk, vsv = cp[("vload", h)]
                waits = {vsk: vsv, "v": pbt_free[half], "sd": cp["consts_sd"]}
                def tr(E, g=g, half=half, buf=buf):
                    ins = None
                    for j in range(4):
                        ins = E.transpose(pbth[half][:, j * 128:(j + 1) * 128],
                                          VB[buf][:, (g * 4 + j) * 128:(g * 4 + j + 1) * 128], IDT)
                    return ins
                tcnt = P.emit("tensor", tr, waits=waits, inc="t")["t"]
                def ev(E, g=g, half=half, buf=buf):
                    return E.tensor_copy(KTB[buf][:, g * 512:(g + 1) * 512],
                                         pbth[half][:, 0:512])
                pbt_free[half] = P.emit("vector", ev, waits={"t": tcnt}, inc="v")["v"]
                cp[("ktev", h, g)] = pbt_free[half]
            for g in ((1, 0) if A >= 2 else ()):
                half = g % 2
                waits = {"gd": cp["xtb"], "v": pbt_free[half]}
                if h >= 1:
                    waits["t"] = cp.get(("qk_done", h - 1, 4 * g), 0)
                def tr(E, g=g, half=half, h=h):
                    ins = None
                    for j in range(4):
                        s = g * 4 + j
                        ins = E.transpose(pbth[half][:, j * 128:(j + 1) * 128],
                                          xt[:, s * H + h * 128: s * H + (h + 1) * 128], IDT)
                    return ins
                tcnt = P.emit("tensor", tr, waits=waits, inc="t")["t"]
                def ev(E, g=g, half=half):
                    return E.tensor_copy(QTA[:, g * 512:(g + 1) * 512],
                                         pbth[half][:, 0:512])
                pbt_free[half] = P.emit("vector", ev, waits={"t": tcnt}, inc="v")["v"]
                cp[("qtev", h, g)] = pbt_free[half]
            cp[("qtready", h)] = P.cnt["v"]

            def emit_qk(s, i):
                N = N_LIST[s]
                ncols = N * 128
                kbn = (ncols + 511) // 512
                par = i % 2
                ktw = cp[("ktev", h, (N + 3) // 4 - 1)]
                qtw = cp[("qtev", h, s // 4)]
                waits = {"v": max(ktw, qtw), "gd": cp["consts_gd"]}
                if (h, i) != (0, 0):
                    ph, ps = (h, SEQ[i - 1]) if i > 0 else (h - 1, SEQ[7])
                    waits["s"] = cp.get(("exp_done", ph, ps), 0)
                def qk(E, s=s, ncols=ncols, kbn=kbn, h=h):
                    ins = None
                    for kb in range(kbn):
                        n = min(512, ncols - kb * 512)
                        last = kb == kbn - 1
                        ins = E.matmul(pbs[kb][:, 0:n], QTA[:, s * 128:(s + 1) * 128],
                                       KTB[h % 2][:, kb * 512: kb * 512 + n],
                                       start=True, stop=not last)
                        if last:
                            noff = n - 256
                            ins = E.matmul(pbs[kb][:, noff:noff + 256], TRI,
                                           BXS[:, s * 256:(s + 1) * 256],
                                           start=False, stop=False)
                            ins = E.matmul(pbs[kb][:, noff:noff + 256], ONE,
                                           BYS[:, s * 256:(s + 1) * 256],
                                           start=False, stop=True)
                    return ins
                cp[("qk_done", h, s)] = P.emit("tensor", qk, waits=waits, inc="t")["t"]

            def emit_exp(s, i):
                N = N_LIST[s]
                ncols = N * 128
                kbn = (ncols + 511) // 512
                par = i % 2
                waits = {"v": cp["nrm_done"], "t": cp[("qk_done", h, s)]}
                if (h, i) != (0, 0):
                    ph, ps = (h, SEQ[i - 1]) if i > 0 else (h - 1, SEQ[7])
                    waits["t"] = max(waits["t"], cp.get(("ptr_done", ph, ps), 0))
                def ex(E, s=s, ncols=ncols, kbn=kbn, par=par):
                    ins = None
                    for kb in range(kbn):
                        n = min(512, ncols - kb * 512)
                        ins = E.activation(PSB[:, kb * 512: kb * 512 + n], pbs[kb][:, 0:n],
                                           AF.Exp, bias=stv[:, 32 + s:33 + s],
                                           scale=SCALE,
                                           accum_out=stv[:, 40 + par * 4 + kb:41 + par * 4 + kb])
                    return ins
                cp[("exp_done", h, s)] = P.emit("scalar", ex, waits=waits, inc="s")["s"]
                def lv(E, s=s, kbn=kbn, par=par):
                    E.reduce_sum(stv[:, 50 + par:51 + par],
                                 stv[:, 40 + par * 4: 40 + par * 4 + kbn], axis=AX.X)
                    E.drain()
                    E.reciprocal(stv[:, 52 + par:53 + par], stv[:, 50 + par:51 + par])
                    return E.drain()
                P.emit("vector", lv, waits={"s": cp[("exp_done", h, s)]}, inc=None)

            def emit_pt(s):
                N = N_LIST[s]
                kt = 0
                while kt < N:
                    m = min(4, N - kt)
                    half = (kt // 4) % 2
                    waits = {"s": cp[("exp_done", h, s)], "v": pbt_free[half]}
                    def tr(E, kt=kt, m=m, half=half):
                        ins = None
                        for j in range(m):
                            ins = E.transpose(pbth[half][:, j * 128:(j + 1) * 128],
                                              PSB[:, (kt + j) * 128:(kt + j + 1) * 128], IDT)
                        return ins
                    tcnt = P.emit("tensor", tr, waits=waits, inc="t")["t"]
                    if kt + m >= N:
                        cp[("ptr_done", h, s)] = tcnt
                    def ev(E, kt=kt, m=m, half=half):
                        return E.tensor_copy(PTB[:, kt * 128:(kt + m) * 128],
                                             pbth[half][:, 0:m * 128])
                    pbt_free[half] = P.emit("vector", ev, waits={"t": tcnt}, inc="v")["v"]
                    kt += m
                cp[("ptev", h, s)] = P.cnt["v"]

            def emit_pv(s, i):
                N = N_LIST[s]
                par = i % 2
                waits = {"v": max(cp[("ptev", h, s)], cp.get(("pv_evac_prev",), 0))}
                def pv(E, s=s, N=N, h=h):
                    ins = None
                    for kt in range(N):
                        ins = E.matmul(pb4[:, 0:128], PTB[:, kt * 128:(kt + 1) * 128],
                                       VB[h % 2][:, kt * 128:(kt + 1) * 128],
                                       start=(kt == 0), stop=(kt == N - 1))
                    return ins
                tcnt = P.emit("tensor", pv, waits=waits, inc="t")["t"]
                cp[("pv_done_slot", h, s)] = tcnt
                if A < 7:
                    return
                def evp(E, s=s, h=h, par=par):
                    return E.scalar_tensor_tensor(
                        out=stream[:, s * H + h * 128: s * H + (h + 1) * 128],
                        in0=pb4[:, 0:128], scalar=stv[:, 52 + par:53 + par],
                        in1=stream[:, s * H + h * 128: s * H + (h + 1) * 128],
                        op0=ALU.mult, op1=ALU.add)
                cp[("pv_evac_prev",)] = P.emit(
                    "vector", evp, waits={"t": tcnt, "sd": cp["stream"]}, inc="v")["v"]

            if A >= 3:
                emit_qk(SEQ[0], 0)
            if A >= 4:
                emit_exp(SEQ[0], 0)
            for i in range(1, 8):
                if A >= 3:
                    emit_qk(SEQ[i], i)
                if A >= 5:
                    emit_pt(SEQ[i - 1])
                if A >= 4:
                    emit_exp(SEQ[i], i)
                if A >= 6:
                    emit_pv(SEQ[i - 1], i - 1)
            if A >= 5:
                emit_pt(SEQ[7])
            if A >= 6:
                emit_pv(SEQ[7], 7)
                cp[("pv_done", h)] = cp[("pv_done_slot", h, SEQ[7])]
            if h + 2 < nheads:
                emit_vload(h + 2)
            g0 = 0 if do_shared else 22
            q0 = 0 if do_shared else 8
            if h == 1:
                emit_w13_load(g0)
            elif h == 2:
                emit_w13_load(g0 + 1)
            elif h == 3:
                emit_w2_load(q0)
            elif h == 6:
                emit_w2_load(q0 + 1)

        if LIMIT["experts"] > 0:
            g0 = 0 if do_shared else 22
            q0 = 0 if do_shared else 8
            emit_w13_load(g0)
            emit_w13_load(g0 + 1)
            emit_w2_load(q0)
            emit_w2_load(q0 + 1)

        def emit_ln():
            for tt in range(NT):
                def rs(E, tt=tt):
                    if tt == 0:
                        E.drain()
                    return E.reduce_sum(stv[:, tt:tt + 1], stream[:, tt * H:(tt + 1) * H], axis=AX.X)
                P.emit("vector", rs, waits={"sd": cp["stream"]}, inc="v")
            vafter = P.cnt["v"]
            for tt in range(NT):
                def sq(E, tt=tt):
                    return E.activation(ABUF[0][:, 0:2048], stream[:, tt * H:(tt + 1) * H],
                                        AF.Square, accum_out=stv[:, 8 + tt:9 + tt])
                P.emit("scalar", sq,
                       waits={"v": vafter, "s": cp.get(("sq_prev",), 0)}, inc="s")
                cp[("sq_prev",)] = P.cnt["s"]
            sq_done = P.cnt["s"]
            def fin1(E):
                E.drain()
                E.tensor_scalar(stv[:, 16:24], stv[:, 0:8], 1.0 / H, None, op0=ALU.mult)
                E.tensor_scalar(stv[:, 24:32], stv[:, 8:16], 1.0 / H, None, op0=ALU.mult)
                E.drain()
                E.tensor_mul(stv[:, 32:40], stv[:, 16:24], stv[:, 16:24])
                E.drain()
                E.tensor_sub(stv[:, 24:32], stv[:, 24:32], stv[:, 32:40])
                E.drain()
                return E.tensor_scalar(stv[:, 24:32], stv[:, 24:32], 1e-5, None, op0=ALU.add)
            c1 = P.emit("vector", fin1, waits={"s": sq_done}, inc="v")["v"]
            def fsq(E):
                return E.activation(stv[:, 32:40], stv[:, 24:32], AF.Sqrt)
            c2 = P.emit("scalar", fsq, waits={"v": c1}, inc="s")["s"]
            def fin2(E):
                E.drain()
                E.reciprocal(stv[:, 40:48], stv[:, 32:40])
                E.drain()
                E.tensor_mul(stv[:, 32:40], stv[:, 40:48], stv[:, 40:48])
                E.drain()
                E.tensor_mul(stv[:, 32:40], stv[:, 32:40], stv[:, 24:32])
                E.drain()
                E.tensor_scalar(stv[:, 32:40], stv[:, 32:40], -0.5, 1.5, op0=ALU.mult, op1=ALU.add)
                E.drain()
                E.tensor_mul(stv[:, 24:32], stv[:, 40:48], stv[:, 32:40])
                return E.drain()
            P.emit("vector", fin2, waits={"s": c2}, inc=None)
            for tt in range(NT):
                def ap(E, tt=tt):
                    return E.tensor_scalar(stream[:, tt * H:(tt + 1) * H],
                                           stream[:, tt * H:(tt + 1) * H],
                                           stv[:, 16 + tt:17 + tt], stv[:, 24 + tt:25 + tt],
                                           op0=ALU.subtract, op1=ALU.mult)
                P.emit("vector", ap, inc="v")
            return P.cnt["v"]

        if LIMIT["ln1"]:
            emit_ln()

        p3range = range(NT) if (LIMIT["p3"] or LIMIT["experts"] > 0) else range(0)
        for tt in p3range:
            def cst(E, tt=tt):
                if tt == 0:
                    E.drain()
                return E.tensor_copy(x1r(tt), stream[:, tt * H:(tt + 1) * H])
            ccast = P.emit("vector", cst, waits={"t": cp.get(("x1tr", tt - 1), 0)},
                           inc="v")["v"]
            for g in range(4):
                half = g % 2
                waits = {"v": max(ccast, pbt_free[half]), "gd": cp["xtb"]}
                if tt == 0 and g < 2 and nheads > 0:
                    waits["t"] = cp.get(("pv_done", nheads - 1), 0)
                def tr(E, g=g, half=half, tt=tt):
                    ins = None
                    for j in range(4):
                        hc = g * 4 + j
                        ins = E.transpose(pbth[half][:, j * 128:(j + 1) * 128],
                                          x1r(tt)[:, hc * 128:(hc + 1) * 128], IDT)
                    return ins
                tcnt = P.emit("tensor", tr, waits=waits, inc="t")["t"]
                def ev(E, g=g, half=half, tt=tt):
                    return E.tensor_copy(
                        xt[:].rearrange("p (hc t) -> p hc t", hc=16)[:, g * 4:(g + 1) * 4,
                                                                     tt * 128:(tt + 1) * 128],
                        pbth[half][:, 0:512].rearrange("p (a b) -> p a b", a=4))
                pbt_free[half] = P.emit("vector", ev, waits={"t": tcnt}, inc="v")["v"]
            cp[("x1t", tt)] = P.cnt["v"]
            cp[("x1tr", tt)] = P.cnt["t"]

        if len(p3range) > 0:
            waits = {"v": max(cp[("x1t", NT - 1)], cp.get(("pv_evac_prev",), 0)),
                     "gd": cp["consts_gd"]}
            def gmm(E):
                ins = None
                for hc in range(16):
                    for tt in range(NT):
                        ins = E.matmul(pb4[:, tt * 16:(tt + 1) * 16],
                                       xt[:, hc * 1024 + tt * 128: hc * 1024 + (tt + 1) * 128],
                                       GWS[:, hc * 16:(hc + 1) * 16],
                                       start=(hc == 0 and tt == 0),
                                       stop=(hc == 15 and tt == NT - 1),
                                       skip_group_check=True)
                return ins
            gq = P.emit("tensor", gmm, waits=waits, inc="t")["t"]
        for tt in p3range:
            waits = {"t": gq}
            if tt > 0:
                waits["s"] = cp[("gexp", tt - 1)]
            def gmx(E, tt=tt):
                return E.reduce_max(stv[:, 54:55], pb4[:, tt * 16:(tt + 1) * 16],
                                    axis=AX.X, negate=True)
            gv = P.emit("vector", gmx, waits=waits, inc="v")["v"]
            def gex(E, tt=tt):
                return E.activation(gsm[:], pb4[:, tt * 16:(tt + 1) * 16], AF.Exp,
                                    bias=stv[:, 54:55], accum_out=stv[:, 55:56])
            cp[("gexp", tt)] = P.emit("scalar", gex, waits={"v": gv}, inc="s")["s"]
            def gtk(E, tt=tt):
                E.drain()
                E.reciprocal(stv[:, 56:57], stv[:, 55:56])
                E.drain()
                E.tensor_scalar_mul(gsm[:], gsm[:], stv[:, 56:57])
                E.drain()
                E.max(out=m8[:], in_=gsm[:])
                E.drain()
                E.tensor_scalar(g16[:], gsm[:], m8[:, 5:6], None, op0=ALU.is_ge)
                E.drain()
                E.tensor_mul(g16[:], gsm[:], g16[:])
                E.drain()
                E.reduce_sum(stv[:, 57:58], g16[:], axis=AX.X)
                E.drain()
                E.tensor_scalar(stv[:, 57:58], stv[:, 57:58], 1e-20, None, op0=ALU.add)
                E.drain()
                E.reciprocal(stv[:, 58:59], stv[:, 57:58])
                E.drain()
                return E.tensor_scalar(comb[:, tt * 16:(tt + 1) * 16], g16[:],
                                       stv[:, 58:59], None, op0=ALU.mult)
            P.emit("vector", gtk, waits={"s": cp[("gexp", tt)]}, inc="v")
        def gdr(E):
            return E.drain()
        P.emit("vector", gdr, inc=None)
        comb_done = P.cnt["v"]

        cp[("m1_silu_prev",)] = cp.get(("exp_done", nheads - 1, 7), 0)

        def selb(tt):
            return xt[:, tt * 2048 + 512: tt * 2048 + 1024]
        SELT = arena[:, 18048:22144]

        def abase(half):
            return half * 5632

        def xgv(hc):
            return xt[:, hc * 1024: hc * 1024 + 512]

        def emit_m1s(gj0, nk, ab_off, rhs_fn, xg_key):
            for k in range(nk):
                gj = gj0 + k
                emit_w13_load(gj)
                wbuf = wbufs[gj % 2]
                wsk, wsv = cp[("w13ld", gj)]
                gb, ub = (pb2, pb3) if k % 2 == 0 else (pb0, pb1)
                waits = {wsk: wsv, "s": cp[("m1_silu_prev",)],
                         "v": cp.get(xg_key, cp[("x1t", NT - 1)])}
                def gmmf(E, wbuf=wbuf, gb=gb, rhs_fn=rhs_fn):
                    ins = None
                    for hc in range(16):
                        ins = E.matmul(gb[:, 0:512], wbuf[:, hc * 128:(hc + 1) * 128],
                                       rhs_fn(hc), start=(hc == 0), stop=(hc == 15),
                                       skip_group_check=True)
                    return ins
                gcnt = P.emit("tensor", gmmf, waits=waits, inc="t")["t"]
                uw = {}
                if cp.get(("m1_aev2",), 0):
                    uw["v"] = cp[("m1_aev2",)]
                def ummf(E, wbuf=wbuf, ub=ub, rhs_fn=rhs_fn):
                    ins = None
                    for hc in range(16):
                        ins = E.matmul(ub[:, 0:512],
                                       wbuf[:, 2048 + hc * 128: 2048 + (hc + 1) * 128],
                                       rhs_fn(hc), start=(hc == 0), stop=(hc == 15),
                                       skip_group_check=True)
                    return ins
                ucnt = P.emit("tensor", ummf, waits=uw, inc="t")["t"]
                cp[("w13used", gj % 2)] = ucnt
                def slf(E, gb=gb):
                    return E.activation(sg[:], gb[:, 0:512], AF.Silu)
                scnt = P.emit("scalar", slf,
                              waits={"t": gcnt, "v": cp.get(("m1_aev_prev",), 0)},
                              inc="s")["s"]
                cp[("m1_silu_prev",)] = scnt
                emit_w13_load(gj + 2)
                icol = ab_off + k * 512
                def aevf(E, ub=ub, icol=icol):
                    return E.tensor_mul(arena[:, icol: icol + 512], sg[:], ub[:, 0:512])
                acnt = P.emit("vector", aevf, waits={"t": ucnt, "s": scnt}, inc="v")["v"]
                cp[("m1_aev2",)] = cp.get(("m1_aev_prev",), 0)
                cp[("m1_aev_prev",)] = acnt
            return P.cnt["v"]

        def emit_routing(en):
            def mskf(E, en=en):
                E.drain()
                E.tensor_scalar(sklb[:, 0:8].rearrange("p (a b) -> p a b", b=1),
                                comb[:].rearrange("p (t g) -> p t g", g=16)[:, :, en:en + 1],
                                1e-30, None, op0=ALU.is_ge)
                E.drain()
                E.tensor_tensor_scan(sklb[:, 8:16], sklb[:, 0:8], sklb[:, 16:24],
                                     0.0, op0=ALU.add, op1=ALU.add)
                E.drain()
                E.tensor_sub(sklb[:, 8:16], sklb[:, 8:16], sklb[:, 0:8])
                return E.drain()
            mc = P.emit("vector", mskf,
                        waits={"v": comb_done, "t": cp.get(("posmm", en - 1), 0)},
                        inc="v")["v"]
            def posmm(E):
                E.matmul(pb0[:, 0:8], ltsb[:], sklb[:, 0:8], start=True, stop=False,
                         skip_group_check=True)
                return E.matmul(pb0[:, 0:8], ONE, sklb[:, 8:16], start=False, stop=True,
                                skip_group_check=True)
            pt_ = P.emit("tensor", posmm,
                         waits={"v": mc, "s": cp[("m1_silu_prev",)]}, inc="t")["t"]
            cp[("posmm", en)] = pt_
            def posev(E):
                E.drain()
                E.tensor_scalar(sstat[:, 0:8], pb0[:, 0:8], 1e6, None, op0=ALU.add)
                E.drain()
                E.scalar_tensor_tensor(out=sstat[:, 8:16], in0=sklb[:, 0:8],
                                       scalar=-1e6, in1=sstat[:, 0:8],
                                       op0=ALU.mult, op1=ALU.add)
                return E.drain()
            pv_ = P.emit("vector", posev, waits={"t": pt_}, inc="v")["v"]
            cp[("pos", en)] = pv_
            for tt in range(NT):
                def self_(E, tt=tt):
                    return E.tensor_scalar(selb(tt), iotab[:],
                                           sstat[:, 8 + tt:9 + tt], None, op0=ALU.is_equal)
                P.emit("vector", self_,
                       waits={"v": pv_, "t": cp.get(("gath_done", en - 1), 0)},
                       inc="v")
            cp[("sel", en)] = P.cnt["v"]

        def emit_selt(en):
            for tt in range(NT):
                half = tt % 2
                waits = {"v": max(cp[("sel", en)], pbt_free[half])}
                def trf(E, tt=tt, half=half):
                    ins = None
                    for ct in range(4):
                        ins = E.transpose(pbth[half][:, ct * 128:(ct + 1) * 128],
                                          selb(tt)[:, ct * 128:(ct + 1) * 128],
                                          IDT)
                    return ins
                tcnt = P.emit("tensor", trf, waits=waits, inc="t")["t"]
                def evt(E, tt=tt, half=half):
                    return E.tensor_copy(
                        SELT[:].rearrange("p (ct t c) -> p ct t c", ct=4, t=8)[:, :, tt, :],
                        pbth[half][:, 0:512].rearrange("p (a b) -> p a b", a=4))
                pbt_free[half] = P.emit("vector", evt, waits={"t": tcnt}, inc="v")["v"]
            cp[("selt", en)] = P.cnt["v"]

        def emit_gather(en):
            for hc in range(16):
                gb = pb0 if hc % 2 == 0 else pb1
                wv = max(cp[("sel", en)], cp.get(("xgev", hc % 2), 0))
                if hc < 2:
                    wv = max(wv, cp.get(("scatev", 0), 0), cp.get(("scatev", 1), 0),
                             cp.get(("pos", en), 0))
                def gmf(E, gb=gb, hc=hc):
                    ins = None
                    for tt in range(NT):
                        ins = E.matmul(gb[:, 0:512], x1r(tt)[:, hc * 128:(hc + 1) * 128],
                                       selb(tt),
                                       start=(tt == 0), stop=(tt == NT - 1))
                    return ins
                tcnt = P.emit("tensor", gmf, waits={"v": wv}, inc="t")["t"]
                def xev(E, gb=gb, hc=hc):
                    return E.tensor_copy(xgv(hc), gb[:, 0:512])
                cp[("xgev", hc % 2)] = P.emit("vector", xev, waits={"t": tcnt},
                                              inc="v")["v"]
            cp[("xgd", en)] = P.cnt["v"]
            cp[("gath_done", en)] = P.cnt["t"]

        def emit_m2_shared(th, adep):
            gidx = cp.get(("m2gidx",), 0)
            for q in range(4):
                for hb in range(2):
                    wj = th * 8 + q * 2 + hb
                    emit_w2_load(wj)
                    wsb = wsbufs[wj % 2]
                    w2k, w2v = cp[("w2ld", wj)]
                    for ct in range(4):
                        bk = pb4 if gidx % 2 == 0 else pb5
                        pkey = ("m2ev", gidx % 2)
                        waits = {w2k: w2v, "v": adep, "s": cp[("m1_silu_prev",)]}
                        if pkey in cp:
                            waits["v"] = max(waits["v"], cp[pkey])
                        ihalf = q // 2
                        def mmf(E, bk=bk, wsb=wsb, ct=ct, ihalf=ihalf):
                            ins = None
                            for ici in range(IC):
                                icol = (ihalf * 11 + ici) * 512
                                ins = E.matmul(bk[:, 0:512],
                                               arena[:, icol + ct * 128: icol + (ct + 1) * 128],
                                               wsb[:, ici * 512:(ici + 1) * 512],
                                               start=(ici == 0), stop=(ici == IC - 1),
                                               skip_group_check=True)
                            return ins
                        tcnt = P.emit("tensor", mmf, waits=waits, inc="t")["t"]
                        if ct == 3:
                            cp[("w2used", wj % 2)] = tcnt
                        tt = th * 4 + ct
                        def evf(E, bk=bk, tt=tt, q=q, hb=hb):
                            dst = stream[:, tt * H + (q % 2) * 1024 + hb * 512:
                                         tt * H + (q % 2) * 1024 + (hb + 1) * 512]
                            return E.scalar_tensor_tensor(out=dst, in0=bk[:, 0:512],
                                                          scalar=1.0, in1=dst,
                                                          op0=ALU.mult, op1=ALU.add)
                        c = P.emit("vector", evf,
                                   waits={"t": tcnt,
                                          "v": max(comb_done,
                                                   cp.get(("evf_prev", tt), 0))},
                                   inc="v")["v"]
                        cp[pkey] = c
                        cp[("evf_prev", tt)] = c
                        gidx += 1
                    emit_w2_load(wj + 2)
            cp[("m2gidx",)] = gidx

        def emit_m2_routed(e, wj0, adep):
            ybase = abase(1 - e % 2)
            gidx = cp.get(("m2gidx",), 0)
            for hh in range(2):
                for hb in range(2):
                    wj = wj0 + hh * 2 + hb
                    emit_w2_load(wj)
                    wsb = wsbufs[wj % 2]
                    w2k, w2v = cp[("w2ld", wj)]
                    for ct in range(4):
                        bk = pb4 if gidx % 2 == 0 else pb5
                        pkey = ("m2ev", gidx % 2)
                        waits = {w2k: w2v, "v": adep, "s": cp[("m1_silu_prev",)]}
                        if pkey in cp:
                            waits["v"] = max(waits["v"], cp[pkey])
                        abse = abase(e % 2)
                        def mmf(E, bk=bk, wsb=wsb, ct=ct, abse=abse):
                            ins = None
                            for ici in range(IC):
                                ins = E.matmul(
                                    bk[:, 0:512],
                                    arena[:, abse + ici * 512 + ct * 128:
                                           abse + ici * 512 + (ct + 1) * 128],
                                    wsb[:, ici * 512:(ici + 1) * 512],
                                    start=(ici == 0), stop=(ici == IC - 1),
                                    skip_group_check=True)
                            return ins
                        tcnt = P.emit("tensor", mmf, waits=waits, inc="t")["t"]
                        if ct == 3:
                            cp[("w2used", wj % 2)] = tcnt
                        yc = ybase + ct * 1024 + hb * 512
                        def evf(E, bk=bk, yc=yc):
                            return E.tensor_copy(arena[:, yc: yc + 512], bk[:, 0:512])
                        c = P.emit("vector", evf, waits={"t": tcnt}, inc="v")["v"]
                        cp[pkey] = c
                        gidx += 1
                    emit_w2_load(wj + 2)
                if hh == 0 and e + 1 < nroute:
                    emit_routing(e + 1)
                ev_y = P.cnt["v"]
                for tt in range(NT):
                    sbk = (pb0, pb1) if tt % 2 == 0 else (pb2, pb3)
                    skey = ("scatev", tt % 2)
                    wv = max(ev_y, cp[("selt", e)], cp.get(skey, 0))
                    if tt < 2:
                        wv = max(wv, cp.get(("pos", e + 1), 0))
                    def smm(E, sbk=sbk, tt=tt, ybase=ybase):
                        ins = None
                        for ct in range(4):
                            lhs = SELT[:, (ct * 8 + tt) * 128:(ct * 8 + tt + 1) * 128]
                            for hb in range(2):
                                ins = E.matmul(
                                    sbk[hb][:, 0:512], lhs,
                                    arena[:, ybase + ct * 1024 + hb * 512:
                                           ybase + ct * 1024 + (hb + 1) * 512],
                                    start=(ct == 0), stop=(ct == 3),
                                    skip_group_check=True)
                        return ins
                    tcnt = P.emit("tensor", smm, waits={"v": wv}, inc="t")["t"]
                    def sev(E, sbk=sbk, tt=tt, hh=hh, e=e):
                        ins = None
                        for hb in range(2):
                            dst = stream[:, tt * H + hh * 1024 + hb * 512:
                                         tt * H + hh * 1024 + (hb + 1) * 512]
                            ins = E.scalar_tensor_tensor(
                                out=dst, in0=sbk[hb][:, 0:512],
                                scalar=comb[:, tt * 16 + e: tt * 16 + e + 1],
                                in1=dst, op0=ALU.mult, op1=ALU.add)
                        return ins
                    c = P.emit("vector", sev,
                               waits={"t": tcnt,
                                      "v": max(comb_done,
                                               cp.get(("evf_prev", tt), 0))},
                               inc="v")["v"]
                    cp[skey] = c
                    cp[("evf_prev", tt)] = c
            cp[("m2gidx",)] = gidx

        def emit_m1d(gj0, nk):
            for k in range(nk):
                gj = gj0 + k
                emit_w13_load(gj)
                wbuf = wbufs[gj % 2]
                wsk, wsv = cp[("w13ld", gj)]
                waits = {wsk: wsv, "s": cp[("m1_silu_prev",)],
                         "v": cp[("x1t", NT - 1)]}
                for tb in range(2):
                    gb, ub = (pb2, pb3) if tb == 0 else (pb0, pb1)
                    def gmmf(E, wbuf=wbuf, gb=gb, tb=tb):
                        ins = None
                        for hc in range(16):
                            ins = E.matmul(gb[:, 0:512], wbuf[:, hc * 128:(hc + 1) * 128],
                                           xt[:, hc * 1024 + tb * 512: hc * 1024 + (tb + 1) * 512],
                                           start=(hc == 0), stop=(hc == 15),
                                           skip_group_check=True)
                        return ins
                    gcnt = P.emit("tensor", gmmf, waits=waits if tb == 0 else {},
                                  inc="t")["t"]
                    uw = {}
                    if cp.get(("m1_aev2",), 0):
                        uw["v"] = cp[("m1_aev2",)]
                    def ummf(E, wbuf=wbuf, ub=ub, tb=tb):
                        ins = None
                        for hc in range(16):
                            ins = E.matmul(ub[:, 0:512],
                                           wbuf[:, 2048 + hc * 128: 2048 + (hc + 1) * 128],
                                           xt[:, hc * 1024 + tb * 512: hc * 1024 + (tb + 1) * 512],
                                           start=(hc == 0), stop=(hc == 15),
                                           skip_group_check=True)
                        return ins
                    ucnt = P.emit("tensor", ummf, waits=uw, inc="t")["t"]
                    if tb == 1:
                        cp[("w13used", gj % 2)] = ucnt
                    def slf(E, gb=gb):
                        return E.activation(sg[:], gb[:, 0:512], AF.Silu)
                    scnt = P.emit("scalar", slf,
                                  waits={"t": gcnt, "v": cp.get(("m1_aev_prev",), 0)},
                                  inc="s")["s"]
                    cp[("m1_silu_prev",)] = scnt
                    icol = k * 1024 + tb * 512
                    def aevf(E, ub=ub, icol=icol):
                        return E.tensor_mul(arena[:, icol: icol + 512], sg[:], ub[:, 0:512])
                    acnt = P.emit("vector", aevf, waits={"t": ucnt, "s": scnt},
                                  inc="v")["v"]
                    cp[("m1_aev2",)] = cp.get(("m1_aev_prev",), 0)
                    cp[("m1_aev_prev",)] = acnt
                emit_w13_load(gj + 2)
            return P.cnt["v"]

        def emit_m2_sharedAB(ihalf, adep):
            gidx = cp.get(("m2gidx",), 0)
            for q in (2 * ihalf, 2 * ihalf + 1):
                for hb in range(2):
                    wj = q * 2 + hb
                    emit_w2_load(wj)
                    wsb = wsbufs[wj % 2]
                    w2k, w2v = cp[("w2ld", wj)]
                    for tt in range(NT):
                        bk = pb4 if gidx % 2 == 0 else pb5
                        pkey = ("m2ev", gidx % 2)
                        waits = {w2k: w2v, "v": adep, "s": cp[("m1_silu_prev",)]}
                        if pkey in cp:
                            waits["v"] = max(waits["v"], cp[pkey])
                        def mmf(E, bk=bk, wsb=wsb, tt=tt):
                            ins = None
                            for ici in range(IC):
                                ins = E.matmul(bk[:, 0:512],
                                               arena[:, ici * 1024 + tt * 128: ici * 1024 + (tt + 1) * 128],
                                               wsb[:, ici * 512:(ici + 1) * 512],
                                               start=(ici == 0), stop=(ici == IC - 1),
                                               skip_group_check=True)
                            return ins
                        tcnt = P.emit("tensor", mmf, waits=waits, inc="t")["t"]
                        if tt == NT - 1:
                            cp[("w2used", wj % 2)] = tcnt
                        def evf(E, bk=bk, tt=tt, q=q, hb=hb):
                            dst = stream[:, tt * H + (q % 2) * 1024 + hb * 512:
                                         tt * H + (q % 2) * 1024 + (hb + 1) * 512]
                            return E.scalar_tensor_tensor(out=dst, in0=bk[:, 0:512],
                                                          scalar=1.0, in1=dst,
                                                          op0=ALU.mult, op1=ALU.add)
                        c = P.emit("vector", evf,
                                   waits={"t": tcnt,
                                          "v": max(comb_done,
                                                   cp.get(("evf_prev", tt), 0))},
                                   inc="v")["v"]
                        cp[pkey] = c
                        cp[("evf_prev", tt)] = c
                        gidx += 1
                    emit_w2_load(wj + 2)
            cp[("m2gidx",)] = gidx

        if do_shared:
            adep_a = emit_m1d(0, 11)
            emit_m2_sharedAB(0, adep_a)
            adep_b = emit_m1d(11, 11)
            if nroute > 0:
                emit_routing(0)
            emit_m2_sharedAB(1, adep_b)
        if nroute > 0:
            if ("sel", 0) not in cp:
                emit_routing(0)
            for e in range(nroute):
                emit_selt(e)
                emit_gather(e)
                adep = emit_m1s(22 + e * 11, 11, abase(e % 2), xgv, ("xgd", e))
                emit_m2_routed(e, 8 + e * 4, adep)

        if LIMIT["ln2"]:
            emit_ln()
        st_wait = P.cnt["v"]
        for tt in range(NT):
            def st(E, tt=tt):
                return E.dma_start(out=d_out[tt * 128:(tt + 1) * 128, :],
                                   in_=stream[:, tt * H:(tt + 1) * H])
            P.emit("sync", st, waits={"v": st_wait}, inc="sd", amt=16)
        final_sd = P.cnt["sd"]
        def fw(E):
            return E.wait_ge(sems["sd"], final_sd)
        P.ops["sync"].append(({}, fw, None, 0))

        with nc.Block() as block:
            def replay(E, name):
                lastw = {}
                for waits, fn, inc, amt in P.ops[name]:
                    for sname, val in waits.items():
                        if val > 0 and lastw.get(sname, -1) < val:
                            E.wait_ge(sems[sname], val)
                            lastw[sname] = val
                    ins = fn(E)
                    if inc is not None:
                        ins.then_inc(sems[inc], amt)

            @block.sync
            def _(E):
                replay(E, "sync")

            @block.gpsimd
            def _(E):
                replay(E, "gpsimd")

            @block.tensor
            def _(E):
                replay(E, "tensor")

            @block.vector
            def _(E):
                replay(E, "vector")

            @block.scalar
            def _(E):
                replay(E, "scalar")

    return nc


def _host_prep(L0_ffn_gate_w, L0_ffn_w13, L0_ffn_w2, L0_ffn_sw13, L0_ffn_sw2):
    gw = np.asarray(L0_ffn_gate_w, dtype=np.float32)
    w13 = np.asarray(L0_ffn_w13, dtype=np.float32)
    w2 = np.asarray(L0_ffn_w2, dtype=np.float32)
    sw13 = np.asarray(L0_ffn_sw13, dtype=np.float32)
    sw2 = np.asarray(L0_ffn_sw2, dtype=np.float32)

    w13t = w13.reshape(16, 22, 128, 16, 128).transpose(0, 1, 4, 3, 2).reshape(16, 22, 128, 2048)
    w13t = np.ascontiguousarray(w13t[:, JORD])
    sw13t = sw13.reshape(44, 128, 16, 128).transpose(0, 3, 2, 1).reshape(44, 128, 2048)
    sw13t = np.ascontiguousarray(sw13t[SJORD])
    w2t = np.ascontiguousarray(
        w2.transpose(0, 2, 1).reshape(16, 11, 128, 2, 1024).transpose(0, 3, 1, 2, 4))
    sw2t = np.ascontiguousarray(
        sw2.T.reshape(2, 11, 128, 2, 1024).transpose(0, 3, 1, 2, 4).reshape(4, 11, 128, 1024))
    gwt = np.ascontiguousarray(gw.T.reshape(16, 128, 16))

    identb = np.eye(128, dtype=ml_dtypes.bfloat16)
    dd = np.arange(128)[:, None]
    qq = np.arange(128)[None, :]
    trib = (dd > qq).astype(ml_dtypes.bfloat16)
    onesb = np.ones((128, 128), dtype=ml_dtypes.bfloat16)

    kl = np.arange(128)[None, :]
    lower_eq = (dd <= kl).astype(np.float32)
    bx = np.zeros((2, 8, 128, 256), dtype=np.float32)
    by = np.zeros((2, 8, 128, 256), dtype=np.float32)
    for s in range(8):
        bx[0, s, :, 128:256] = -CBIG * lower_eq
        bx[1, s, :, 0:128] = -CBIG * lower_eq
        by[1, s, :, 128:256] = -CBIG / 128.0

    lts = (dd < kl).astype(ml_dtypes.bfloat16)
    iota = np.broadcast_to(np.arange(512, dtype=np.float16)[None, :],
                           (128, 512)).copy()
    return (w13t, sw13t, w2t, sw2t, gwt, identb, trib, onesb, bx, by,
            lts, iota)


def _row_tiles(role):
    return [(N_LIST[s] - 1 - role) for s in range(8)]


def kernel(L0_attn_x, L0_ffn_gate_w, L0_ffn_w13, L0_ffn_w2, L0_ffn_sw13, L0_ffn_sw2):
    x = np.asarray(L0_attn_x, dtype=np.float32)
    (w13t, sw13t, w2t, sw2t, gwt, identb, trib, onesb, bx, by,
     lts, iota) = _host_prep(
        L0_ffn_gate_w, L0_ffn_w13, L0_ffn_w2, L0_ffn_sw13, L0_ffn_sw2)

    nc = build_program()

    in_maps = []
    for c in range(8):
        b, role = c // 2, c % 2
        tiles = _row_tiles(role)
        rows = np.concatenate([np.arange(t * 128, (t + 1) * 128) for t in tiles])
        in_maps.append({
            "xmy": np.ascontiguousarray(x[b][rows]),
            "xkv": np.ascontiguousarray(x[b]),
            "w13t": w13t,
            "sw13t": sw13t,
            "w2t": w2t,
            "sw2t": sw2t,
            "gwt": gwt,
            "bxm": np.ascontiguousarray(bx[role]),
            "bym": np.ascontiguousarray(by[role]),
            "identb": identb,
            "trib": trib,
            "onesb": onesb,
            "ltsm": lts,
            "iotam": iota,
        })

    res = run_bass_kernel_spmd(nc, in_maps, core_ids=list(range(8)))

    out = np.empty((B, L, H), dtype=np.float32)
    for c in range(8):
        b, role = c // 2, c % 2
        o = res.results[c]["out"]
        for s, t in enumerate(_row_tiles(role)):
            out[b, t * 128:(t + 1) * 128, :] = o[s * 128:(s + 1) * 128, :]
    return out


# version 08
# speedup vs baseline: 1.3514x; 1.0263x over previous
import numpy as np
import ml_dtypes
import concourse.bass as bass
import concourse.mybir as mybir
from concourse.bass_utils import run_bass_kernel_spmd

F32 = mybir.dt.float32
BF16 = mybir.dt.bfloat16
FP16 = mybir.dt.float16
AF = mybir.ActivationFunctionType
ALU = mybir.AluOpType
AX = mybir.AxisListType

B = 4
L = 2048
H = 2048
NH = 16
NT = 8
NE = 16
IC = 11
SCALE = 1.0 / float(np.sqrt(128.0))
CBIG = 1.0e30
N_LIST = [16, 14, 12, 10, 8, 6, 4, 2]
SEQ = [7, 6, 5, 4, 3, 2, 1, 0]

JORD = [x for i in range(11) for x in (i, 11 + i)]
SJORD = [x for i in range(22) for x in (i, 22 + i)]

W13SEQ = [("s", None, k) for k in range(22)] + \
         [("r", e, k) for e in range(NE) for k in range(11)]
W2SEQ = [("s", None, q, hb) for q in range(4) for hb in range(2)] + \
        [("r", e, hh, hb) for e in range(NE) for hh in range(2) for hb in range(2)]


class Plan:
    def __init__(self):
        self.ops = {"tensor": [], "vector": [], "scalar": [], "sync": [], "gpsimd": []}
        self.cnt = {"sd": 0, "gd": 0, "t": 0, "v": 0, "s": 0,
                    "kva": 0, "kvb": 0, "w1a": 0, "w1b": 0, "w2a": 0, "w2b": 0,
                    "w1c": 0}

    def emit(self, eng, fn, waits=None, inc=None, amt=1):
        w = {k: v for k, v in (waits or {}).items() if v and v > 0}
        self.ops[eng].append((w, fn, inc, amt))
        if inc is not None:
            self.cnt[inc] += amt
        return dict(self.cnt)


LIMIT = {"heads": NH, "ln1": 1, "p3": 1, "experts": 18, "ln2": 1, "astage": 9}


def build_program():
    nheads = LIMIT["heads"]
    nroute = min(NE, LIMIT["experts"])
    do_shared = LIMIT["experts"] >= 17
    A = LIMIT["astage"]
    nc = bass.Bass("TRN2", target_bir_lowering=False)

    d_xmy = nc.dram_tensor("xmy", [1024, H], F32, kind="ExternalInput").ap()
    d_xkv = nc.dram_tensor("xkv", [L, H], F32, kind="ExternalInput").ap()
    d_w13 = nc.dram_tensor("w13t", [NE, 22, 128, 2048], F32, kind="ExternalInput").ap()
    d_sw13 = nc.dram_tensor("sw13t", [44, 128, 2048], F32, kind="ExternalInput").ap()
    d_w2 = nc.dram_tensor("w2t", [NE, 2, IC, 128, 1024], F32, kind="ExternalInput").ap()
    d_sw2 = nc.dram_tensor("sw2t", [4, IC, 128, 1024], F32, kind="ExternalInput").ap()
    d_gw = nc.dram_tensor("gwt", [16, 128, 16], F32, kind="ExternalInput").ap()
    d_bx = nc.dram_tensor("bxm", [8, 128, 256], F32, kind="ExternalInput").ap()
    d_by = nc.dram_tensor("bym", [8, 128, 256], F32, kind="ExternalInput").ap()
    d_id = nc.dram_tensor("identb", [128, 128], BF16, kind="ExternalInput").ap()
    d_tri = nc.dram_tensor("trib", [128, 128], BF16, kind="ExternalInput").ap()
    d_one = nc.dram_tensor("onesb", [128, 128], BF16, kind="ExternalInput").ap()
    d_lts = nc.dram_tensor("ltsm", [128, 128], BF16, kind="ExternalInput").ap()
    d_iota = nc.dram_tensor("iotam", [128, 512], FP16, kind="ExternalInput").ap()
    d_out = nc.dram_tensor("out", [1024, H], F32, kind="ExternalOutput").ap()

    from contextlib import ExitStack
    with ExitStack() as ctx:
        stream = ctx.enter_context(nc.sbuf_tensor("stream", [128, NT * H], F32))
        xt = ctx.enter_context(nc.sbuf_tensor("xt", [128, NT * H], BF16))
        arena = ctx.enter_context(nc.sbuf_tensor("arena", [128, 22528], BF16))
        wb0 = ctx.enter_context(nc.sbuf_tensor("wb0", [128, 4096], BF16))
        wb1 = ctx.enter_context(nc.sbuf_tensor("wb1", [128, 4096], BF16))
        ws0 = ctx.enter_context(nc.sbuf_tensor("ws0", [128, IC * 512], BF16))
        ws1 = ctx.enter_context(nc.sbuf_tensor("ws1", [128, IC * 512], BF16))
        x1r6 = ctx.enter_context(nc.sbuf_tensor("x1r6", [128, 6 * H], BF16))
        ltsb = ctx.enter_context(nc.sbuf_tensor("ltsb", [128, 128], BF16))
        iotab = ctx.enter_context(nc.sbuf_tensor("iotab", [128, 512], FP16))
        sstat = ctx.enter_context(nc.sbuf_tensor("sstat", [128, 24], F32))
        sklb = ctx.enter_context(nc.sbuf_tensor("sklb", [128, 24], BF16))
        sg = ctx.enter_context(nc.sbuf_tensor("sg", [128, 512], F32))
        comb = ctx.enter_context(nc.sbuf_tensor("comb", [128, NT * 16], F32))
        stv = ctx.enter_context(nc.sbuf_tensor("stat", [128, 64], F32))
        gsm = ctx.enter_context(nc.sbuf_tensor("gsm", [128, 16], F32))
        g16 = ctx.enter_context(nc.sbuf_tensor("g16", [128, 16], F32))
        m8 = ctx.enter_context(nc.sbuf_tensor("m8", [128, 8], F32))
        pb0 = ctx.enter_context(nc.psum_tensor("pb0", [128, 512], F32))
        pb1 = ctx.enter_context(nc.psum_tensor("pb1", [128, 512], F32))
        pb2 = ctx.enter_context(nc.psum_tensor("pb2", [128, 512], F32))
        pb3 = ctx.enter_context(nc.psum_tensor("pb3", [128, 512], F32))
        pb4 = ctx.enter_context(nc.psum_tensor("pb4", [128, 512], F32))
        pb5 = ctx.enter_context(nc.psum_tensor("pb5", [128, 512], F32))
        pbtA = ctx.enter_context(nc.psum_tensor("pbtA", [128, 1024], BF16))
        pbtB = ctx.enter_context(nc.psum_tensor("pbtB", [128, 1024], BF16))
        sem_sd = ctx.enter_context(nc.semaphore("sem_sd"))
        sem_gd = ctx.enter_context(nc.semaphore("sem_gd"))
        sem_t = ctx.enter_context(nc.semaphore("sem_t"))
        sem_v = ctx.enter_context(nc.semaphore("sem_v"))
        sem_s = ctx.enter_context(nc.semaphore("sem_s"))
        sem_kva = ctx.enter_context(nc.semaphore("sem_kva"))
        sem_kvb = ctx.enter_context(nc.semaphore("sem_kvb"))
        sem_w1a = ctx.enter_context(nc.semaphore("sem_w1a"))
        sem_w1b = ctx.enter_context(nc.semaphore("sem_w1b"))
        sem_w2a = ctx.enter_context(nc.semaphore("sem_w2a"))
        sem_w2b = ctx.enter_context(nc.semaphore("sem_w2b"))
        sem_w1c = ctx.enter_context(nc.semaphore("sem_w1c"))
        sems = {"sd": sem_sd, "gd": sem_gd, "t": sem_t, "v": sem_v, "s": sem_s,
                "kva": sem_kva, "kvb": sem_kvb, "w1a": sem_w1a, "w1b": sem_w1b,
                "w2a": sem_w2a, "w2b": sem_w2b, "w1c": sem_w1c}
        pbs = [pb0, pb1, pb2, pb3, pb4, pb5]
        pbth = [pbtA, pbtB]
        wbufs = [wb0, wb1]
        wsbufs = [ws0, ws1]

        def ar(off, n):
            return arena[:, off:off + n]
        VB = [ar(0, 2048), ar(2048, 2048)]
        KTB = [ar(4096, 2048), ar(6144, 2048)]
        PSB = ar(8192, 2048)
        PTB = ar(10240, 2048)
        QTA = ar(12288, 1024)
        BXS = ar(13312, 2048)
        BYS = ar(15360, 2048)
        IDT = ar(17408, 128)
        TRI = ar(17536, 128)
        ONE = ar(17664, 128)
        GWS = ar(17792, 256)
        TMPB = ar(18048, 2048)
        ABUF = [arena[:, 0:11264], arena[:, 11264:22528]]

        def x1r(tt):
            if tt == 6:
                return arena[:, 13312:15360]
            if tt == 7:
                return arena[:, 15360:17408]
            return x1r6[:, tt * H:(tt + 1) * H]


        P = Plan()
        cp = {}

        for tt in range(NT):
            def ld(E, tt=tt):
                return E.dma_start(out=stream[:, tt * H:(tt + 1) * H],
                                   in_=d_xmy[tt * 128:(tt + 1) * 128, :])
            P.emit("sync", ld, inc="sd", amt=16)
        for dst, srcd in ((IDT, d_id), (TRI, d_tri), (ONE, d_one), (ltsb[:], d_lts),
                          (iotab[:], d_iota)):
            def ld(E, dst=dst, srcd=srcd):
                return E.dma_start(out=dst, in_=srcd)
            P.emit("sync", ld, inc="sd", amt=16)
        def zsr(E):
            return E.memset(sklb[:, 16:24], 0.0)
        P.emit("vector", zsr, inc="v")
        cp["stream"] = P.cnt["sd"]
        cp["consts_sd"] = P.cnt["sd"]

        if LIMIT["heads"] < NH or LIMIT["experts"] < 18:
            for src in (d_xkv[0:128, 0:128], d_w13[0, 0, :, 0:128], d_sw13[0, :, 0:128],
                        d_w2[0, 0, 0, :, 0:128], d_sw2[0, 0, :, 0:128], d_gw[0],
                        d_bx[0, :, 0:128], d_by[0, :, 0:128]):
                def ld(E, src=src):
                    return E.dma_start(out=wb0[:, 0:src.shape[-1]], in_=src)
                P.emit("gpsimd", ld, waits={"gd": P.cnt["gd"]}, inc="gd", amt=16)
                cp["dbg_touch"] = P.cnt["gd"]

        for half in range(2):
            def ld(E, half=half):
                return E.dma_start(
                    out=xt[:, half * 4 * H:(half + 1) * 4 * H].rearrange("p (tt e) -> p tt e", tt=4),
                    in_=d_xmy[half * 512:(half + 1) * 512, :].rearrange("(tt p) e -> p tt e", p=128))
            P.emit("gpsimd", ld, inc="gd", amt=16)
        for dst, src in ((BXS, d_bx), (BYS, d_by), (GWS, d_gw)):
            def ld(E, dst=dst, src=src):
                return E.dma_start(out=dst.rearrange("p (a b) -> p a b", b=src.shape[2]),
                                   in_=src.rearrange("a p b -> p a b"))
            P.emit("gpsimd", ld, inc="gd", amt=16)
        cp["xtb"] = P.cnt["gd"]
        cp["consts_gd"] = P.cnt["gd"]

        for tt in range(NT):
            def sqn(E, tt=tt):
                return E.activation(PSB, stream[:, tt * H:(tt + 1) * H],
                                    AF.Square, accum_out=stv[:, 32 + tt:33 + tt])
            P.emit("scalar", sqn, waits={"sd": cp["stream"],
                                         "s": cp.get(("sqn_prev",), 0)}, inc="s")
            cp[("sqn_prev",)] = P.cnt["s"]
        def nrmsc(E):
            E.drain()
            E.tensor_scalar(stv[:, 32:40], stv[:, 32:40], -1.0 / 128.0, None,
                            op0=ALU.mult)
            return E.drain()
        cp["nrm_done"] = P.emit("vector", nrmsc, waits={"s": P.cnt["s"]},
                                inc="v")["v"]

        w13_emitted = set()

        def w13buf(gj):
            return gj % 2 if gj < 22 else (gj - 22) % 3

        def wb2_out():
            return xt[:].rearrange("p (r q c) -> p r q c", r=8, q=4)[:, :, 3, :]

        def wsl(gj, v):
            buf = w13buf(gj)
            if buf < 2:
                return wbufs[buf][:, v:v + 128]
            r, off = v // 512, v % 512
            base = r * 2048 + 1536 + off
            return xt[:, base:base + 128]

        W1SK = {0: "w1a", 1: "w1b", 2: "w1c"}

        def emit_w13_load(gj):
            if gj >= len(W13SEQ) or gj in w13_emitted:
                return
            w13_emitted.add(gj)
            kind, e, k = W13SEQ[gj][0], W13SEQ[gj][1], W13SEQ[gj][2]
            buf = w13buf(gj)
            waits = {}
            if "dbg_touch" in cp:
                waits["gd"] = cp["consts_gd"]
            if ("w13used", buf) in cp:
                waits["t"] = cp[("w13used", buf)]
            if buf == 2 and ("w13used", 2) not in cp:
                waits["t"] = cp.get(("shared_m1_done",), 0)
            sk = W1SK[buf]
            if buf < 2:
                def ld(E, e=e, k=k, kind=kind, buf=buf):
                    srcd = (d_w13[e, 2 * k:2 * k + 2] if kind == "r"
                            else d_sw13[2 * k:2 * k + 2])
                    return E.dma_start(
                        out=wbufs[buf][:].rearrange("p (g c) -> p g c", g=2),
                        in_=srcd.rearrange("g p c -> p g c"))
                cp[("w13ld", gj)] = (sk, P.emit("gpsimd", ld, waits=waits,
                                                inc=sk, amt=16)[sk])
            else:
                for g in range(2):
                    def ld(E, e=e, k=k, kind=kind, g=g):
                        srcd = (d_w13[e, 2 * k + g] if kind == "r"
                                else d_sw13[2 * k + g])
                        return E.dma_start(
                            out=wb2_out()[:, 4 * g:4 * g + 4, :],
                            in_=srcd.rearrange("p (s c) -> p s c", s=4))
                    v = P.emit("gpsimd", ld, waits=waits if g == 0 else {},
                               inc=sk, amt=16)[sk]
                cp[("w13ld", gj)] = (sk, v)

        w2_emitted = set()

        def emit_w2_load(wj):
            if wj >= len(W2SEQ) or wj in w2_emitted:
                return
            w2_emitted.add(wj)
            kind, e, hh, hb = W2SEQ[wj][0], W2SEQ[wj][1], W2SEQ[wj][2], W2SEQ[wj][3]
            buf = wj % 2
            waits = {}
            if ("w2used", buf) in cp:
                waits["t"] = cp[("w2used", buf)]
            if kind == "r":
                def ld(E, e=e, hh=hh, hb=hb, buf=buf):
                    return E.dma_start(
                        out=wsbufs[buf][:].rearrange("p (i c) -> p i c", i=IC),
                        in_=d_w2[e, hh][:, :, hb * 512:(hb + 1) * 512]
                        .rearrange("i p c -> p i c"))
            else:
                def ld(E, hh=hh, hb=hb, buf=buf):
                    return E.dma_start(
                        out=wsbufs[buf][:].rearrange("p (i c) -> p i c", i=IC),
                        in_=d_sw2[hh][:, :, hb * 512:(hb + 1) * 512]
                        .rearrange("i p c -> p i c"))
            sk = "w2a" if buf == 0 else "w2b"
            cp[("w2ld", wj)] = (sk, P.emit("gpsimd", ld, waits=waits, inc=sk, amt=16)[sk])

        def emit_vload(h):
            waits = {}
            if h >= 2:
                waits["t"] = cp.get(("pv_done", h - 2), 0)
            def ld(E, h=h):
                return E.dma_start(
                    out=VB[h % 2].rearrange("p (kt d) -> p kt d", kt=16),
                    in_=d_xkv[:, h * 128:(h + 1) * 128].rearrange("(kt p) d -> p kt d", p=128))
            sk = "kva" if h % 2 == 0 else "kvb"
            cp[("vload", h)] = (sk, P.emit("gpsimd", ld, waits=waits, inc=sk, amt=16)[sk])

        if nheads > 0:
            emit_vload(0)
        if nheads > 1:
            emit_vload(1)

        pbt_free = [0, 0]

        for h in range(nheads):
            buf = h % 2
            for g in range(4 if A >= 1 else 0):
                half = g % 2
                vsk, vsv = cp[("vload", h)]
                waits = {vsk: vsv, "v": pbt_free[half], "sd": cp["consts_sd"]}
                def tr(E, g=g, half=half, buf=buf):
                    ins = None
                    for j in range(4):
                        ins = E.transpose(pbth[half][:, j * 128:(j + 1) * 128],
                                          VB[buf][:, (g * 4 + j) * 128:(g * 4 + j + 1) * 128], IDT)
                    return ins
                tcnt = P.emit("tensor", tr, waits=waits, inc="t")["t"]
                def ev(E, g=g, half=half, buf=buf):
                    return E.tensor_copy(KTB[buf][:, g * 512:(g + 1) * 512],
                                         pbth[half][:, 0:512])
                pbt_free[half] = P.emit("vector", ev, waits={"t": tcnt}, inc="v")["v"]
                cp[("ktev", h, g)] = pbt_free[half]
            for g in ((1, 0) if A >= 2 else ()):
                half = g % 2
                waits = {"gd": cp["xtb"], "v": pbt_free[half]}
                if h >= 1:
                    waits["t"] = cp.get(("qk_done", h - 1, 4 * g), 0)
                def tr(E, g=g, half=half, h=h):
                    ins = None
                    for j in range(4):
                        s = g * 4 + j
                        ins = E.transpose(pbth[half][:, j * 128:(j + 1) * 128],
                                          xt[:, s * H + h * 128: s * H + (h + 1) * 128], IDT)
                    return ins
                tcnt = P.emit("tensor", tr, waits=waits, inc="t")["t"]
                def ev(E, g=g, half=half):
                    return E.tensor_copy(QTA[:, g * 512:(g + 1) * 512],
                                         pbth[half][:, 0:512])
                pbt_free[half] = P.emit("vector", ev, waits={"t": tcnt}, inc="v")["v"]
                cp[("qtev", h, g)] = pbt_free[half]
            cp[("qtready", h)] = P.cnt["v"]

            def emit_qk(s, i):
                N = N_LIST[s]
                ncols = N * 128
                kbn = (ncols + 511) // 512
                par = i % 2
                ktw = cp[("ktev", h, (N + 3) // 4 - 1)]
                qtw = cp[("qtev", h, s // 4)]
                waits = {"v": max(ktw, qtw), "gd": cp["consts_gd"]}
                if (h, i) != (0, 0):
                    ph, ps = (h, SEQ[i - 1]) if i > 0 else (h - 1, SEQ[7])
                    waits["s"] = cp.get(("exp_done", ph, ps), 0)
                def qk(E, s=s, ncols=ncols, kbn=kbn, h=h):
                    ins = None
                    for kb in range(kbn):
                        n = min(512, ncols - kb * 512)
                        last = kb == kbn - 1
                        ins = E.matmul(pbs[kb][:, 0:n], QTA[:, s * 128:(s + 1) * 128],
                                       KTB[h % 2][:, kb * 512: kb * 512 + n],
                                       start=True, stop=not last)
                        if last:
                            noff = n - 256
                            ins = E.matmul(pbs[kb][:, noff:noff + 256], TRI,
                                           BXS[:, s * 256:(s + 1) * 256],
                                           start=False, stop=False)
                            ins = E.matmul(pbs[kb][:, noff:noff + 256], ONE,
                                           BYS[:, s * 256:(s + 1) * 256],
                                           start=False, stop=True)
                    return ins
                cp[("qk_done", h, s)] = P.emit("tensor", qk, waits=waits, inc="t")["t"]

            def emit_exp(s, i):
                N = N_LIST[s]
                ncols = N * 128
                kbn = (ncols + 511) // 512
                par = i % 2
                waits = {"v": cp["nrm_done"], "t": cp[("qk_done", h, s)]}
                if (h, i) != (0, 0):
                    ph, ps = (h, SEQ[i - 1]) if i > 0 else (h - 1, SEQ[7])
                    waits["t"] = max(waits["t"], cp.get(("ptr_done", ph, ps), 0))
                def ex(E, s=s, ncols=ncols, kbn=kbn, par=par):
                    ins = None
                    for kb in range(kbn):
                        n = min(512, ncols - kb * 512)
                        ins = E.activation(PSB[:, kb * 512: kb * 512 + n], pbs[kb][:, 0:n],
                                           AF.Exp, bias=stv[:, 32 + s:33 + s],
                                           scale=SCALE,
                                           accum_out=stv[:, 40 + par * 4 + kb:41 + par * 4 + kb])
                    return ins
                cp[("exp_done", h, s)] = P.emit("scalar", ex, waits=waits, inc="s")["s"]
                def lv(E, s=s, kbn=kbn, par=par):
                    E.reduce_sum(stv[:, 50 + par:51 + par],
                                 stv[:, 40 + par * 4: 40 + par * 4 + kbn], axis=AX.X)
                    E.drain()
                    E.reciprocal(stv[:, 52 + par:53 + par], stv[:, 50 + par:51 + par])
                    return E.drain()
                P.emit("vector", lv, waits={"s": cp[("exp_done", h, s)]}, inc=None)

            def emit_pt(s):
                N = N_LIST[s]
                kt = 0
                while kt < N:
                    m = min(4, N - kt)
                    half = (kt // 4) % 2
                    waits = {"s": cp[("exp_done", h, s)], "v": pbt_free[half]}
                    def tr(E, kt=kt, m=m, half=half):
                        ins = None
                        for j in range(m):
                            ins = E.transpose(pbth[half][:, j * 128:(j + 1) * 128],
                                              PSB[:, (kt + j) * 128:(kt + j + 1) * 128], IDT)
                        return ins
                    tcnt = P.emit("tensor", tr, waits=waits, inc="t")["t"]
                    if kt + m >= N:
                        cp[("ptr_done", h, s)] = tcnt
                    def ev(E, kt=kt, m=m, half=half):
                        return E.tensor_copy(PTB[:, kt * 128:(kt + m) * 128],
                                             pbth[half][:, 0:m * 128])
                    pbt_free[half] = P.emit("vector", ev, waits={"t": tcnt}, inc="v")["v"]
                    kt += m
                cp[("ptev", h, s)] = P.cnt["v"]

            def emit_pv(s, i):
                N = N_LIST[s]
                par = i % 2
                waits = {"v": max(cp[("ptev", h, s)], cp.get(("pv_evac_prev",), 0))}
                def pv(E, s=s, N=N, h=h):
                    ins = None
                    for kt in range(N):
                        ins = E.matmul(pb4[:, 0:128], PTB[:, kt * 128:(kt + 1) * 128],
                                       VB[h % 2][:, kt * 128:(kt + 1) * 128],
                                       start=(kt == 0), stop=(kt == N - 1))
                    return ins
                tcnt = P.emit("tensor", pv, waits=waits, inc="t")["t"]
                cp[("pv_done_slot", h, s)] = tcnt
                if A < 7:
                    return
                def evp(E, s=s, h=h, par=par):
                    return E.scalar_tensor_tensor(
                        out=stream[:, s * H + h * 128: s * H + (h + 1) * 128],
                        in0=pb4[:, 0:128], scalar=stv[:, 52 + par:53 + par],
                        in1=stream[:, s * H + h * 128: s * H + (h + 1) * 128],
                        op0=ALU.mult, op1=ALU.add)
                cp[("pv_evac_prev",)] = P.emit(
                    "vector", evp, waits={"t": tcnt, "sd": cp["stream"]}, inc="v")["v"]

            if A >= 3:
                emit_qk(SEQ[0], 0)
            if A >= 4:
                emit_exp(SEQ[0], 0)
            for i in range(1, 8):
                if A >= 3:
                    emit_qk(SEQ[i], i)
                if A >= 5:
                    emit_pt(SEQ[i - 1])
                if A >= 4:
                    emit_exp(SEQ[i], i)
                if A >= 6:
                    emit_pv(SEQ[i - 1], i - 1)
            if A >= 5:
                emit_pt(SEQ[7])
            if A >= 6:
                emit_pv(SEQ[7], 7)
                cp[("pv_done", h)] = cp[("pv_done_slot", h, SEQ[7])]
            if h + 2 < nheads:
                emit_vload(h + 2)
            g0 = 0 if do_shared else 22
            q0 = 0 if do_shared else 8
            if h == 1:
                emit_w13_load(g0)
            elif h == 2:
                emit_w13_load(g0 + 1)
            elif h == 3:
                emit_w2_load(q0)
            elif h == 6:
                emit_w2_load(q0 + 1)

        if LIMIT["experts"] > 0:
            g0 = 0 if do_shared else 22
            q0 = 0 if do_shared else 8
            emit_w13_load(g0)
            emit_w13_load(g0 + 1)
            emit_w2_load(q0)
            emit_w2_load(q0 + 1)

        def emit_ln():
            for tt in range(NT):
                def rs(E, tt=tt):
                    if tt == 0:
                        E.drain()
                    return E.reduce_sum(stv[:, tt:tt + 1], stream[:, tt * H:(tt + 1) * H], axis=AX.X)
                P.emit("vector", rs, waits={"sd": cp["stream"]}, inc="v")
            vafter = P.cnt["v"]
            for tt in range(NT):
                def sq(E, tt=tt):
                    return E.activation(ABUF[0][:, 0:2048], stream[:, tt * H:(tt + 1) * H],
                                        AF.Square, accum_out=stv[:, 8 + tt:9 + tt])
                P.emit("scalar", sq,
                       waits={"v": vafter, "s": cp.get(("sq_prev",), 0)}, inc="s")
                cp[("sq_prev",)] = P.cnt["s"]
            sq_done = P.cnt["s"]
            def fin1(E):
                E.drain()
                E.tensor_scalar(stv[:, 16:24], stv[:, 0:8], 1.0 / H, None, op0=ALU.mult)
                E.tensor_scalar(stv[:, 24:32], stv[:, 8:16], 1.0 / H, None, op0=ALU.mult)
                E.drain()
                E.tensor_mul(stv[:, 32:40], stv[:, 16:24], stv[:, 16:24])
                E.drain()
                E.tensor_sub(stv[:, 24:32], stv[:, 24:32], stv[:, 32:40])
                E.drain()
                return E.tensor_scalar(stv[:, 24:32], stv[:, 24:32], 1e-5, None, op0=ALU.add)
            c1 = P.emit("vector", fin1, waits={"s": sq_done}, inc="v")["v"]
            def fsq(E):
                return E.activation(stv[:, 32:40], stv[:, 24:32], AF.Sqrt)
            c2 = P.emit("scalar", fsq, waits={"v": c1}, inc="s")["s"]
            def fin2(E):
                E.drain()
                E.reciprocal(stv[:, 40:48], stv[:, 32:40])
                E.drain()
                E.tensor_mul(stv[:, 32:40], stv[:, 40:48], stv[:, 40:48])
                E.drain()
                E.tensor_mul(stv[:, 32:40], stv[:, 32:40], stv[:, 24:32])
                E.drain()
                E.tensor_scalar(stv[:, 32:40], stv[:, 32:40], -0.5, 1.5, op0=ALU.mult, op1=ALU.add)
                E.drain()
                E.tensor_mul(stv[:, 24:32], stv[:, 40:48], stv[:, 32:40])
                return E.drain()
            P.emit("vector", fin2, waits={"s": c2}, inc=None)
            for tt in range(NT):
                def ap(E, tt=tt):
                    return E.tensor_scalar(stream[:, tt * H:(tt + 1) * H],
                                           stream[:, tt * H:(tt + 1) * H],
                                           stv[:, 16 + tt:17 + tt], stv[:, 24 + tt:25 + tt],
                                           op0=ALU.subtract, op1=ALU.mult)
                P.emit("vector", ap, inc="v")
            return P.cnt["v"]

        if LIMIT["ln1"]:
            emit_ln()

        p3range = range(NT) if (LIMIT["p3"] or LIMIT["experts"] > 0) else range(0)
        for tt in p3range:
            def cst(E, tt=tt):
                if tt == 0:
                    E.drain()
                return E.tensor_copy(x1r(tt), stream[:, tt * H:(tt + 1) * H])
            ccast = P.emit("vector", cst, waits={"t": cp.get(("x1tr", tt - 1), 0)},
                           inc="v")["v"]
            for g in range(4):
                half = g % 2
                waits = {"v": max(ccast, pbt_free[half]), "gd": cp["xtb"]}
                if tt == 0 and g < 2 and nheads > 0:
                    waits["t"] = cp.get(("pv_done", nheads - 1), 0)
                def tr(E, g=g, half=half, tt=tt):
                    ins = None
                    for j in range(4):
                        hc = g * 4 + j
                        ins = E.transpose(pbth[half][:, j * 128:(j + 1) * 128],
                                          x1r(tt)[:, hc * 128:(hc + 1) * 128], IDT)
                    return ins
                tcnt = P.emit("tensor", tr, waits=waits, inc="t")["t"]
                def ev(E, g=g, half=half, tt=tt):
                    return E.tensor_copy(
                        xt[:].rearrange("p (hc t) -> p hc t", hc=16)[:, g * 4:(g + 1) * 4,
                                                                     tt * 128:(tt + 1) * 128],
                        pbth[half][:, 0:512].rearrange("p (a b) -> p a b", a=4))
                pbt_free[half] = P.emit("vector", ev, waits={"t": tcnt}, inc="v")["v"]
            cp[("x1t", tt)] = P.cnt["v"]
            cp[("x1tr", tt)] = P.cnt["t"]

        if len(p3range) > 0:
            waits = {"v": max(cp[("x1t", NT - 1)], cp.get(("pv_evac_prev",), 0)),
                     "gd": cp["consts_gd"]}
            def gmm(E):
                ins = None
                for hc in range(16):
                    for tt in range(NT):
                        ins = E.matmul(pb4[:, tt * 16:(tt + 1) * 16],
                                       xt[:, hc * 1024 + tt * 128: hc * 1024 + (tt + 1) * 128],
                                       GWS[:, hc * 16:(hc + 1) * 16],
                                       start=(hc == 0 and tt == 0),
                                       stop=(hc == 15 and tt == NT - 1),
                                       skip_group_check=True)
                return ins
            gq = P.emit("tensor", gmm, waits=waits, inc="t")["t"]
        for tt in p3range:
            waits = {"t": gq}
            if tt > 0:
                waits["s"] = cp[("gexp", tt - 1)]
            def gmx(E, tt=tt):
                return E.reduce_max(stv[:, 54:55], pb4[:, tt * 16:(tt + 1) * 16],
                                    axis=AX.X, negate=True)
            gv = P.emit("vector", gmx, waits=waits, inc="v")["v"]
            def gex(E, tt=tt):
                return E.activation(gsm[:], pb4[:, tt * 16:(tt + 1) * 16], AF.Exp,
                                    bias=stv[:, 54:55], accum_out=stv[:, 55:56])
            cp[("gexp", tt)] = P.emit("scalar", gex, waits={"v": gv}, inc="s")["s"]
            def gtk(E, tt=tt):
                E.drain()
                E.reciprocal(stv[:, 56:57], stv[:, 55:56])
                E.drain()
                E.tensor_scalar_mul(gsm[:], gsm[:], stv[:, 56:57])
                E.drain()
                E.max(out=m8[:], in_=gsm[:])
                E.drain()
                E.tensor_scalar(g16[:], gsm[:], m8[:, 5:6], None, op0=ALU.is_ge)
                E.drain()
                E.tensor_mul(g16[:], gsm[:], g16[:])
                E.drain()
                E.reduce_sum(stv[:, 57:58], g16[:], axis=AX.X)
                E.drain()
                E.tensor_scalar(stv[:, 57:58], stv[:, 57:58], 1e-20, None, op0=ALU.add)
                E.drain()
                E.reciprocal(stv[:, 58:59], stv[:, 57:58])
                E.drain()
                return E.tensor_scalar(comb[:, tt * 16:(tt + 1) * 16], g16[:],
                                       stv[:, 58:59], None, op0=ALU.mult)
            P.emit("vector", gtk, waits={"s": cp[("gexp", tt)]}, inc="v")
        def gdr(E):
            return E.drain()
        P.emit("vector", gdr, inc=None)
        comb_done = P.cnt["v"]

        cp[("m1_silu_prev",)] = cp.get(("exp_done", nheads - 1, 7), 0)

        def selb(tt):
            return xt[:, tt * 2048 + 512: tt * 2048 + 1024]
        SELT = arena[:, 18048:22144]

        def abase(half):
            return half * 5632

        def xgv(hc):
            return xt[:, hc * 1024: hc * 1024 + 512]

        def emit_m1s(gj0, nk, ab_off, rhs_fn, xg_key):
            for k in range(nk):
                gj = gj0 + k
                emit_w13_load(gj)
                wsk, wsv = cp[("w13ld", gj)]
                gb, ub = (pb2, pb3) if k % 2 == 0 else (pb0, pb1)
                waits = {wsk: wsv, "s": cp[("m1_silu_prev",)],
                         "v": cp.get(xg_key, cp[("x1t", NT - 1)])}
                def gmmf(E, gj=gj, gb=gb, rhs_fn=rhs_fn):
                    ins = None
                    for hc in range(16):
                        ins = E.matmul(gb[:, 0:512], wsl(gj, hc * 128),
                                       rhs_fn(hc), start=(hc == 0), stop=(hc == 15),
                                       skip_group_check=True)
                    return ins
                gcnt = P.emit("tensor", gmmf, waits=waits, inc="t")["t"]
                uw = {}
                if cp.get(("m1_aev2",), 0):
                    uw["v"] = cp[("m1_aev2",)]
                def ummf(E, gj=gj, ub=ub, rhs_fn=rhs_fn):
                    ins = None
                    for hc in range(16):
                        ins = E.matmul(ub[:, 0:512], wsl(gj, 2048 + hc * 128),
                                       rhs_fn(hc), start=(hc == 0), stop=(hc == 15),
                                       skip_group_check=True)
                    return ins
                ucnt = P.emit("tensor", ummf, waits=uw, inc="t")["t"]
                cp[("w13used", w13buf(gj))] = ucnt
                def slf(E, gb=gb):
                    return E.activation(sg[:], gb[:, 0:512], AF.Silu)
                scnt = P.emit("scalar", slf,
                              waits={"t": gcnt, "v": cp.get(("m1_aev_prev",), 0)},
                              inc="s")["s"]
                cp[("m1_silu_prev",)] = scnt
                emit_w13_load(gj + 3)
                icol = ab_off + k * 512
                def aevf(E, ub=ub, icol=icol):
                    return E.tensor_mul(arena[:, icol: icol + 512], sg[:], ub[:, 0:512])
                acnt = P.emit("vector", aevf, waits={"t": ucnt, "s": scnt}, inc="v")["v"]
                cp[("m1_aev2",)] = cp.get(("m1_aev_prev",), 0)
                cp[("m1_aev_prev",)] = acnt
            return P.cnt["v"]

        def emit_routing(en):
            def mskf(E, en=en):
                E.drain()
                E.tensor_scalar(sklb[:, 0:8].rearrange("p (a b) -> p a b", b=1),
                                comb[:].rearrange("p (t g) -> p t g", g=16)[:, :, en:en + 1],
                                1e-30, None, op0=ALU.is_ge)
                E.drain()
                E.tensor_tensor_scan(sklb[:, 8:16], sklb[:, 0:8], sklb[:, 16:24],
                                     0.0, op0=ALU.add, op1=ALU.add)
                E.drain()
                E.tensor_sub(sklb[:, 8:16], sklb[:, 8:16], sklb[:, 0:8])
                return E.drain()
            mc = P.emit("vector", mskf,
                        waits={"v": comb_done, "t": cp.get(("posmm", en - 1), 0)},
                        inc="v")["v"]
            def posmm(E):
                E.matmul(pb0[:, 0:8], ltsb[:], sklb[:, 0:8], start=True, stop=False,
                         skip_group_check=True)
                return E.matmul(pb0[:, 0:8], ONE, sklb[:, 8:16], start=False, stop=True,
                                skip_group_check=True)
            pt_ = P.emit("tensor", posmm,
                         waits={"v": mc, "s": cp[("m1_silu_prev",)]}, inc="t")["t"]
            cp[("posmm", en)] = pt_
            def posev(E):
                E.drain()
                E.tensor_scalar(sstat[:, 0:8], pb0[:, 0:8], 1e6, None, op0=ALU.add)
                E.drain()
                E.scalar_tensor_tensor(out=sstat[:, 8:16], in0=sklb[:, 0:8],
                                       scalar=-1e6, in1=sstat[:, 0:8],
                                       op0=ALU.mult, op1=ALU.add)
                return E.drain()
            pv_ = P.emit("vector", posev, waits={"t": pt_}, inc="v")["v"]
            cp[("pos", en)] = pv_
            for tt in range(NT):
                def self_(E, tt=tt):
                    return E.tensor_scalar(selb(tt), iotab[:],
                                           sstat[:, 8 + tt:9 + tt], None, op0=ALU.is_equal)
                P.emit("vector", self_,
                       waits={"v": pv_, "t": cp.get(("gath_done", en - 1), 0)},
                       inc="v")
            cp[("sel", en)] = P.cnt["v"]

        def emit_selt(en):
            for tt in range(NT):
                half = tt % 2
                waits = {"v": max(cp[("sel", en)], pbt_free[half])}
                def trf(E, tt=tt, half=half):
                    ins = None
                    for ct in range(4):
                        ins = E.transpose(pbth[half][:, ct * 128:(ct + 1) * 128],
                                          selb(tt)[:, ct * 128:(ct + 1) * 128],
                                          IDT)
                    return ins
                tcnt = P.emit("tensor", trf, waits=waits, inc="t")["t"]
                def evt(E, tt=tt, half=half):
                    return E.tensor_copy(
                        SELT[:].rearrange("p (ct t c) -> p ct t c", ct=4, t=8)[:, :, tt, :],
                        pbth[half][:, 0:512].rearrange("p (a b) -> p a b", a=4))
                pbt_free[half] = P.emit("vector", evt, waits={"t": tcnt}, inc="v")["v"]
            cp[("selt", en)] = P.cnt["v"]

        def emit_gather(en):
            for hc in range(16):
                gb = pb0 if hc % 2 == 0 else pb1
                wv = max(cp[("sel", en)], cp.get(("xgev", hc % 2), 0))
                if hc < 2:
                    wv = max(wv, cp.get(("scatev", 0), 0), cp.get(("scatev", 1), 0),
                             cp.get(("pos", en), 0))
                def gmf(E, gb=gb, hc=hc):
                    ins = None
                    for tt in range(NT):
                        ins = E.matmul(gb[:, 0:512], x1r(tt)[:, hc * 128:(hc + 1) * 128],
                                       selb(tt),
                                       start=(tt == 0), stop=(tt == NT - 1))
                    return ins
                tcnt = P.emit("tensor", gmf, waits={"v": wv}, inc="t")["t"]
                def xev(E, gb=gb, hc=hc):
                    return E.tensor_copy(xgv(hc), gb[:, 0:512])
                cp[("xgev", hc % 2)] = P.emit("vector", xev, waits={"t": tcnt},
                                              inc="v")["v"]
            cp[("xgd", en)] = P.cnt["v"]
            cp[("gath_done", en)] = P.cnt["t"]

        def emit_m2_shared(th, adep):
            gidx = cp.get(("m2gidx",), 0)
            for q in range(4):
                for hb in range(2):
                    wj = th * 8 + q * 2 + hb
                    emit_w2_load(wj)
                    wsb = wsbufs[wj % 2]
                    w2k, w2v = cp[("w2ld", wj)]
                    for ct in range(4):
                        bk = pb4 if gidx % 2 == 0 else pb5
                        pkey = ("m2ev", gidx % 2)
                        waits = {w2k: w2v, "v": adep, "s": cp[("m1_silu_prev",)]}
                        if pkey in cp:
                            waits["v"] = max(waits["v"], cp[pkey])
                        ihalf = q // 2
                        def mmf(E, bk=bk, wsb=wsb, ct=ct, ihalf=ihalf):
                            ins = None
                            for ici in range(IC):
                                icol = (ihalf * 11 + ici) * 512
                                ins = E.matmul(bk[:, 0:512],
                                               arena[:, icol + ct * 128: icol + (ct + 1) * 128],
                                               wsb[:, ici * 512:(ici + 1) * 512],
                                               start=(ici == 0), stop=(ici == IC - 1),
                                               skip_group_check=True)
                            return ins
                        tcnt = P.emit("tensor", mmf, waits=waits, inc="t")["t"]
                        if ct == 3:
                            cp[("w2used", wj % 2)] = tcnt
                        tt = th * 4 + ct
                        def evf(E, bk=bk, tt=tt, q=q, hb=hb):
                            dst = stream[:, tt * H + (q % 2) * 1024 + hb * 512:
                                         tt * H + (q % 2) * 1024 + (hb + 1) * 512]
                            return E.scalar_tensor_tensor(out=dst, in0=bk[:, 0:512],
                                                          scalar=1.0, in1=dst,
                                                          op0=ALU.mult, op1=ALU.add)
                        c = P.emit("vector", evf,
                                   waits={"t": tcnt,
                                          "v": max(comb_done,
                                                   cp.get(("evf_prev", tt), 0))},
                                   inc="v")["v"]
                        cp[pkey] = c
                        cp[("evf_prev", tt)] = c
                        gidx += 1
                    emit_w2_load(wj + 2)
            cp[("m2gidx",)] = gidx

        def emit_m2_routed(e, wj0, adep):
            ybase = abase(1 - e % 2)
            gidx = cp.get(("m2gidx",), 0)
            for hh in range(2):
                for hb in range(2):
                    wj = wj0 + hh * 2 + hb
                    emit_w2_load(wj)
                    wsb = wsbufs[wj % 2]
                    w2k, w2v = cp[("w2ld", wj)]
                    for ct in range(4):
                        bk = pb4 if gidx % 2 == 0 else pb5
                        pkey = ("m2ev", gidx % 2)
                        waits = {w2k: w2v, "v": adep, "s": cp[("m1_silu_prev",)]}
                        if pkey in cp:
                            waits["v"] = max(waits["v"], cp[pkey])
                        abse = abase(e % 2)
                        def mmf(E, bk=bk, wsb=wsb, ct=ct, abse=abse):
                            ins = None
                            for ici in range(IC):
                                ins = E.matmul(
                                    bk[:, 0:512],
                                    arena[:, abse + ici * 512 + ct * 128:
                                           abse + ici * 512 + (ct + 1) * 128],
                                    wsb[:, ici * 512:(ici + 1) * 512],
                                    start=(ici == 0), stop=(ici == IC - 1),
                                    skip_group_check=True)
                            return ins
                        tcnt = P.emit("tensor", mmf, waits=waits, inc="t")["t"]
                        if ct == 3:
                            cp[("w2used", wj % 2)] = tcnt
                        yc = ybase + ct * 1024 + hb * 512
                        def evf(E, bk=bk, yc=yc):
                            return E.tensor_copy(arena[:, yc: yc + 512], bk[:, 0:512])
                        c = P.emit("vector", evf, waits={"t": tcnt}, inc="v")["v"]
                        cp[pkey] = c
                        gidx += 1
                    emit_w2_load(wj + 2)
                if hh == 0 and e + 1 < nroute:
                    emit_routing(e + 1)
                ev_y = P.cnt["v"]
                for tt in range(NT):
                    sbk = (pb0, pb1) if tt % 2 == 0 else (pb2, pb3)
                    skey = ("scatev", tt % 2)
                    wv = max(ev_y, cp[("selt", e)], cp.get(skey, 0))
                    if tt < 2:
                        wv = max(wv, cp.get(("pos", e + 1), 0))
                    def smm(E, sbk=sbk, tt=tt, ybase=ybase):
                        ins = None
                        for ct in range(4):
                            lhs = SELT[:, (ct * 8 + tt) * 128:(ct * 8 + tt + 1) * 128]
                            for hb in range(2):
                                ins = E.matmul(
                                    sbk[hb][:, 0:512], lhs,
                                    arena[:, ybase + ct * 1024 + hb * 512:
                                           ybase + ct * 1024 + (hb + 1) * 512],
                                    start=(ct == 0), stop=(ct == 3),
                                    skip_group_check=True)
                        return ins
                    tcnt = P.emit("tensor", smm, waits={"v": wv}, inc="t")["t"]
                    def sev(E, sbk=sbk, tt=tt, hh=hh, e=e):
                        ins = None
                        for hb in range(2):
                            dst = stream[:, tt * H + hh * 1024 + hb * 512:
                                         tt * H + hh * 1024 + (hb + 1) * 512]
                            ins = E.scalar_tensor_tensor(
                                out=dst, in0=sbk[hb][:, 0:512],
                                scalar=comb[:, tt * 16 + e: tt * 16 + e + 1],
                                in1=dst, op0=ALU.mult, op1=ALU.add)
                        return ins
                    c = P.emit("vector", sev,
                               waits={"t": tcnt,
                                      "v": max(comb_done,
                                               cp.get(("evf_prev", tt), 0))},
                               inc="v")["v"]
                    cp[skey] = c
                    cp[("evf_prev", tt)] = c
            cp[("m2gidx",)] = gidx

        def emit_m1d(gj0, nk):
            for k in range(nk):
                gj = gj0 + k
                emit_w13_load(gj)
                wbuf = wbufs[gj % 2]
                wsk, wsv = cp[("w13ld", gj)]
                waits = {wsk: wsv, "s": cp[("m1_silu_prev",)],
                         "v": cp[("x1t", NT - 1)]}
                for tb in range(2):
                    gb, ub = (pb2, pb3) if tb == 0 else (pb0, pb1)
                    def gmmf(E, wbuf=wbuf, gb=gb, tb=tb):
                        ins = None
                        for hc in range(16):
                            ins = E.matmul(gb[:, 0:512], wbuf[:, hc * 128:(hc + 1) * 128],
                                           xt[:, hc * 1024 + tb * 512: hc * 1024 + (tb + 1) * 512],
                                           start=(hc == 0), stop=(hc == 15),
                                           skip_group_check=True)
                        return ins
                    gcnt = P.emit("tensor", gmmf, waits=waits if tb == 0 else {},
                                  inc="t")["t"]
                    uw = {}
                    if cp.get(("m1_aev2",), 0):
                        uw["v"] = cp[("m1_aev2",)]
                    def ummf(E, wbuf=wbuf, ub=ub, tb=tb):
                        ins = None
                        for hc in range(16):
                            ins = E.matmul(ub[:, 0:512],
                                           wbuf[:, 2048 + hc * 128: 2048 + (hc + 1) * 128],
                                           xt[:, hc * 1024 + tb * 512: hc * 1024 + (tb + 1) * 512],
                                           start=(hc == 0), stop=(hc == 15),
                                           skip_group_check=True)
                        return ins
                    ucnt = P.emit("tensor", ummf, waits=uw, inc="t")["t"]
                    if tb == 1:
                        cp[("w13used", gj % 2)] = ucnt
                    def slf(E, gb=gb):
                        return E.activation(sg[:], gb[:, 0:512], AF.Silu)
                    scnt = P.emit("scalar", slf,
                                  waits={"t": gcnt, "v": cp.get(("m1_aev_prev",), 0)},
                                  inc="s")["s"]
                    cp[("m1_silu_prev",)] = scnt
                    icol = k * 1024 + tb * 512
                    def aevf(E, ub=ub, icol=icol):
                        return E.tensor_mul(arena[:, icol: icol + 512], sg[:], ub[:, 0:512])
                    acnt = P.emit("vector", aevf, waits={"t": ucnt, "s": scnt},
                                  inc="v")["v"]
                    cp[("m1_aev2",)] = cp.get(("m1_aev_prev",), 0)
                    cp[("m1_aev_prev",)] = acnt
                emit_w13_load(gj + 2)
            return P.cnt["v"]

        def emit_m2_sharedAB(ihalf, adep):
            gidx = cp.get(("m2gidx",), 0)
            for q in (2 * ihalf, 2 * ihalf + 1):
                for hb in range(2):
                    wj = q * 2 + hb
                    emit_w2_load(wj)
                    wsb = wsbufs[wj % 2]
                    w2k, w2v = cp[("w2ld", wj)]
                    for tt in range(NT):
                        bk = pb4 if gidx % 2 == 0 else pb5
                        pkey = ("m2ev", gidx % 2)
                        waits = {w2k: w2v, "v": adep, "s": cp[("m1_silu_prev",)]}
                        if pkey in cp:
                            waits["v"] = max(waits["v"], cp[pkey])
                        def mmf(E, bk=bk, wsb=wsb, tt=tt):
                            ins = None
                            for ici in range(IC):
                                ins = E.matmul(bk[:, 0:512],
                                               arena[:, ici * 1024 + tt * 128: ici * 1024 + (tt + 1) * 128],
                                               wsb[:, ici * 512:(ici + 1) * 512],
                                               start=(ici == 0), stop=(ici == IC - 1),
                                               skip_group_check=True)
                            return ins
                        tcnt = P.emit("tensor", mmf, waits=waits, inc="t")["t"]
                        if tt == NT - 1:
                            cp[("w2used", wj % 2)] = tcnt
                        def evf(E, bk=bk, tt=tt, q=q, hb=hb):
                            dst = stream[:, tt * H + (q % 2) * 1024 + hb * 512:
                                         tt * H + (q % 2) * 1024 + (hb + 1) * 512]
                            return E.scalar_tensor_tensor(out=dst, in0=bk[:, 0:512],
                                                          scalar=1.0, in1=dst,
                                                          op0=ALU.mult, op1=ALU.add)
                        c = P.emit("vector", evf,
                                   waits={"t": tcnt,
                                          "v": max(comb_done,
                                                   cp.get(("evf_prev", tt), 0))},
                                   inc="v")["v"]
                        cp[pkey] = c
                        cp[("evf_prev", tt)] = c
                        gidx += 1
                    emit_w2_load(wj + 2)
            cp[("m2gidx",)] = gidx

        if do_shared:
            adep_a = emit_m1d(0, 11)
            emit_m2_sharedAB(0, adep_a)
            adep_b = emit_m1d(11, 11)
            cp[("shared_m1_done",)] = P.cnt["t"]
            if nroute > 0:
                emit_routing(0)
            emit_m2_sharedAB(1, adep_b)
        if nroute > 0:
            if ("sel", 0) not in cp:
                emit_routing(0)
            for e in range(nroute):
                emit_selt(e)
                emit_gather(e)
                adep = emit_m1s(22 + e * 11, 11, abase(e % 2), xgv, ("xgd", e))
                emit_m2_routed(e, 8 + e * 4, adep)

        if LIMIT["ln2"]:
            emit_ln()
        st_wait = P.cnt["v"]
        for tt in range(NT):
            def st(E, tt=tt):
                return E.dma_start(out=d_out[tt * 128:(tt + 1) * 128, :],
                                   in_=stream[:, tt * H:(tt + 1) * H])
            P.emit("sync", st, waits={"v": st_wait}, inc="sd", amt=16)
        final_sd = P.cnt["sd"]
        def fw(E):
            return E.wait_ge(sems["sd"], final_sd)
        P.ops["sync"].append(({}, fw, None, 0))

        with nc.Block() as block:
            def replay(E, name):
                lastw = {}
                for waits, fn, inc, amt in P.ops[name]:
                    for sname, val in waits.items():
                        if val > 0 and lastw.get(sname, -1) < val:
                            E.wait_ge(sems[sname], val)
                            lastw[sname] = val
                    ins = fn(E)
                    if inc is not None:
                        ins.then_inc(sems[inc], amt)

            @block.sync
            def _(E):
                replay(E, "sync")

            @block.gpsimd
            def _(E):
                replay(E, "gpsimd")

            @block.tensor
            def _(E):
                replay(E, "tensor")

            @block.vector
            def _(E):
                replay(E, "vector")

            @block.scalar
            def _(E):
                replay(E, "scalar")

    return nc


def _host_prep(L0_ffn_gate_w, L0_ffn_w13, L0_ffn_w2, L0_ffn_sw13, L0_ffn_sw2):
    gw = np.asarray(L0_ffn_gate_w, dtype=np.float32)
    w13 = np.asarray(L0_ffn_w13, dtype=np.float32)
    w2 = np.asarray(L0_ffn_w2, dtype=np.float32)
    sw13 = np.asarray(L0_ffn_sw13, dtype=np.float32)
    sw2 = np.asarray(L0_ffn_sw2, dtype=np.float32)

    w13t = w13.reshape(16, 22, 128, 16, 128).transpose(0, 1, 4, 3, 2).reshape(16, 22, 128, 2048)
    w13t = np.ascontiguousarray(w13t[:, JORD])
    sw13t = sw13.reshape(44, 128, 16, 128).transpose(0, 3, 2, 1).reshape(44, 128, 2048)
    sw13t = np.ascontiguousarray(sw13t[SJORD])
    w2t = np.ascontiguousarray(
        w2.transpose(0, 2, 1).reshape(16, 11, 128, 2, 1024).transpose(0, 3, 1, 2, 4))
    sw2t = np.ascontiguousarray(
        sw2.T.reshape(2, 11, 128, 2, 1024).transpose(0, 3, 1, 2, 4).reshape(4, 11, 128, 1024))
    gwt = np.ascontiguousarray(gw.T.reshape(16, 128, 16))

    identb = np.eye(128, dtype=ml_dtypes.bfloat16)
    dd = np.arange(128)[:, None]
    qq = np.arange(128)[None, :]
    trib = (dd > qq).astype(ml_dtypes.bfloat16)
    onesb = np.ones((128, 128), dtype=ml_dtypes.bfloat16)

    kl = np.arange(128)[None, :]
    lower_eq = (dd <= kl).astype(np.float32)
    bx = np.zeros((2, 8, 128, 256), dtype=np.float32)
    by = np.zeros((2, 8, 128, 256), dtype=np.float32)
    for s in range(8):
        bx[0, s, :, 128:256] = -CBIG * lower_eq
        bx[1, s, :, 0:128] = -CBIG * lower_eq
        by[1, s, :, 128:256] = -CBIG / 128.0

    lts = (dd < kl).astype(ml_dtypes.bfloat16)
    iota = np.broadcast_to(np.arange(512, dtype=np.float16)[None, :],
                           (128, 512)).copy()
    return (w13t, sw13t, w2t, sw2t, gwt, identb, trib, onesb, bx, by,
            lts, iota)


def _row_tiles(role):
    return [(N_LIST[s] - 1 - role) for s in range(8)]


def kernel(L0_attn_x, L0_ffn_gate_w, L0_ffn_w13, L0_ffn_w2, L0_ffn_sw13, L0_ffn_sw2):
    x = np.asarray(L0_attn_x, dtype=np.float32)
    (w13t, sw13t, w2t, sw2t, gwt, identb, trib, onesb, bx, by,
     lts, iota) = _host_prep(
        L0_ffn_gate_w, L0_ffn_w13, L0_ffn_w2, L0_ffn_sw13, L0_ffn_sw2)

    nc = build_program()

    in_maps = []
    for c in range(8):
        b, role = c // 2, c % 2
        tiles = _row_tiles(role)
        rows = np.concatenate([np.arange(t * 128, (t + 1) * 128) for t in tiles])
        in_maps.append({
            "xmy": np.ascontiguousarray(x[b][rows]),
            "xkv": np.ascontiguousarray(x[b]),
            "w13t": w13t,
            "sw13t": sw13t,
            "w2t": w2t,
            "sw2t": sw2t,
            "gwt": gwt,
            "bxm": np.ascontiguousarray(bx[role]),
            "bym": np.ascontiguousarray(by[role]),
            "identb": identb,
            "trib": trib,
            "onesb": onesb,
            "ltsm": lts,
            "iotam": iota,
        })

    res = run_bass_kernel_spmd(nc, in_maps, core_ids=list(range(8)))

    out = np.empty((B, L, H), dtype=np.float32)
    for c in range(8):
        b, role = c // 2, c % 2
        o = res.results[c]["out"]
        for s, t in enumerate(_row_tiles(role)):
            out[b, t * 128:(t + 1) * 128, :] = o[s * 128:(s + 1) * 128, :]
    return out


# version 09
# speedup vs baseline: 1.4056x; 1.0401x over previous
import numpy as np
import ml_dtypes
import concourse.bass as bass
import concourse.mybir as mybir
from concourse.bass_utils import run_bass_kernel_spmd

F32 = mybir.dt.float32
BF16 = mybir.dt.bfloat16
FP16 = mybir.dt.float16
AF = mybir.ActivationFunctionType
ALU = mybir.AluOpType
AX = mybir.AxisListType

B = 4
L = 2048
H = 2048
NH = 16
NT = 8
NE = 16
IC = 11
SCALE = 1.0 / float(np.sqrt(128.0))
CBIG = 1.0e30
N_LIST = [16, 14, 12, 10, 8, 6, 4, 2]
SEQ = [7, 6, 5, 4, 3, 2, 1, 0]

JORD = [x for i in range(11) for x in (i, 11 + i)]
SJORD = [x for i in range(22) for x in (i, 22 + i)]

W13SEQ = [("s", None, k) for k in range(22)] + \
         [("r", e, k) for e in range(NE) for k in range(11)]
W2SEQ = [("s", None, q, hb) for q in range(4) for hb in range(2)] + \
        [("r", e, hh, hb) for e in range(NE) for hh in range(2) for hb in range(2)]


class Plan:
    def __init__(self):
        self.ops = {"tensor": [], "vector": [], "scalar": [], "sync": [], "gpsimd": []}
        self.cnt = {"sd": 0, "gd": 0, "t": 0, "v": 0, "s": 0,
                    "kva": 0, "kvb": 0, "w1a": 0, "w1b": 0, "w2a": 0, "w2b": 0,
                    "w1c": 0}

    def emit(self, eng, fn, waits=None, inc=None, amt=1):
        w = {k: v for k, v in (waits or {}).items() if v and v > 0}
        self.ops[eng].append((w, fn, inc, amt))
        if inc is not None:
            self.cnt[inc] += amt
        return dict(self.cnt)


LIMIT = {"heads": NH, "ln1": 1, "p3": 1, "experts": 18, "ln2": 1, "astage": 9}


def build_program():
    nheads = LIMIT["heads"]
    nroute = min(NE, LIMIT["experts"])
    do_shared = LIMIT["experts"] >= 17
    A = LIMIT["astage"]
    nc = bass.Bass("TRN2", target_bir_lowering=False)

    d_xmy = nc.dram_tensor("xmy", [1024, H], F32, kind="ExternalInput").ap()
    d_xkv = nc.dram_tensor("xkv", [L, H], F32, kind="ExternalInput").ap()
    d_w13 = nc.dram_tensor("w13t", [NE, 22, 128, 2048], F32, kind="ExternalInput").ap()
    d_sw13 = nc.dram_tensor("sw13t", [44, 128, 2048], F32, kind="ExternalInput").ap()
    d_w2 = nc.dram_tensor("w2t", [NE, 2, IC, 128, 1024], F32, kind="ExternalInput").ap()
    d_sw2 = nc.dram_tensor("sw2t", [4, IC, 128, 1024], F32, kind="ExternalInput").ap()
    d_gw = nc.dram_tensor("gwt", [16, 128, 16], F32, kind="ExternalInput").ap()
    d_bx = nc.dram_tensor("bxm", [8, 128, 256], F32, kind="ExternalInput").ap()
    d_by = nc.dram_tensor("bym", [8, 128, 256], F32, kind="ExternalInput").ap()
    d_id = nc.dram_tensor("identb", [128, 128], BF16, kind="ExternalInput").ap()
    d_tri = nc.dram_tensor("trib", [128, 128], BF16, kind="ExternalInput").ap()
    d_one = nc.dram_tensor("onesb", [128, 128], BF16, kind="ExternalInput").ap()
    d_lts = nc.dram_tensor("ltsm", [128, 128], BF16, kind="ExternalInput").ap()
    d_iota = nc.dram_tensor("iotam", [128, 512], FP16, kind="ExternalInput").ap()
    d_out = nc.dram_tensor("out", [1024, H], F32, kind="ExternalOutput").ap()

    from contextlib import ExitStack
    with ExitStack() as ctx:
        stream = ctx.enter_context(nc.sbuf_tensor("stream", [128, NT * H], F32))
        xt = ctx.enter_context(nc.sbuf_tensor("xt", [128, NT * H], BF16))
        arena = ctx.enter_context(nc.sbuf_tensor("arena", [128, 22528], BF16))
        wb0 = ctx.enter_context(nc.sbuf_tensor("wb0", [128, 4096], BF16))
        wb1 = ctx.enter_context(nc.sbuf_tensor("wb1", [128, 4096], BF16))
        ws0 = ctx.enter_context(nc.sbuf_tensor("ws0", [128, IC * 512], BF16))
        ws1 = ctx.enter_context(nc.sbuf_tensor("ws1", [128, IC * 512], BF16))
        x1r6 = ctx.enter_context(nc.sbuf_tensor("x1r6", [128, 6 * H], BF16))
        ltsb = ctx.enter_context(nc.sbuf_tensor("ltsb", [128, 128], BF16))
        iotab = ctx.enter_context(nc.sbuf_tensor("iotab", [128, 512], FP16))
        sstat = ctx.enter_context(nc.sbuf_tensor("sstat", [128, 24], F32))
        sklb = ctx.enter_context(nc.sbuf_tensor("sklb", [128, 24], BF16))
        sg = ctx.enter_context(nc.sbuf_tensor("sg", [128, 512], F32))
        comb = ctx.enter_context(nc.sbuf_tensor("comb", [128, NT * 16], F32))
        stv = ctx.enter_context(nc.sbuf_tensor("stat", [128, 64], F32))
        gsm = ctx.enter_context(nc.sbuf_tensor("gsm", [128, 16], F32))
        g16 = ctx.enter_context(nc.sbuf_tensor("g16", [128, 16], F32))
        m8 = ctx.enter_context(nc.sbuf_tensor("m8", [128, 8], F32))
        pb0 = ctx.enter_context(nc.psum_tensor("pb0", [128, 512], F32))
        pb1 = ctx.enter_context(nc.psum_tensor("pb1", [128, 512], F32))
        pb2 = ctx.enter_context(nc.psum_tensor("pb2", [128, 512], F32))
        pb3 = ctx.enter_context(nc.psum_tensor("pb3", [128, 512], F32))
        pb4 = ctx.enter_context(nc.psum_tensor("pb4", [128, 512], F32))
        pb5 = ctx.enter_context(nc.psum_tensor("pb5", [128, 512], F32))
        pbtA = ctx.enter_context(nc.psum_tensor("pbtA", [128, 1024], BF16))
        pbtB = ctx.enter_context(nc.psum_tensor("pbtB", [128, 1024], BF16))
        sem_sd = ctx.enter_context(nc.semaphore("sem_sd"))
        sem_gd = ctx.enter_context(nc.semaphore("sem_gd"))
        sem_t = ctx.enter_context(nc.semaphore("sem_t"))
        sem_v = ctx.enter_context(nc.semaphore("sem_v"))
        sem_s = ctx.enter_context(nc.semaphore("sem_s"))
        sem_kva = ctx.enter_context(nc.semaphore("sem_kva"))
        sem_kvb = ctx.enter_context(nc.semaphore("sem_kvb"))
        sem_w1a = ctx.enter_context(nc.semaphore("sem_w1a"))
        sem_w1b = ctx.enter_context(nc.semaphore("sem_w1b"))
        sem_w2a = ctx.enter_context(nc.semaphore("sem_w2a"))
        sem_w2b = ctx.enter_context(nc.semaphore("sem_w2b"))
        sem_w1c = ctx.enter_context(nc.semaphore("sem_w1c"))
        sems = {"sd": sem_sd, "gd": sem_gd, "t": sem_t, "v": sem_v, "s": sem_s,
                "kva": sem_kva, "kvb": sem_kvb, "w1a": sem_w1a, "w1b": sem_w1b,
                "w2a": sem_w2a, "w2b": sem_w2b, "w1c": sem_w1c}
        pbs = [pb0, pb1, pb2, pb3, pb4, pb5]
        pbth = [pbtA, pbtB]
        wbufs = [wb0, wb1]
        wsbufs = [ws0, ws1]

        def ar(off, n):
            return arena[:, off:off + n]
        VB = [ar(0, 2048), ar(2048, 2048)]
        KTB = [ar(4096, 2048), ar(6144, 2048)]
        PSB = ar(8192, 2048)
        PTB = ar(10240, 2048)
        QTA = ar(12288, 1024)
        BXS = ar(13312, 2048)
        BYS = ar(15360, 2048)
        IDT = ar(17408, 128)
        TRI = ar(17536, 128)
        ONE = ar(17664, 128)
        GWS = ar(17792, 256)
        TMPB = ar(18048, 2048)
        ABUF = [arena[:, 0:11264], arena[:, 11264:22528]]

        def x1r(tt):
            if tt == 6:
                return arena[:, 13312:15360]
            if tt == 7:
                return arena[:, 15360:17408]
            return x1r6[:, tt * H:(tt + 1) * H]


        P = Plan()
        cp = {}

        for tt in range(NT):
            def ld(E, tt=tt):
                return E.dma_start(out=stream[:, tt * H:(tt + 1) * H],
                                   in_=d_xmy[tt * 128:(tt + 1) * 128, :])
            P.emit("sync", ld, inc="sd", amt=16)
        for dst, srcd in ((IDT, d_id), (TRI, d_tri), (ONE, d_one), (ltsb[:], d_lts),
                          (iotab[:], d_iota)):
            def ld(E, dst=dst, srcd=srcd):
                return E.dma_start(out=dst, in_=srcd)
            P.emit("sync", ld, inc="sd", amt=16)
        def zsr(E):
            return E.memset(sklb[:, 16:24], 0.0)
        P.emit("vector", zsr, inc="v")
        cp["stream"] = P.cnt["sd"]
        cp["consts_sd"] = P.cnt["sd"]

        if LIMIT["heads"] < NH or LIMIT["experts"] < 18:
            for src in (d_xkv[0:128, 0:128], d_w13[0, 0, :, 0:128], d_sw13[0, :, 0:128],
                        d_w2[0, 0, 0, :, 0:128], d_sw2[0, 0, :, 0:128], d_gw[0],
                        d_bx[0, :, 0:128], d_by[0, :, 0:128]):
                def ld(E, src=src):
                    return E.dma_start(out=wb0[:, 0:src.shape[-1]], in_=src)
                P.emit("gpsimd", ld, waits={"gd": P.cnt["gd"]}, inc="gd", amt=16)
                cp["dbg_touch"] = P.cnt["gd"]

        for half in range(2):
            def ld(E, half=half):
                return E.dma_start(
                    out=xt[:, half * 4 * H:(half + 1) * 4 * H].rearrange("p (tt e) -> p tt e", tt=4),
                    in_=d_xmy[half * 512:(half + 1) * 512, :].rearrange("(tt p) e -> p tt e", p=128))
            P.emit("gpsimd", ld, inc="gd", amt=16)
        for dst, src in ((BXS, d_bx), (BYS, d_by), (GWS, d_gw)):
            def ld(E, dst=dst, src=src):
                return E.dma_start(out=dst.rearrange("p (a b) -> p a b", b=src.shape[2]),
                                   in_=src.rearrange("a p b -> p a b"))
            P.emit("gpsimd", ld, inc="gd", amt=16)
        cp["xtb"] = P.cnt["gd"]
        cp["consts_gd"] = P.cnt["gd"]

        for tt in range(NT):
            def sqn(E, tt=tt):
                return E.activation(PSB, stream[:, tt * H:(tt + 1) * H],
                                    AF.Square, accum_out=stv[:, 32 + tt:33 + tt])
            P.emit("scalar", sqn, waits={"sd": cp["stream"],
                                         "s": cp.get(("sqn_prev",), 0)}, inc="s")
            cp[("sqn_prev",)] = P.cnt["s"]
        def nrmsc(E):
            E.drain()
            E.tensor_scalar(stv[:, 32:40], stv[:, 32:40], -1.0 / 128.0, None,
                            op0=ALU.mult)
            return E.drain()
        cp["nrm_done"] = P.emit("vector", nrmsc, waits={"s": P.cnt["s"]},
                                inc="v")["v"]

        w13_emitted = set()

        def w13buf(gj):
            return gj % 2 if gj < 22 else (gj - 22) % 3

        def wb2_out():
            return xt[:].rearrange("p (r q c) -> p r q c", r=8, q=4)[:, :, 3, :]

        def wsl(gj, v):
            buf = w13buf(gj)
            if buf < 2:
                return wbufs[buf][:, v:v + 128]
            r, off = v // 512, v % 512
            base = r * 2048 + 1536 + off
            return xt[:, base:base + 128]

        W1SK = {0: "w1a", 1: "w1b", 2: "w1c"}

        def emit_w13_load(gj):
            if gj >= len(W13SEQ) or gj in w13_emitted:
                return
            w13_emitted.add(gj)
            kind, e, k = W13SEQ[gj][0], W13SEQ[gj][1], W13SEQ[gj][2]
            buf = w13buf(gj)
            waits = {}
            if "dbg_touch" in cp:
                waits["gd"] = cp["consts_gd"]
            if ("w13used", buf) in cp:
                waits["t"] = cp[("w13used", buf)]
            if buf == 2 and ("w13used", 2) not in cp:
                waits["t"] = cp.get(("shared_m1_done",), 0)
            sk = W1SK[buf]
            if buf < 2:
                def ld(E, e=e, k=k, kind=kind, buf=buf):
                    srcd = (d_w13[e, 2 * k:2 * k + 2] if kind == "r"
                            else d_sw13[2 * k:2 * k + 2])
                    return E.dma_start(
                        out=wbufs[buf][:].rearrange("p (g c) -> p g c", g=2),
                        in_=srcd.rearrange("g p c -> p g c"))
                cp[("w13ld", gj)] = (sk, P.emit("gpsimd", ld, waits=waits,
                                                inc=sk, amt=16)[sk])
            else:
                for g in range(2):
                    def ld(E, e=e, k=k, kind=kind, g=g):
                        srcd = (d_w13[e, 2 * k + g] if kind == "r"
                                else d_sw13[2 * k + g])
                        return E.dma_start(
                            out=wb2_out()[:, 4 * g:4 * g + 4, :],
                            in_=srcd.rearrange("p (s c) -> p s c", s=4))
                    v = P.emit("gpsimd", ld, waits=waits if g == 0 else {},
                               inc=sk, amt=16)[sk]
                cp[("w13ld", gj)] = (sk, v)

        w2_emitted = set()

        def emit_w2_load(wj):
            if wj >= len(W2SEQ) or wj in w2_emitted:
                return
            w2_emitted.add(wj)
            kind, e, hh, hb = W2SEQ[wj][0], W2SEQ[wj][1], W2SEQ[wj][2], W2SEQ[wj][3]
            buf = wj % 2
            waits = {}
            if ("w2used", buf) in cp:
                waits["t"] = cp[("w2used", buf)]
            if kind == "r":
                def ld(E, e=e, hh=hh, hb=hb, buf=buf):
                    return E.dma_start(
                        out=wsbufs[buf][:].rearrange("p (i c) -> p i c", i=IC),
                        in_=d_w2[e, hh][:, :, hb * 512:(hb + 1) * 512]
                        .rearrange("i p c -> p i c"))
            else:
                def ld(E, hh=hh, hb=hb, buf=buf):
                    return E.dma_start(
                        out=wsbufs[buf][:].rearrange("p (i c) -> p i c", i=IC),
                        in_=d_sw2[hh][:, :, hb * 512:(hb + 1) * 512]
                        .rearrange("i p c -> p i c"))
            sk = "w2a" if buf == 0 else "w2b"
            cp[("w2ld", wj)] = (sk, P.emit("gpsimd", ld, waits=waits, inc=sk, amt=16)[sk])

        def emit_vload(h):
            waits = {}
            if h >= 2:
                waits["t"] = cp.get(("pv_done", h - 2), 0)
            def ld(E, h=h):
                return E.dma_start(
                    out=VB[h % 2].rearrange("p (kt d) -> p kt d", kt=16),
                    in_=d_xkv[:, h * 128:(h + 1) * 128].rearrange("(kt p) d -> p kt d", p=128))
            sk = "kva" if h % 2 == 0 else "kvb"
            cp[("vload", h)] = (sk, P.emit("gpsimd", ld, waits=waits, inc=sk, amt=16)[sk])

        if nheads > 0:
            emit_vload(0)
        if nheads > 1:
            emit_vload(1)

        pbt_free = [0, 0]
        lastexp = {}

        for h in range(nheads):
            buf = h % 2
            for g in range(4 if A >= 1 else 0):
                half = g % 2
                vsk, vsv = cp[("vload", h)]
                waits = {vsk: vsv, "v": pbt_free[half], "sd": cp["consts_sd"]}
                def tr(E, g=g, half=half, buf=buf):
                    ins = None
                    for j in range(4):
                        ins = E.transpose(pbth[half][:, j * 128:(j + 1) * 128],
                                          VB[buf][:, (g * 4 + j) * 128:(g * 4 + j + 1) * 128], IDT)
                    return ins
                tcnt = P.emit("tensor", tr, waits=waits, inc="t")["t"]
                def ev(E, g=g, half=half, buf=buf):
                    return E.tensor_copy(KTB[buf][:, g * 512:(g + 1) * 512],
                                         pbth[half][:, 0:512])
                pbt_free[half] = P.emit("vector", ev, waits={"t": tcnt}, inc="v")["v"]
                cp[("ktev", h, g)] = pbt_free[half]
            for g in ((1, 0) if A >= 2 else ()):
                half = g % 2
                waits = {"gd": cp["xtb"], "v": pbt_free[half]}
                if h >= 1:
                    waits["t"] = cp.get(("qk_done", h - 1, 4 * g), 0)
                def tr(E, g=g, half=half, h=h):
                    ins = None
                    for j in range(4):
                        s = g * 4 + j
                        ins = E.transpose(pbth[half][:, j * 128:(j + 1) * 128],
                                          xt[:, s * H + h * 128: s * H + (h + 1) * 128], IDT)
                    return ins
                tcnt = P.emit("tensor", tr, waits=waits, inc="t")["t"]
                def ev(E, g=g, half=half):
                    return E.tensor_copy(QTA[:, g * 512:(g + 1) * 512],
                                         pbth[half][:, 0:512])
                pbt_free[half] = P.emit("vector", ev, waits={"t": tcnt}, inc="v")["v"]
                cp[("qtev", h, g)] = pbt_free[half]
            cp[("qtready", h)] = P.cnt["v"]

            def emit_qk(s, i):
                N = N_LIST[s]
                ncols = N * 128
                kbn = (ncols + 511) // 512
                ktw = cp[("ktev", h, (N + 3) // 4 - 1)]
                qtw = cp[("qtev", h, s // 4)]
                for kb in range(kbn):
                    n = min(512, ncols - kb * 512)
                    last = kb == kbn - 1
                    waits = {"s": lastexp.get(kb, 0)}
                    if kb == 0:
                        waits["v"] = max(ktw, qtw)
                        waits["gd"] = cp["consts_gd"]
                    def qk(E, s=s, kb=kb, n=n, last=last, h=h):
                        ins = E.matmul(pbs[kb][:, 0:n], QTA[:, s * 128:(s + 1) * 128],
                                       KTB[h % 2][:, kb * 512: kb * 512 + n],
                                       start=True, stop=not last)
                        if last:
                            noff = n - 256
                            ins = E.matmul(pbs[kb][:, noff:noff + 256], TRI,
                                           BXS[:, s * 256:(s + 1) * 256],
                                           start=False, stop=False)
                            ins = E.matmul(pbs[kb][:, noff:noff + 256], ONE,
                                           BYS[:, s * 256:(s + 1) * 256],
                                           start=False, stop=True)
                        return ins
                    cp[("qk_kb", h, s, kb)] = P.emit("tensor", qk, waits=waits,
                                                     inc="t")["t"]
                cp[("qk_done", h, s)] = P.cnt["t"]

            def emit_exp(s, i):
                N = N_LIST[s]
                ncols = N * 128
                kbn = (ncols + 511) // 512
                par = i % 2
                if (h, i) != (0, 0):
                    ph, ps = (h, SEQ[i - 1]) if i > 0 else (h - 1, SEQ[7])
                    ptw = cp.get(("ptr_done", ph, ps), 0)
                else:
                    ptw = 0
                for kb in range(kbn):
                    n = min(512, ncols - kb * 512)
                    waits = {"v": cp["nrm_done"],
                             "t": max(cp[("qk_kb", h, s, kb)], ptw)}
                    def ex(E, s=s, kb=kb, n=n, par=par):
                        return E.activation(PSB[:, kb * 512: kb * 512 + n],
                                            pbs[kb][:, 0:n],
                                            AF.Exp, bias=stv[:, 32 + s:33 + s],
                                            scale=SCALE,
                                            accum_out=stv[:, 40 + par * 4 + kb:41 + par * 4 + kb])
                    lastexp[kb] = P.emit("scalar", ex, waits=waits, inc="s")["s"]
                cp[("exp_done", h, s)] = P.cnt["s"]
                def lv(E, s=s, kbn=kbn, par=par):
                    E.reduce_sum(stv[:, 50 + par:51 + par],
                                 stv[:, 40 + par * 4: 40 + par * 4 + kbn], axis=AX.X)
                    E.drain()
                    E.reciprocal(stv[:, 52 + par:53 + par], stv[:, 50 + par:51 + par])
                    return E.drain()
                P.emit("vector", lv, waits={"s": cp[("exp_done", h, s)]}, inc=None)

            def emit_pt(s):
                N = N_LIST[s]
                kt = 0
                while kt < N:
                    m = min(4, N - kt)
                    half = (kt // 4) % 2
                    waits = {"s": cp[("exp_done", h, s)], "v": pbt_free[half]}
                    def tr(E, kt=kt, m=m, half=half):
                        ins = None
                        for j in range(m):
                            ins = E.transpose(pbth[half][:, j * 128:(j + 1) * 128],
                                              PSB[:, (kt + j) * 128:(kt + j + 1) * 128], IDT)
                        return ins
                    tcnt = P.emit("tensor", tr, waits=waits, inc="t")["t"]
                    if kt + m >= N:
                        cp[("ptr_done", h, s)] = tcnt
                    def ev(E, kt=kt, m=m, half=half):
                        return E.tensor_copy(PTB[:, kt * 128:(kt + m) * 128],
                                             pbth[half][:, 0:m * 128])
                    pbt_free[half] = P.emit("vector", ev, waits={"t": tcnt}, inc="v")["v"]
                    kt += m
                cp[("ptev", h, s)] = P.cnt["v"]

            def emit_pv(s, i):
                N = N_LIST[s]
                par = i % 2
                waits = {"v": max(cp[("ptev", h, s)], cp.get(("pv_evac_prev",), 0))}
                def pv(E, s=s, N=N, h=h):
                    ins = None
                    for kt in range(N):
                        ins = E.matmul(pb4[:, 0:128], PTB[:, kt * 128:(kt + 1) * 128],
                                       VB[h % 2][:, kt * 128:(kt + 1) * 128],
                                       start=(kt == 0), stop=(kt == N - 1))
                    return ins
                tcnt = P.emit("tensor", pv, waits=waits, inc="t")["t"]
                cp[("pv_done_slot", h, s)] = tcnt
                if A < 7:
                    return
                def evp(E, s=s, h=h, par=par):
                    return E.scalar_tensor_tensor(
                        out=stream[:, s * H + h * 128: s * H + (h + 1) * 128],
                        in0=pb4[:, 0:128], scalar=stv[:, 52 + par:53 + par],
                        in1=stream[:, s * H + h * 128: s * H + (h + 1) * 128],
                        op0=ALU.mult, op1=ALU.add)
                cp[("pv_evac_prev",)] = P.emit(
                    "vector", evp, waits={"t": tcnt, "sd": cp["stream"]}, inc="v")["v"]

            if A >= 3:
                emit_qk(SEQ[0], 0)
            if A >= 4:
                emit_exp(SEQ[0], 0)
            for i in range(1, 8):
                if A >= 3:
                    emit_qk(SEQ[i], i)
                if A >= 5:
                    emit_pt(SEQ[i - 1])
                if A >= 4:
                    emit_exp(SEQ[i], i)
                if A >= 6:
                    emit_pv(SEQ[i - 1], i - 1)
            if A >= 5:
                emit_pt(SEQ[7])
            if A >= 6:
                emit_pv(SEQ[7], 7)
                cp[("pv_done", h)] = cp[("pv_done_slot", h, SEQ[7])]
            if h + 2 < nheads:
                emit_vload(h + 2)
            g0 = 0 if do_shared else 22
            q0 = 0 if do_shared else 8
            if h == 1:
                emit_w13_load(g0)
            elif h == 2:
                emit_w13_load(g0 + 1)
            elif h == 3:
                emit_w2_load(q0)
            elif h == 6:
                emit_w2_load(q0 + 1)

        if LIMIT["experts"] > 0:
            g0 = 0 if do_shared else 22
            q0 = 0 if do_shared else 8
            emit_w13_load(g0)
            emit_w13_load(g0 + 1)
            emit_w2_load(q0)
            emit_w2_load(q0 + 1)

        def emit_ln(make_x1r=False):
            for tt in range(NT):
                def rs(E, tt=tt):
                    if tt == 0:
                        E.drain()
                    return E.reduce_sum(stv[:, tt:tt + 1], stream[:, tt * H:(tt + 1) * H], axis=AX.X)
                P.emit("vector", rs, waits={"sd": cp["stream"]}, inc="v")
            vafter = P.cnt["v"]
            for tt in range(NT):
                def sq(E, tt=tt):
                    return E.activation(ABUF[0][:, 0:2048], stream[:, tt * H:(tt + 1) * H],
                                        AF.Square, accum_out=stv[:, 8 + tt:9 + tt])
                P.emit("scalar", sq,
                       waits={"v": vafter, "s": cp.get(("sq_prev",), 0)}, inc="s")
                cp[("sq_prev",)] = P.cnt["s"]
            sq_done = P.cnt["s"]
            def fin1(E):
                E.drain()
                E.tensor_scalar(stv[:, 16:24], stv[:, 0:8], 1.0 / H, None, op0=ALU.mult)
                E.tensor_scalar(stv[:, 24:32], stv[:, 8:16], 1.0 / H, None, op0=ALU.mult)
                E.drain()
                E.tensor_mul(stv[:, 32:40], stv[:, 16:24], stv[:, 16:24])
                E.drain()
                E.tensor_sub(stv[:, 24:32], stv[:, 24:32], stv[:, 32:40])
                E.drain()
                return E.tensor_scalar(stv[:, 24:32], stv[:, 24:32], 1e-5, None, op0=ALU.add)
            c1 = P.emit("vector", fin1, waits={"s": sq_done}, inc="v")["v"]
            def fsq(E):
                return E.activation(stv[:, 32:40], stv[:, 24:32], AF.Sqrt)
            c2 = P.emit("scalar", fsq, waits={"v": c1}, inc="s")["s"]
            def fin2(E):
                E.drain()
                E.reciprocal(stv[:, 40:48], stv[:, 32:40])
                E.drain()
                E.tensor_mul(stv[:, 32:40], stv[:, 40:48], stv[:, 40:48])
                E.drain()
                E.tensor_mul(stv[:, 32:40], stv[:, 32:40], stv[:, 24:32])
                E.drain()
                E.tensor_scalar(stv[:, 32:40], stv[:, 32:40], -0.5, 1.5, op0=ALU.mult, op1=ALU.add)
                E.drain()
                E.tensor_mul(stv[:, 24:32], stv[:, 40:48], stv[:, 32:40])
                E.drain()
                E.tensor_mul(stv[:, 32:40], stv[:, 16:24], stv[:, 24:32])
                E.drain()
                E.tensor_scalar(stv[:, 32:40], stv[:, 32:40], -1.0, None, op0=ALU.mult)
                return E.drain()
            P.emit("vector", fin2, waits={"s": c2}, inc="v")
            cp[("ln_fin",)] = P.cnt["v"]
            for tt in range(NT):
                if make_x1r:
                    def apx(E, tt=tt):
                        return E.activation(x1r(tt), stream[:, tt * H:(tt + 1) * H],
                                            AF.Identity,
                                            bias=stv[:, 32 + tt:33 + tt],
                                            scale=stv[:, 24 + tt:25 + tt])
                    cp[("apx", tt)] = P.emit(
                        "scalar", apx, waits={"v": cp[("ln_fin",)]}, inc="s")["s"]
                def ap(E, tt=tt):
                    return E.tensor_scalar(stream[:, tt * H:(tt + 1) * H],
                                           stream[:, tt * H:(tt + 1) * H],
                                           stv[:, 16 + tt:17 + tt], stv[:, 24 + tt:25 + tt],
                                           op0=ALU.subtract, op1=ALU.mult)
                w = {"s": cp[("apx", tt)]} if make_x1r else None
                P.emit("vector", ap, waits=w, inc="v")
            return P.cnt["v"]

        if LIMIT["ln1"]:
            emit_ln(make_x1r=True)

        p3range = range(NT) if (LIMIT["p3"] or LIMIT["experts"] > 0) else range(0)
        for tt in p3range:
            for g in range(4):
                half = g % 2
                waits = {"v": pbt_free[half], "gd": cp["xtb"],
                         "s": cp.get(("apx", tt), 0)}
                if tt == 0 and g < 2 and nheads > 0:
                    waits["t"] = cp.get(("pv_done", nheads - 1), 0)
                def tr(E, g=g, half=half, tt=tt):
                    ins = None
                    for j in range(4):
                        hc = g * 4 + j
                        ins = E.transpose(pbth[half][:, j * 128:(j + 1) * 128],
                                          x1r(tt)[:, hc * 128:(hc + 1) * 128], IDT)
                    return ins
                tcnt = P.emit("tensor", tr, waits=waits, inc="t")["t"]
                def ev(E, g=g, half=half, tt=tt):
                    return E.tensor_copy(
                        xt[:].rearrange("p (hc t) -> p hc t", hc=16)[:, g * 4:(g + 1) * 4,
                                                                     tt * 128:(tt + 1) * 128],
                        pbth[half][:, 0:512].rearrange("p (a b) -> p a b", a=4))
                pbt_free[half] = P.emit("vector", ev, waits={"t": tcnt}, inc="v")["v"]
            cp[("x1t", tt)] = P.cnt["v"]
            cp[("x1tr", tt)] = P.cnt["t"]

        if len(p3range) > 0:
            waits = {"v": max(cp[("x1t", NT - 1)], cp.get(("pv_evac_prev",), 0)),
                     "gd": cp["consts_gd"]}
            def gmm(E):
                ins = None
                for hc in range(16):
                    for tt in range(NT):
                        ins = E.matmul(pb4[:, tt * 16:(tt + 1) * 16],
                                       xt[:, hc * 1024 + tt * 128: hc * 1024 + (tt + 1) * 128],
                                       GWS[:, hc * 16:(hc + 1) * 16],
                                       start=(hc == 0 and tt == 0),
                                       stop=(hc == 15 and tt == NT - 1),
                                       skip_group_check=True)
                return ins
            gq = P.emit("tensor", gmm, waits=waits, inc="t")["t"]
        for tt in p3range:
            waits = {"t": gq}
            if tt > 0:
                waits["s"] = cp[("gexp", tt - 1)]
            def gmx(E, tt=tt):
                return E.reduce_max(stv[:, 54:55], pb4[:, tt * 16:(tt + 1) * 16],
                                    axis=AX.X, negate=True)
            gv = P.emit("vector", gmx, waits=waits, inc="v")["v"]
            def gex(E, tt=tt):
                return E.activation(gsm[:], pb4[:, tt * 16:(tt + 1) * 16], AF.Exp,
                                    bias=stv[:, 54:55], accum_out=stv[:, 55:56])
            cp[("gexp", tt)] = P.emit("scalar", gex, waits={"v": gv}, inc="s")["s"]
            def gtk(E, tt=tt):
                E.drain()
                E.reciprocal(stv[:, 56:57], stv[:, 55:56])
                E.drain()
                E.tensor_scalar_mul(gsm[:], gsm[:], stv[:, 56:57])
                E.drain()
                E.max(out=m8[:], in_=gsm[:])
                E.drain()
                E.tensor_scalar(g16[:], gsm[:], m8[:, 5:6], None, op0=ALU.is_ge)
                E.drain()
                E.tensor_mul(g16[:], gsm[:], g16[:])
                E.drain()
                E.reduce_sum(stv[:, 57:58], g16[:], axis=AX.X)
                E.drain()
                E.tensor_scalar(stv[:, 57:58], stv[:, 57:58], 1e-20, None, op0=ALU.add)
                E.drain()
                E.reciprocal(stv[:, 58:59], stv[:, 57:58])
                E.drain()
                return E.tensor_scalar(comb[:, tt * 16:(tt + 1) * 16], g16[:],
                                       stv[:, 58:59], None, op0=ALU.mult)
            P.emit("vector", gtk, waits={"s": cp[("gexp", tt)]}, inc="v")
        def gdr(E):
            return E.drain()
        P.emit("vector", gdr, inc=None)
        comb_done = P.cnt["v"]

        cp[("m1_silu_prev",)] = cp.get(("exp_done", nheads - 1, 7), 0)

        def selb(tt):
            return xt[:, tt * 2048 + 512: tt * 2048 + 1024]
        SELT = arena[:, 18048:22144]

        def abase(half):
            return half * 5632

        def xgv(hc):
            return xt[:, hc * 1024: hc * 1024 + 512]

        def emit_m1s(gj0, nk, ab_off, rhs_fn, xg_key):
            for k in range(nk):
                gj = gj0 + k
                emit_w13_load(gj)
                wsk, wsv = cp[("w13ld", gj)]
                gb, ub = (pb2, pb3) if k % 2 == 0 else (pb0, pb1)
                waits = {wsk: wsv, "s": cp[("m1_silu_prev",)],
                         "v": cp.get(xg_key, cp[("x1t", NT - 1)])}
                def gmmf(E, gj=gj, gb=gb, rhs_fn=rhs_fn):
                    ins = None
                    for hc in range(16):
                        ins = E.matmul(gb[:, 0:512], wsl(gj, hc * 128),
                                       rhs_fn(hc), start=(hc == 0), stop=(hc == 15),
                                       skip_group_check=True)
                    return ins
                gcnt = P.emit("tensor", gmmf, waits=waits, inc="t")["t"]
                uw = {}
                if cp.get(("m1_aev2",), 0):
                    uw["v"] = cp[("m1_aev2",)]
                def ummf(E, gj=gj, ub=ub, rhs_fn=rhs_fn):
                    ins = None
                    for hc in range(16):
                        ins = E.matmul(ub[:, 0:512], wsl(gj, 2048 + hc * 128),
                                       rhs_fn(hc), start=(hc == 0), stop=(hc == 15),
                                       skip_group_check=True)
                    return ins
                ucnt = P.emit("tensor", ummf, waits=uw, inc="t")["t"]
                cp[("w13used", w13buf(gj))] = ucnt
                def slf(E, gb=gb):
                    return E.activation(sg[:], gb[:, 0:512], AF.Silu)
                scnt = P.emit("scalar", slf,
                              waits={"t": gcnt, "v": cp.get(("m1_aev_prev",), 0)},
                              inc="s")["s"]
                cp[("m1_silu_prev",)] = scnt
                emit_w13_load(gj + 3)
                icol = ab_off + k * 512
                def aevf(E, ub=ub, icol=icol):
                    return E.tensor_mul(arena[:, icol: icol + 512], sg[:], ub[:, 0:512])
                acnt = P.emit("vector", aevf, waits={"t": ucnt, "s": scnt}, inc="v")["v"]
                cp[("m1_aev2",)] = cp.get(("m1_aev_prev",), 0)
                cp[("m1_aev_prev",)] = acnt
            return P.cnt["v"]

        def emit_routing(en):
            def mskf(E, en=en):
                E.drain()
                E.tensor_scalar(sklb[:, 0:8].rearrange("p (a b) -> p a b", b=1),
                                comb[:].rearrange("p (t g) -> p t g", g=16)[:, :, en:en + 1],
                                1e-30, None, op0=ALU.is_ge)
                E.drain()
                E.tensor_tensor_scan(sklb[:, 8:16], sklb[:, 0:8], sklb[:, 16:24],
                                     0.0, op0=ALU.add, op1=ALU.add)
                E.drain()
                E.tensor_sub(sklb[:, 8:16], sklb[:, 8:16], sklb[:, 0:8])
                return E.drain()
            mc = P.emit("vector", mskf,
                        waits={"v": comb_done, "t": cp.get(("posmm", en - 1), 0)},
                        inc="v")["v"]
            def posmm(E):
                E.matmul(pb0[:, 0:8], ltsb[:], sklb[:, 0:8], start=True, stop=False,
                         skip_group_check=True)
                return E.matmul(pb0[:, 0:8], ONE, sklb[:, 8:16], start=False, stop=True,
                                skip_group_check=True)
            pt_ = P.emit("tensor", posmm,
                         waits={"v": mc, "s": cp[("m1_silu_prev",)]}, inc="t")["t"]
            cp[("posmm", en)] = pt_
            def posev(E):
                E.drain()
                E.tensor_scalar(sstat[:, 0:8], pb0[:, 0:8], 1e6, None, op0=ALU.add)
                E.drain()
                E.scalar_tensor_tensor(out=sstat[:, 8:16], in0=sklb[:, 0:8],
                                       scalar=-1e6, in1=sstat[:, 0:8],
                                       op0=ALU.mult, op1=ALU.add)
                return E.drain()
            pv_ = P.emit("vector", posev, waits={"t": pt_}, inc="v")["v"]
            cp[("pos", en)] = pv_
            for tt in range(NT):
                def self_(E, tt=tt):
                    return E.tensor_scalar(selb(tt), iotab[:],
                                           sstat[:, 8 + tt:9 + tt], None, op0=ALU.is_equal)
                P.emit("vector", self_,
                       waits={"v": pv_, "t": cp.get(("gath_done", en - 1), 0)},
                       inc="v")
            cp[("sel", en)] = P.cnt["v"]

        def emit_selt(en):
            for tt in range(NT):
                half = tt % 2
                waits = {"v": max(cp[("sel", en)], pbt_free[half])}
                def trf(E, tt=tt, half=half):
                    ins = None
                    for ct in range(4):
                        ins = E.transpose(pbth[half][:, ct * 128:(ct + 1) * 128],
                                          selb(tt)[:, ct * 128:(ct + 1) * 128],
                                          IDT)
                    return ins
                tcnt = P.emit("tensor", trf, waits=waits, inc="t")["t"]
                def evt(E, tt=tt, half=half):
                    return E.tensor_copy(
                        SELT[:].rearrange("p (ct t c) -> p ct t c", ct=4, t=8)[:, :, tt, :],
                        pbth[half][:, 0:512].rearrange("p (a b) -> p a b", a=4))
                pbt_free[half] = P.emit("vector", evt, waits={"t": tcnt}, inc="v")["v"]
            cp[("selt", en)] = P.cnt["v"]

        def emit_gather(en):
            for hc in range(16):
                gb = pb0 if hc % 2 == 0 else pb1
                wv = max(cp[("sel", en)], cp.get(("xgev", hc % 2), 0))
                if hc < 2:
                    wv = max(wv, cp.get(("scatev", 0), 0), cp.get(("scatev", 1), 0),
                             cp.get(("pos", en), 0))
                def gmf(E, gb=gb, hc=hc):
                    ins = None
                    for tt in range(NT):
                        ins = E.matmul(gb[:, 0:512], x1r(tt)[:, hc * 128:(hc + 1) * 128],
                                       selb(tt),
                                       start=(tt == 0), stop=(tt == NT - 1))
                    return ins
                tcnt = P.emit("tensor", gmf, waits={"v": wv}, inc="t")["t"]
                def xev(E, gb=gb, hc=hc):
                    return E.tensor_copy(xgv(hc), gb[:, 0:512])
                cp[("xgev", hc % 2)] = P.emit("vector", xev, waits={"t": tcnt},
                                              inc="v")["v"]
            cp[("xgd", en)] = P.cnt["v"]
            cp[("gath_done", en)] = P.cnt["t"]

        def emit_m2_shared(th, adep):
            gidx = cp.get(("m2gidx",), 0)
            for q in range(4):
                for hb in range(2):
                    wj = th * 8 + q * 2 + hb
                    emit_w2_load(wj)
                    wsb = wsbufs[wj % 2]
                    w2k, w2v = cp[("w2ld", wj)]
                    for ct in range(4):
                        bk = pb4 if gidx % 2 == 0 else pb5
                        pkey = ("m2ev", gidx % 2)
                        waits = {w2k: w2v, "v": adep, "s": cp[("m1_silu_prev",)]}
                        if pkey in cp:
                            waits["v"] = max(waits["v"], cp[pkey])
                        ihalf = q // 2
                        def mmf(E, bk=bk, wsb=wsb, ct=ct, ihalf=ihalf):
                            ins = None
                            for ici in range(IC):
                                icol = (ihalf * 11 + ici) * 512
                                ins = E.matmul(bk[:, 0:512],
                                               arena[:, icol + ct * 128: icol + (ct + 1) * 128],
                                               wsb[:, ici * 512:(ici + 1) * 512],
                                               start=(ici == 0), stop=(ici == IC - 1),
                                               skip_group_check=True)
                            return ins
                        tcnt = P.emit("tensor", mmf, waits=waits, inc="t")["t"]
                        if ct == 3:
                            cp[("w2used", wj % 2)] = tcnt
                        tt = th * 4 + ct
                        def evf(E, bk=bk, tt=tt, q=q, hb=hb):
                            dst = stream[:, tt * H + (q % 2) * 1024 + hb * 512:
                                         tt * H + (q % 2) * 1024 + (hb + 1) * 512]
                            return E.scalar_tensor_tensor(out=dst, in0=bk[:, 0:512],
                                                          scalar=1.0, in1=dst,
                                                          op0=ALU.mult, op1=ALU.add)
                        c = P.emit("vector", evf,
                                   waits={"t": tcnt,
                                          "v": max(comb_done,
                                                   cp.get(("evf_prev", tt), 0))},
                                   inc="v")["v"]
                        cp[pkey] = c
                        cp[("evf_prev", tt)] = c
                        gidx += 1
                    emit_w2_load(wj + 2)
            cp[("m2gidx",)] = gidx

        def emit_m2_routed(e, wj0, adep):
            ybase = abase(1 - e % 2)
            gidx = cp.get(("m2gidx",), 0)
            for hh in range(2):
                for hb in range(2):
                    wj = wj0 + hh * 2 + hb
                    emit_w2_load(wj)
                    wsb = wsbufs[wj % 2]
                    w2k, w2v = cp[("w2ld", wj)]
                    for ct in range(4):
                        bk = pb4 if gidx % 2 == 0 else pb5
                        pkey = ("m2ev", gidx % 2)
                        waits = {w2k: w2v, "v": adep, "s": cp[("m1_silu_prev",)]}
                        if pkey in cp:
                            waits["v"] = max(waits["v"], cp[pkey])
                        abse = abase(e % 2)
                        def mmf(E, bk=bk, wsb=wsb, ct=ct, abse=abse):
                            ins = None
                            for ici in range(IC):
                                ins = E.matmul(
                                    bk[:, 0:512],
                                    arena[:, abse + ici * 512 + ct * 128:
                                           abse + ici * 512 + (ct + 1) * 128],
                                    wsb[:, ici * 512:(ici + 1) * 512],
                                    start=(ici == 0), stop=(ici == IC - 1),
                                    skip_group_check=True)
                            return ins
                        tcnt = P.emit("tensor", mmf, waits=waits, inc="t")["t"]
                        if ct == 3:
                            cp[("w2used", wj % 2)] = tcnt
                        yc = ybase + ct * 1024 + hb * 512
                        def evf(E, bk=bk, yc=yc):
                            return E.tensor_copy(arena[:, yc: yc + 512], bk[:, 0:512])
                        c = P.emit("vector", evf, waits={"t": tcnt}, inc="v")["v"]
                        cp[pkey] = c
                        gidx += 1
                    emit_w2_load(wj + 2)
                if hh == 0 and e + 1 < nroute:
                    emit_routing(e + 1)
                ev_y = P.cnt["v"]
                for tt in range(NT):
                    sbk = (pb0, pb1) if tt % 2 == 0 else (pb2, pb3)
                    skey = ("scatev", tt % 2)
                    wv = max(ev_y, cp[("selt", e)], cp.get(skey, 0))
                    if tt < 2:
                        wv = max(wv, cp.get(("pos", e + 1), 0))
                    def smm(E, sbk=sbk, tt=tt, ybase=ybase):
                        ins = None
                        for ct in range(4):
                            lhs = SELT[:, (ct * 8 + tt) * 128:(ct * 8 + tt + 1) * 128]
                            for hb in range(2):
                                ins = E.matmul(
                                    sbk[hb][:, 0:512], lhs,
                                    arena[:, ybase + ct * 1024 + hb * 512:
                                           ybase + ct * 1024 + (hb + 1) * 512],
                                    start=(ct == 0), stop=(ct == 3),
                                    skip_group_check=True)
                        return ins
                    tcnt = P.emit("tensor", smm, waits={"v": wv}, inc="t")["t"]
                    def sev(E, sbk=sbk, tt=tt, hh=hh, e=e):
                        ins = None
                        for hb in range(2):
                            dst = stream[:, tt * H + hh * 1024 + hb * 512:
                                         tt * H + hh * 1024 + (hb + 1) * 512]
                            ins = E.scalar_tensor_tensor(
                                out=dst, in0=sbk[hb][:, 0:512],
                                scalar=comb[:, tt * 16 + e: tt * 16 + e + 1],
                                in1=dst, op0=ALU.mult, op1=ALU.add)
                        return ins
                    c = P.emit("vector", sev,
                               waits={"t": tcnt,
                                      "v": max(comb_done,
                                               cp.get(("evf_prev", tt), 0))},
                               inc="v")["v"]
                    cp[skey] = c
                    cp[("evf_prev", tt)] = c
            cp[("m2gidx",)] = gidx

        def emit_m1d(gj0, nk):
            for k in range(nk):
                gj = gj0 + k
                emit_w13_load(gj)
                wbuf = wbufs[gj % 2]
                wsk, wsv = cp[("w13ld", gj)]
                waits = {wsk: wsv, "s": cp[("m1_silu_prev",)],
                         "v": cp[("x1t", NT - 1)]}
                for tb in range(2):
                    gb, ub = (pb2, pb3) if tb == 0 else (pb0, pb1)
                    def gmmf(E, wbuf=wbuf, gb=gb, tb=tb):
                        ins = None
                        for hc in range(16):
                            ins = E.matmul(gb[:, 0:512], wbuf[:, hc * 128:(hc + 1) * 128],
                                           xt[:, hc * 1024 + tb * 512: hc * 1024 + (tb + 1) * 512],
                                           start=(hc == 0), stop=(hc == 15),
                                           skip_group_check=True)
                        return ins
                    gcnt = P.emit("tensor", gmmf, waits=waits if tb == 0 else {},
                                  inc="t")["t"]
                    uw = {}
                    if cp.get(("m1_aev2",), 0):
                        uw["v"] = cp[("m1_aev2",)]
                    def ummf(E, wbuf=wbuf, ub=ub, tb=tb):
                        ins = None
                        for hc in range(16):
                            ins = E.matmul(ub[:, 0:512],
                                           wbuf[:, 2048 + hc * 128: 2048 + (hc + 1) * 128],
                                           xt[:, hc * 1024 + tb * 512: hc * 1024 + (tb + 1) * 512],
                                           start=(hc == 0), stop=(hc == 15),
                                           skip_group_check=True)
                        return ins
                    ucnt = P.emit("tensor", ummf, waits=uw, inc="t")["t"]
                    if tb == 1:
                        cp[("w13used", gj % 2)] = ucnt
                    def slf(E, gb=gb):
                        return E.activation(sg[:], gb[:, 0:512], AF.Silu)
                    scnt = P.emit("scalar", slf,
                                  waits={"t": gcnt, "v": cp.get(("m1_aev_prev",), 0)},
                                  inc="s")["s"]
                    cp[("m1_silu_prev",)] = scnt
                    icol = k * 1024 + tb * 512
                    def aevf(E, ub=ub, icol=icol):
                        return E.tensor_mul(arena[:, icol: icol + 512], sg[:], ub[:, 0:512])
                    acnt = P.emit("vector", aevf, waits={"t": ucnt, "s": scnt},
                                  inc="v")["v"]
                    cp[("m1_aev2",)] = cp.get(("m1_aev_prev",), 0)
                    cp[("m1_aev_prev",)] = acnt
                emit_w13_load(gj + 2)
            return P.cnt["v"]

        def emit_m2_sharedAB(ihalf, adep):
            gidx = cp.get(("m2gidx",), 0)
            for q in (2 * ihalf, 2 * ihalf + 1):
                for hb in range(2):
                    wj = q * 2 + hb
                    emit_w2_load(wj)
                    wsb = wsbufs[wj % 2]
                    w2k, w2v = cp[("w2ld", wj)]
                    for tt in range(NT):
                        bk = pb4 if gidx % 2 == 0 else pb5
                        pkey = ("m2ev", gidx % 2)
                        waits = {w2k: w2v, "v": adep, "s": cp[("m1_silu_prev",)]}
                        if pkey in cp:
                            waits["v"] = max(waits["v"], cp[pkey])
                        def mmf(E, bk=bk, wsb=wsb, tt=tt):
                            ins = None
                            for ici in range(IC):
                                ins = E.matmul(bk[:, 0:512],
                                               arena[:, ici * 1024 + tt * 128: ici * 1024 + (tt + 1) * 128],
                                               wsb[:, ici * 512:(ici + 1) * 512],
                                               start=(ici == 0), stop=(ici == IC - 1),
                                               skip_group_check=True)
                            return ins
                        tcnt = P.emit("tensor", mmf, waits=waits, inc="t")["t"]
                        if tt == NT - 1:
                            cp[("w2used", wj % 2)] = tcnt
                        def evf(E, bk=bk, tt=tt, q=q, hb=hb):
                            dst = stream[:, tt * H + (q % 2) * 1024 + hb * 512:
                                         tt * H + (q % 2) * 1024 + (hb + 1) * 512]
                            return E.scalar_tensor_tensor(out=dst, in0=bk[:, 0:512],
                                                          scalar=1.0, in1=dst,
                                                          op0=ALU.mult, op1=ALU.add)
                        c = P.emit("vector", evf,
                                   waits={"t": tcnt,
                                          "v": max(comb_done,
                                                   cp.get(("evf_prev", tt), 0))},
                                   inc="v")["v"]
                        cp[pkey] = c
                        cp[("evf_prev", tt)] = c
                        gidx += 1
                    emit_w2_load(wj + 2)
            cp[("m2gidx",)] = gidx

        if do_shared:
            adep_a = emit_m1d(0, 11)
            emit_m2_sharedAB(0, adep_a)
            adep_b = emit_m1d(11, 11)
            cp[("shared_m1_done",)] = P.cnt["t"]
            if nroute > 0:
                emit_routing(0)
            emit_m2_sharedAB(1, adep_b)
        if nroute > 0:
            if ("sel", 0) not in cp:
                emit_routing(0)
            for e in range(nroute):
                emit_selt(e)
                emit_gather(e)
                adep = emit_m1s(22 + e * 11, 11, abase(e % 2), xgv, ("xgd", e))
                emit_m2_routed(e, 8 + e * 4, adep)

        if LIMIT["ln2"]:
            emit_ln()
        st_wait = P.cnt["v"]
        for tt in range(NT):
            def st(E, tt=tt):
                return E.dma_start(out=d_out[tt * 128:(tt + 1) * 128, :],
                                   in_=stream[:, tt * H:(tt + 1) * H])
            P.emit("sync", st, waits={"v": st_wait}, inc="sd", amt=16)
        final_sd = P.cnt["sd"]
        def fw(E):
            return E.wait_ge(sems["sd"], final_sd)
        P.ops["sync"].append(({}, fw, None, 0))

        with nc.Block() as block:
            def replay(E, name):
                lastw = {}
                for waits, fn, inc, amt in P.ops[name]:
                    for sname, val in waits.items():
                        if val > 0 and lastw.get(sname, -1) < val:
                            E.wait_ge(sems[sname], val)
                            lastw[sname] = val
                    ins = fn(E)
                    if inc is not None:
                        ins.then_inc(sems[inc], amt)

            @block.sync
            def _(E):
                replay(E, "sync")

            @block.gpsimd
            def _(E):
                replay(E, "gpsimd")

            @block.tensor
            def _(E):
                replay(E, "tensor")

            @block.vector
            def _(E):
                replay(E, "vector")

            @block.scalar
            def _(E):
                replay(E, "scalar")

    return nc


def _host_prep(L0_ffn_gate_w, L0_ffn_w13, L0_ffn_w2, L0_ffn_sw13, L0_ffn_sw2):
    gw = np.asarray(L0_ffn_gate_w, dtype=np.float32)
    w13 = np.asarray(L0_ffn_w13, dtype=np.float32)
    w2 = np.asarray(L0_ffn_w2, dtype=np.float32)
    sw13 = np.asarray(L0_ffn_sw13, dtype=np.float32)
    sw2 = np.asarray(L0_ffn_sw2, dtype=np.float32)

    w13t = w13.reshape(16, 22, 128, 16, 128).transpose(0, 1, 4, 3, 2).reshape(16, 22, 128, 2048)
    w13t = np.ascontiguousarray(w13t[:, JORD])
    sw13t = sw13.reshape(44, 128, 16, 128).transpose(0, 3, 2, 1).reshape(44, 128, 2048)
    sw13t = np.ascontiguousarray(sw13t[SJORD])
    w2t = np.ascontiguousarray(
        w2.transpose(0, 2, 1).reshape(16, 11, 128, 2, 1024).transpose(0, 3, 1, 2, 4))
    sw2t = np.ascontiguousarray(
        sw2.T.reshape(2, 11, 128, 2, 1024).transpose(0, 3, 1, 2, 4).reshape(4, 11, 128, 1024))
    gwt = np.ascontiguousarray(gw.T.reshape(16, 128, 16))

    identb = np.eye(128, dtype=ml_dtypes.bfloat16)
    dd = np.arange(128)[:, None]
    qq = np.arange(128)[None, :]
    trib = (dd > qq).astype(ml_dtypes.bfloat16)
    onesb = np.ones((128, 128), dtype=ml_dtypes.bfloat16)

    kl = np.arange(128)[None, :]
    lower_eq = (dd <= kl).astype(np.float32)
    bx = np.zeros((2, 8, 128, 256), dtype=np.float32)
    by = np.zeros((2, 8, 128, 256), dtype=np.float32)
    for s in range(8):
        bx[0, s, :, 128:256] = -CBIG * lower_eq
        bx[1, s, :, 0:128] = -CBIG * lower_eq
        by[1, s, :, 128:256] = -CBIG / 128.0

    lts = (dd < kl).astype(ml_dtypes.bfloat16)
    iota = np.broadcast_to(np.arange(512, dtype=np.float16)[None, :],
                           (128, 512)).copy()
    return (w13t, sw13t, w2t, sw2t, gwt, identb, trib, onesb, bx, by,
            lts, iota)


def _row_tiles(role):
    return [(N_LIST[s] - 1 - role) for s in range(8)]


def kernel(L0_attn_x, L0_ffn_gate_w, L0_ffn_w13, L0_ffn_w2, L0_ffn_sw13, L0_ffn_sw2):
    x = np.asarray(L0_attn_x, dtype=np.float32)
    (w13t, sw13t, w2t, sw2t, gwt, identb, trib, onesb, bx, by,
     lts, iota) = _host_prep(
        L0_ffn_gate_w, L0_ffn_w13, L0_ffn_w2, L0_ffn_sw13, L0_ffn_sw2)

    nc = build_program()

    in_maps = []
    for c in range(8):
        b, role = c // 2, c % 2
        tiles = _row_tiles(role)
        rows = np.concatenate([np.arange(t * 128, (t + 1) * 128) for t in tiles])
        in_maps.append({
            "xmy": np.ascontiguousarray(x[b][rows]),
            "xkv": np.ascontiguousarray(x[b]),
            "w13t": w13t,
            "sw13t": sw13t,
            "w2t": w2t,
            "sw2t": sw2t,
            "gwt": gwt,
            "bxm": np.ascontiguousarray(bx[role]),
            "bym": np.ascontiguousarray(by[role]),
            "identb": identb,
            "trib": trib,
            "onesb": onesb,
            "ltsm": lts,
            "iotam": iota,
        })

    res = run_bass_kernel_spmd(nc, in_maps, core_ids=list(range(8)))

    out = np.empty((B, L, H), dtype=np.float32)
    for c in range(8):
        b, role = c // 2, c % 2
        o = res.results[c]["out"]
        for s, t in enumerate(_row_tiles(role)):
            out[b, t * 128:(t + 1) * 128, :] = o[s * 128:(s + 1) * 128, :]
    return out
